# version 05
# speedup vs baseline: 1.1189x; 1.0187x over previous
import contextlib
import numpy as np
import ml_dtypes
import concourse.bass as bass
import concourse.mybir as mybir
from concourse.bass_utils import run_bass_kernel_spmd

F32 = mybir.dt.float32
BF16 = mybir.dt.bfloat16
AF = mybir.ActivationFunctionType
ALU = mybir.AluOpType
AX = mybir.AxisListType
BF = ml_dtypes.bfloat16

NC = 8
D = 2048
CD = D // 128
T = 1024
F1 = 5632
NH, HD = 16, 128
FBIG = 3.0e38
KILL = -1.0e12
LN_EPS = 1e-5
LN_EPS_IN = 1e-12

CORES = list(range(NC))



def _alibi_slopes(n):
    import math
    def pow2_slopes(n):
        start = 2.0 ** (-2.0 ** (-(math.log2(n) - 3)))
        return [start * start ** i for i in range(n)]
    if math.log2(n).is_integer():
        return pow2_slopes(n)
    cp2 = 2 ** math.floor(math.log2(n))
    return pow2_slopes(cp2) + _alibi_slopes(2 * cp2)[0::2][:n - cp2]


def _cm(v):
    return np.ascontiguousarray(np.asarray(v, np.float32).reshape(-1, 128).T)


def _fm_shard(flat, c):
    return np.ascontiguousarray(flat[c * T:(c + 1) * T].T)


MASK_POS = ([(0, kt) for kt in range(4)] + [(1, kt) for kt in range(4, 8)]
            + [(0, kt) for kt in range(8, 12)] + [(1, kt) for kt in range(12, 16)])
SKIP_POS = {(0, kt) for kt in range(12, 16)}


def _attn_consts():
    slopes = np.array(_alibi_slopes(NH), np.float64)
    dk = np.arange(128, dtype=np.float64)
    kb_pc, qb_pc, masks_pc = [], [], []
    for c in range(NC):
        odd = c % 2 == 1
        off = 1024.0 if odd else 0.0
        kb = np.zeros((128, NH * 16), np.float64)
        for h in range(NH):
            for kt in range(16):
                kg = kt * 128.0 + dk if kt < 8 else (kt - 8) * 128.0 + off + dk
                v = slopes[h] * kg
                if (not odd) and kt < 8:
                    v = np.full(128, KILL)
                kb[:, h * 16 + kt] = v
        kb_pc.append(kb.astype(np.float32))

        q_glob = np.arange(T, dtype=np.float64) + off
        qb = np.zeros((NH, 2, 2, 512), np.float64)
        for h in range(NH):
            for qt in range(2):
                v = -128.0 * slopes[h] * (q_glob[qt * 512:(qt + 1) * 512] + 64.0)
                hi = v.astype(BF).astype(np.float64)
                lo = (v - hi).astype(BF).astype(np.float64)
                qb[h, qt, 0] = hi
                qb[h, qt, 1] = lo
        qb_pc.append(qb.astype(BF))

        q_own = np.arange(T, dtype=np.float64)
        masks = np.zeros((16, 128, 512), np.float64)
        for i, (qt, kt) in enumerate(MASK_POS):
            q = q_own[qt * 512:(qt + 1) * 512]
            if kt < 8:
                masks[i] = FBIG if odd else np.where(
                    (kt * 128 + np.arange(128.0))[:, None] <= q[None, :], FBIG, 0.0)
            else:
                k = (kt - 8) * 128 + np.arange(128.0)
                masks[i] = np.where(k[:, None] <= q[None, :], FBIG, 0.0)
        masks_pc.append(masks.astype(BF))
    return kb_pc, qb_pc, masks_pc



class P:

    def __init__(self):
        self.nc = bass.Bass("TRN2", target_bir_lowering=False)
        self.stack = contextlib.ExitStack()
        self.sems = {}
        self.cnt = {}
        orig_sbuf = self.nc.sbuf_tensor
        seen = {}

        def _uniq_sbuf(name, shape, dt, **kw):
            n = seen.get(name, 0)
            seen[name] = n + 1
            return orig_sbuf(name if n == 0 else f"{name}_u{n}", shape, dt, **kw)
        self.nc.sbuf_tensor = _uniq_sbuf
        self.alias = {}
        self.pool_next = 0
        self.sw_next = 0

    _POOLED = ("pr_", "ld1_", "ld3_", "qst_", "cst_", "cld_", "ff_", "at_",
               "mo_", "f1_", "f2_", "f3_")
    _SW = {"at_lwv0", "at_lwv1", "at_lwk0", "at_lwk1", "at_lwq0", "at_lwq1",
           "at_lwp0", "at_lwp1", "f1_lwv0", "f1_lwv1", "f2_lwr0", "f2_lwr1",
           "f2_lwo0", "f2_lwo1", "f3_lwcr0", "f3_lwcr1", "f3_lwck0",
           "f3_lwck1", "f3_lwcv0", "f3_lwcv1", "mo_lwg", "cld_0", "cld_1"}

    def _phys(self, name):
        if not name.startswith(self._POOLED):
            return name
        ph = self.alias.get(name)
        if ph is None:
            if name in self._SW:
                ph = f"h{self.sw_next}"
                self.sw_next += 1
            else:
                ph = f"g{self.pool_next}"
                self.pool_next += 1
            self.alias[name] = ph
        return ph

    def pool_reset(self):
        self.alias = {}
        self.pool_next = 0
        self.sw_next = 0

    def sem(self, name):
        name = self._phys(name)
        if name not in self.sems:
            self.sems[name] = self.stack.enter_context(self.nc.semaphore(name))
            self.cnt[name] = 0
        return self.sems[name]

    def inc(self, ins, name, by=1):
        name = self._phys(name)
        ins.then_inc(self.sem(name), by)
        self.cnt[name] += by
        return self.cnt[name]

    def sbuf(self, name, shape, dt):
        return self.stack.enter_context(self.nc.sbuf_tensor(name, shape, dt))

    def psum(self, name, shape, dt=F32):
        return self.stack.enter_context(self.nc.psum_tensor(name, shape, dt))

    def scope(self):
        return self.stack.enter_context(contextlib.ExitStack())

    def sbuf_in(self, scope, name, shape, dt):
        return scope.enter_context(self.nc.sbuf_tensor(name, shape, dt))

    def chain(self, eng, ins, name="chn"):
        v = self.inc(ins, name, 1)
        eng.wait_ge(self.sem(name), v)

    def snap(self):
        snap = dict(self.cnt)
        pp = self

        class _S:
            def __getitem__(_, n):
                return snap.get(pp._phys(n), 0)
        return _S()

    def fence(self, names=None):
        nms = list(self.cnt) if names is None else names
        vals = [(n, self.cnt[n]) for n in nms if self.cnt.get(n, 0) > 0]
        with self.nc.Block() as blk:
            @blk.sync
            def _(sync):
                for n, v in vals:
                    sync.wait_ge(self.sem(n), v)


def chunked(ap, p=128):
    return ap.rearrange("(c p) f -> p c f", p=p)



CAST_SHARDS = {"W0aT": (D, 4096), "W0bT": (4096, D), "W1aT": (D, 4096),
               "W1bT": (4096, D), "Wck": (D, 8192), "Wcv": (8192, D)}
QN_TOT = float(D * F1)


def emit_prep(p, w1, w2, w3, q1, q2, q3, cin, cout, cc_in, cc_out, colb, rowb,
              do_casts=True, ag_q=None, ag_w=None):
    nc = p.nc
    p.pool_reset()
    B = p.snap()

    wv = {"w1": (chunked(w1), chunked(q1), F1), "w2": (chunked(w2), chunked(q2), F1),
          "w3": (chunked(w3), chunked(q3), 256)}
    TILES = []
    for nm in ("w1", "w2", "w3"):
        iv, ov, wid = wv[nm]
        tw = wid // 4
        for i in range(4):
            TILES.append((nm, iv[:, :, i * tw:(i + 1) * tw], ov[:, :, i * tw:(i + 1) * tw]))

    CT = []
    for nm, (r, cdim) in (CAST_SHARDS.items() if do_casts else []):
        nch = (r // NC) // 128
        na = cdim // 2048
        ivf = chunked(cin[nm]).rearrange("p c (a b) -> p c a b", b=2048)
        ovf = chunked(cout[nm]).rearrange("p c (a b) -> p c a b", b=2048)
        if nch * na == 4:
            CT.append((ivf, ovf, (nch, na)))
        elif na == 4:
            for i in range(2):
                CT.append((ivf[:, :, i * 2:(i + 1) * 2], ovf[:, :, i * 2:(i + 1) * 2], (nch, 2)))
        else:
            for i in range(2):
                CT.append((ivf[:, i * 4:(i + 1) * 4], ovf[:, i * 4:(i + 1) * 4], (4, 1)))

    with contextlib.ExitStack() as _prs:
        def _sb(n, s, d):
            return _prs.enter_context(nc.sbuf_tensor(n, s, d))
        wt = _sb("pr_wt", [128, 2, 2, 1408], F32)
        wt3 = _sb("pr_wt3", [128, 2, 44, 64], F32)
        acc = _sb("pr_acc", [128, 12], F32)
        accw = _sb("pr_pr_accw", [128, 4], F32)
        ta = _sb("pr_ta", [1, 8], F32)
        rr = _sb("pr_rr", [1, 512], F32)
        prow = _sb("pr_prow", [1, 8], F32)
        trow = _sb("pr_trow", [1, 8], F32)
        t2row = _sb("pr_t2row", [1, 8], F32)
        thr = _sb("pr_thr", [128, 8], F32)
        qa = _sb("pr_qa", [128, 2, 1408], F32)
        qa3 = _sb("pr_qa3", [128, 44, 64], F32)
        qo = _sb("pr_qo", [128, 2, 2, 1408], BF16)
        qo3 = _sb("pr_qo3", [128, 2, 44, 64], BF16)
        cb = _sb("pr_cb", [128, 2, 4, 2048], BF16)

        def buf(nm, i):
            return wt3[:, i % 2] if nm == "w3" else wt[:, i % 2]

        def qbuf(nm, i):
            return qo3[:, i % 2] if nm == "w3" else qo[:, i % 2]

        def abuf(nm):
            return qa3 if nm == "w3" else qa

        with nc.Block() as block:

            @block.sync
            def _(sync):
                for i, (nm, iv, ov) in enumerate(TILES):
                    if i >= 2:
                        sync.wait_ge(p.sem("pr_gcons"), B["pr_gcons"] + i - 1)
                    p.inc(sync.dma_start(out=buf(nm, i), in_=iv), f"ld1_{i % 2}", 16)
                sync.wait_ge(p.sem("pr_gcons"), B["pr_gcons"] + 12)
                for i, (nm, iv, ov) in enumerate(TILES):
                    if i >= 2:
                        sync.wait_ge(p.sem("pr_qdone"), B["pr_qdone"] + i - 1)
                    p.inc(sync.dma_start(out=buf(nm, i), in_=iv), f"ld3_{i % 2}", 16)

            @block.vector
            def _(vector):
                for i, (nm, iv, ov) in enumerate(TILES):
                    vector.wait_ge(p.sem(f"ld1_{i % 2}"), 16 * (i // 2 + 1))
                    ins = vector.tensor_reduce(acc[:, i:i + 1], buf(nm, i)[:], AX.XY, ALU.add,
                                               apply_absolute_value=True)
                    p.inc(ins, "pr_gcons", 1)
                vector.wait_ge(p.sem("pr_gcons"), 12)
                for wi in range(3):
                    vector.tensor_reduce(accw[:, wi:wi + 1], acc[:, 4 * wi:4 * wi + 4], AX.X, ALU.add)
                p.inc(vector.tensor_copy(accw[:, 3:4], acc[:, 0:1]), "pr_accw", 1)
                vector.wait_ge(p.sem("pr_rowld"), B["pr_rowld"] + 16)
                for wi in range(3):
                    ins = vector.tensor_reduce(prow[0:1, wi:wi + 1], rr[0:1, wi:512:4], AX.X, ALU.add)
                    p.inc(ins, "vch", 1)
                    vector.wait_ge(p.sem("vch"), p.cnt["vch"])
                p.inc(vector.tensor_copy(prow[0:1, 3:8], rr[0:1, 3:8]), "pr_partials", 1)
                vector.wait_ge(p.sem("pr_trld"), B["pr_trld"] + 16)
                ins = vector.tensor_scalar(ta[0:1, 0:3], trow[0:1, 0:3], 1.0 / QN_TOT, 1e-8, ALU.mult, ALU.add)
                p.inc(ins, "vch", 1)
                vector.wait_ge(p.sem("vch"), p.cnt["vch"])
                ins = vector.tensor_scalar(t2row[0:1, 0:3], ta[0:1, 0:3], 0.5, None, ALU.mult)
                p.inc(ins, "vch", 1)
                vector.wait_ge(p.sem("vch"), p.cnt["vch"])
                vector.tensor_scalar(t2row[0:1, 4:7], t2row[0:1, 0:3], -1.0, None, ALU.mult)
                ins = vector.tensor_copy(t2row[0:1, 3:4], t2row[0:1, 0:1])
                p.inc(ins, "vch", 1)
                vector.wait_ge(p.sem("vch"), p.cnt["vch"])
                p.inc(vector.tensor_copy(t2row[0:1, 7:8], t2row[0:1, 0:1]), "pr_thrmath", 1)
                for i, (nm, iv, ov) in enumerate(TILES):
                    vector.wait_ge(p.sem(f"ld3_{i % 2}"), 16 * (i // 2 + 1))
                    if i == 0:
                        vector.wait_ge(p.sem("pr_thrld"), B["pr_thrld"] + 16)
                    if i >= 2:
                        vector.wait_ge(p.sem(f"qst_{i % 2}"), 16 * (i // 2))
                    wi = i // 4
                    src, dst, sc = buf(nm, i), qbuf(nm, i), abuf(nm)
                    if i >= 1:
                        vector.wait_ge(p.sem("pr_qdone"), i)
                    p.chain(vector, vector.tensor_scalar(sc[:], src[:], thr[:, 4 + wi:5 + wi],
                                                         None, ALU.is_lt), "vch")
                    ins = vector.scalar_tensor_tensor(dst[:], src[:], thr[:, wi:wi + 1], sc[:],
                                                      ALU.is_gt, ALU.subtract)
                    p.inc(ins, "pr_qdone", 1)

            @block.scalar
            def _(scalar):
                scalar.wait_ge(p.sem("pr_accw"), B["pr_accw"] + 1)
                p.inc(scalar.dma_start(out=colb.rearrange("o (pp w) -> pp (o w)", w=4), in_=accw[:]), "pr_cold", 16)
                scalar.wait_ge(p.sem("pr_cold"), B["pr_cold"] + 16)
                p.inc(scalar.dma_start(out=rr[0:1, :], in_=colb[0:1, :]), "pr_rowld", 16)
                scalar.wait_ge(p.sem("pr_partials"), B["pr_partials"] + 1)
                p.inc(scalar.dma_start(out=cc_in[0:1, :], in_=prow[0:1, :]), "pr_ccin", 16)
                scalar.wait_ge(p.sem("pr_ccdone"), B["pr_ccdone"] + 1)
                p.inc(scalar.dma_start(out=trow[0:1, :], in_=cc_out[0:1, :]), "pr_trld", 16)
                scalar.wait_ge(p.sem("pr_thrmath"), B["pr_thrmath"] + 1)
                p.inc(scalar.dma_start(out=rowb[0:1, :], in_=t2row[0:1, :]), "pr_rbst", 16)
                scalar.wait_ge(p.sem("pr_rbst"), B["pr_rbst"] + 16)
                p.inc(scalar.dma_start(out=thr[:], in_=rowb.broadcast_to((128, 8))), "pr_thrld", 16)
                for i, (nm, iv, ov) in enumerate(TILES):
                    scalar.wait_ge(p.sem("pr_qdone"), B["pr_qdone"] + i + 1)
                    p.inc(scalar.dma_start(out=ov, in_=qbuf(nm, i)), f"qst_{i % 2}", 16)
                for j, (ivf, ovf, (cc, aa)) in enumerate(CT):
                    scalar.wait_ge(p.sem(f"cld_{j % 2}"), 16 * (j // 2 + 1))
                    cbv = cb[:, j % 2].rearrange("p (c a) b -> p c a b", c=cc)
                    p.inc(scalar.dma_start(out=ovf, in_=cbv), f"cst_{j % 2}", 16)
                scalar.wait_ge(p.sem("qst_0"), 16 * 6)
                scalar.wait_ge(p.sem("qst_1"), 16 * 6)
                scalar.wait_ge(p.sem("cst_0"), 16 * ((len(CT) + 1) // 2))
                scalar.wait_ge(p.sem("cst_1"), 16 * (len(CT) // 2))

            @block.gpsimd
            def _(gpsimd):
                gpsimd.wait_ge(p.sem("pr_ccin"), B["pr_ccin"] + 16)
                p.inc(gpsimd.collective_compute(
                    "AllReduce", ALU.add, replica_groups=[CORES],
                    ins=[cc_in.opt()], outs=[cc_out.opt()]), "pr_ccdone", 1)
                for j, (ivf, ovf, (cc, aa)) in enumerate(CT):
                    if j >= 2:
                        gpsimd.wait_ge(p.sem(f"cst_{j % 2}"), 16 * (j // 2))
                    cbv = cb[:, j % 2].rearrange("p (c a) b -> p c a b", c=cc)
                    p.inc(gpsimd.dma_start(out=cbv, in_=ivf), f"cld_{j % 2}", 16)
                if ag_q is not None:
                    for idx, (src, dst) in enumerate(ag_q):
                        gpsimd.wait_ge(p.sem("qst_0"), 32 * (idx + 1))
                        gpsimd.wait_ge(p.sem("qst_1"), 32 * (idx + 1))
                        p.inc(gpsimd.collective_compute(
                            "AllGather", ALU.bypass, replica_groups=[CORES],
                            ins=[src.opt()], outs=[dst.opt()]), "k_cc", 1)
                if ag_w is not None:
                    gpsimd.wait_ge(p.sem("cst_0"), 16 * ((len(CT) + 1) // 2))
                    gpsimd.wait_ge(p.sem("cst_1"), 16 * (len(CT) // 2))
                    for src, dst in ag_w:
                        p.inc(gpsimd.collective_compute(
                            "AllGather", ALU.bypass, replica_groups=[CORES],
                            ins=[src.opt()], outs=[dst.opt()]), "k_cc", 1)



def build_prep():
    p = P()
    nc = p.nc
    w1 = nc.dram_tensor("w1T_s", [256, F1], F32, kind="ExternalInput").ap()
    w2 = nc.dram_tensor("w2T_s", [256, F1], F32, kind="ExternalInput").ap()
    w3 = nc.dram_tensor("w3T_s", [F1, 256], F32, kind="ExternalInput").ap()
    q1 = nc.dram_tensor("q1T_s", [256, F1], BF16, kind="ExternalOutput").ap()
    q2 = nc.dram_tensor("q2T_s", [256, F1], BF16, kind="ExternalOutput").ap()
    q3 = nc.dram_tensor("q3T_s", [F1, 256], BF16, kind="ExternalOutput").ap()
    cin, cout = {}, {}
    for nm, (r, cdim) in CAST_SHARDS.items():
        cin[nm] = nc.dram_tensor(nm + "_s", [r // NC, cdim], F32, kind="ExternalInput").ap()
        cout[nm] = nc.dram_tensor(nm + "16_s", [r // NC, cdim], BF16, kind="ExternalOutput").ap()
    cc_in = nc.dram_tensor("cc_in", [1, 8], F32).ap()
    cc_out = nc.dram_tensor("cc_out", [1, 8], F32).ap()
    colb = nc.dram_tensor("colb", [1, 512], F32).ap()
    rowb = nc.dram_tensor("rowb", [1, 8], F32).ap()
    with p.stack:
        emit_prep(p, w1, w2, w3, q1, q2, q3, cin, cout, cc_in, cc_out, colb, rowb)
    return p.nc


def _prep_inputs(inp):
    w1T = np.ascontiguousarray(inp["L0_ffn_w1"].T)
    w2T = np.ascontiguousarray(inp["L0_ffn_w2"].T)
    w3T = np.ascontiguousarray(inp["L0_ffn_w3"].T)
    casts = {"W0aT": np.ascontiguousarray(inp["L1_ffn_W0a"].T),
             "W0bT": np.ascontiguousarray(inp["L1_ffn_W0b"].T),
             "W1aT": np.ascontiguousarray(inp["L1_ffn_W1a"].T),
             "W1bT": np.ascontiguousarray(inp["L1_ffn_W1b"].T),
             "Wck": np.ascontiguousarray(inp["L1_attn_Wck"]),
             "Wcv": np.ascontiguousarray(inp["L1_attn_Wcv"])}
    maps = []
    for c in range(NC):
        m = {
            "w1T_s": np.ascontiguousarray(w1T[c * 256:(c + 1) * 256]),
            "w2T_s": np.ascontiguousarray(w2T[c * 256:(c + 1) * 256]),
            "w3T_s": np.ascontiguousarray(w3T[:, c * 256:(c + 1) * 256]),
        }
        for nm, arr in casts.items():
            rs = arr.shape[0] // NC
            m[nm + "_s"] = np.ascontiguousarray(arr[c * rs:(c + 1) * rs])
        maps.append(m)
    return maps


def _prep_gather(results):
    out = {}
    out["q1T"] = np.vstack([results[c]["q1T_s"] for c in range(NC)])
    out["q2T"] = np.vstack([results[c]["q2T_s"] for c in range(NC)])
    out["q3T"] = np.ascontiguousarray(np.hstack([results[c]["q3T_s"] for c in range(NC)]))
    for nm in CAST_SHARDS:
        out[nm + "16"] = np.vstack([results[c][nm + "16_s"] for c in range(NC)])
    return out



def emit_ln(p, block, xs, y, ttw, ones_bf, b1, b2, mb, rb, tmp, tmp2, xbf, xsq,
            ps_s, ps_q, eps_ap, srow, sqrow, r1, r2, sd, rstd,
            gp_wait=(), sc_wait=(), te_wait=(), ve_wait=(), gains=None, nch=CD,
            prev=False):
    B = p.snap()

    @block.gpsimd
    def _(gp):
        for s, v in gp_wait:
            gp.wait_ge(p.sem(s), v)
        for c in range(nch):
            if c >= 2:
                gp.wait_ge(p.sem("ln_mm"), B["ln_mm"] + c - 1)
            p.inc(gp.tensor_copy(xbf[:, c % 2], xs[:, c]), "ln_xb", 1)

    @block.scalar
    def _(sc):
        for s, v in sc_wait:
            sc.wait_ge(p.sem(s), v)
        for c in range(nch):
            if c >= 2:
                sc.wait_ge(p.sem("ln_mm"), B["ln_mm"] + c - 1)
            p.inc(sc.activation(xsq[:, c % 2], xs[:, c], AF.Square), "ln_xq", 1)
        sc.wait_ge(p.sem("ln_mm"), B["ln_mm"] + nch)
        if prev:
            sc.wait_ge(p.sem("ln_bst"), B["ln_bst"])
        p.inc(sc.activation(srow[0:1, :], ps_s[0:1, :], AF.Identity, scale=1.0 / D), "ln_sr", 1)
        p.inc(sc.activation(sqrow[0:1, :], ps_q[0:1, :], AF.Identity, scale=1.0 / D), "ln_sr", 1)
        sc.wait_ge(p.sem("ln_var"), B["ln_var"] + 1)
        p.inc(sc.activation(sd[0:1, :], r2[0:1, :], AF.Sqrt, bias=eps_ap), "ln_sd", 1)
        sc.wait_ge(p.sem("ln_rstd"), B["ln_rstd"] + 1)
        if prev:
            sc.wait_ge(p.sem("ln_bld"), B["ln_bld"])
        p.inc(sc.dma_start(out=b1[0:1, :], in_=srow[0:1, :]), "ln_bst", 16)
        p.inc(sc.dma_start(out=b2[0:1, :], in_=rstd[0:1, :]), "ln_bst", 16)
        sc.wait_ge(p.sem("ln_bst"), B["ln_bst"] + 32)
        p.inc(sc.dma_start(out=mb[:], in_=b1.broadcast_to((128, ttw))), "ln_bld", 16)
        p.inc(sc.dma_start(out=rb[:], in_=b2.broadcast_to((128, ttw))), "ln_bld", 16)

    @block.tensor
    def _(te):
        for s, v in te_wait:
            te.wait_ge(p.sem(s), v)
        for c in range(nch):
            te.wait_ge(p.sem("ln_xb"), B["ln_xb"] + c + 1)
            te.wait_ge(p.sem("ln_xq"), B["ln_xq"] + c + 1)
            te.matmul(ps_s[0:1, :], ones_bf[:, 0:1], xbf[:, c % 2],
                      start=(c == 0), stop=(c == nch - 1))
            ins = te.matmul(ps_q[0:1, :], ones_bf[:, 0:1], xsq[:, c % 2],
                            start=(c == 0), stop=(c == nch - 1))
            p.inc(ins, "ln_mm", 1)

    @block.vector
    def _(ve):
        for s, v in ve_wait:
            ve.wait_ge(p.sem(s), v)
        ve.wait_ge(p.sem("ln_sr"), B["ln_sr"] + 2)
        p.inc(ve.tensor_mul(r1[0:1, :], srow[0:1, :], srow[0:1, :]), "ln_m2", 1)
        ve.wait_ge(p.sem("ln_m2"), B["ln_m2"] + 1)
        p.inc(ve.tensor_sub(r2[0:1, :], sqrow[0:1, :], r1[0:1, :]), "ln_var", 1)
        ve.wait_ge(p.sem("ln_sd"), B["ln_sd"] + 1)
        p.inc(ve.reciprocal(rstd[0:1, :], sd[0:1, :]), "ln_rstd", 1)
        ve.wait_ge(p.sem("ln_bld"), B["ln_bld"] + 32)
        for c in range(nch):
            p.chain(ve, ve.tensor_sub(tmp[:, c % 2], xs[:, c], mb[:]), "ln_ch")
            if gains is None:
                ins = ve.tensor_mul(y[:, c], tmp[:, c % 2], rb[:])
            else:
                g_cm, b_cm = gains
                p.chain(ve, ve.tensor_mul(tmp2[:, c % 2], tmp[:, c % 2], rb[:]), "ln_ch")
                ins = ve.tensor_scalar(y[:, c], tmp2[:, c % 2], g_cm[:, c:c + 1],
                                       b_cm[:, c:c + 1], ALU.mult, ALU.add)
            p.inc(ins, "ln_y", 1)
    return p.cnt["ln_y"]



MG_B = 256
NGRP_B = F1 // MG_B
FCH = F1 // 128


def emit_l0ffn(p, x, q1T, q2T, q3T, xo, ones_bf, cst, b1, b2, ps, q3q=None,
               sw_jobs=(), q_gate=None):
    nc = p.nc
    p.pool_reset()
    B = p.snap()
    xv, xov = chunked(x), chunked(xo)
    q1v, q2v = chunked(q1T), chunked(q2T)
    q3v = chunked(q3T) if q3q is None else None

    with (nc.sbuf_tensor("ff_xs", [128, CD, 512], F32) as xs,
          nc.sbuf_tensor("ff_yb", [128, CD, 512], BF16) as yb,
          nc.sbuf_tensor("ff_tmp", [128, 2, 512], F32) as tmp,
          nc.sbuf_tensor("ff_xbf", [128, 2, 512], BF16) as xbf,
          nc.sbuf_tensor("ff_xsq", [128, 2, 512], BF16) as xsq,
          nc.sbuf_tensor("ff_mb", [128, 512], F32) as mb,
          nc.sbuf_tensor("ff_rb", [128, 512], F32) as rb,
          nc.sbuf_tensor("ff_rows", [1, 6, 512], F32) as rows,
          nc.sbuf_tensor("ff_w1s", [128, 2, CD, MG_B], BF16) as w1s,
          nc.sbuf_tensor("ff_w2s", [128, 2, CD, MG_B], BF16) as w2s,
          nc.sbuf_tensor("ff_hb", [128, FCH, 512], BF16) as hb,
          nc.sbuf_tensor("ff_sgt", [128, 2, 512], BF16) as sgt,
          nc.sbuf_tensor("ff_q3s", [128, 2, 4, 256], BF16) as q3s,
          nc.sbuf_tensor("ff_ob", [128, 8, 512], F32) as ob,
          nc.Block() as block):
        srow, sqrow, r1, r2, sd, rstd = (rows[:, i] for i in range(6))
        q3sf = q3s[:].rearrange("p h r c -> p h (r c)")

        if sw_jobs:
            @block.gpsimd
            def _(gp):
                for src, dst in sw_jobs:
                    p.inc(gp.dma_start(out=dst[:], in_=src[:]), "k_d2c", 16)

        @block.sync
        def _(sync):
            for tt in range(2):
                tsl = slice(tt * 512, (tt + 1) * 512)
                if tt > 0:
                    sync.wait_ge(p.sem("ff_obd"), B["ff_obd"] + 16 * tt)
                p.inc(sync.dma_start(out=xs[:], in_=xv[:, :, tsl]), "ff_lx", 16)
                for g in range(NGRP_B):
                    G = tt * NGRP_B + g
                    if G == 0 and q_gate is not None:
                        sync.wait_ge(p.sem(q_gate[0]), q_gate[1])
                    if G >= 2:
                        sync.wait_ge(p.sem("ff_mmh"), B["ff_mmh"] + 2 * (G - 1))
                    msl = slice(g * MG_B, (g + 1) * MG_B)
                    p.inc(sync.dma_start(out=w1s[:, G % 2], in_=q1v[:, :, msl]),
                          f"ff_lw1_{G % 2}", 16)
                    p.inc(sync.dma_start(out=w2s[:, G % 2], in_=q2v[:, :, msl]),
                          f"ff_lw2_{G % 2}", 16)
                for dg in range(2):
                    DG = tt * 2 + dg
                    for f in range(FCH):
                        FD = DG * FCH + f
                        if FD == 0 and q_gate is not None:
                            sync.wait_ge(p.sem(q_gate[0]), q_gate[2])
                        if FD >= 2:
                            sync.wait_ge(p.sem("ff_q3c"), B["ff_q3c"] + FD - 1)
                        dsl = slice(dg * 1024, (dg + 1) * 1024)
                        src = (q3v[:, f, dsl] if q3q is None
                               else q3q[:, f, dg * 4:(dg + 1) * 4, :])
                        dst = q3sf[:, FD % 2] if q3q is None else q3s[:, FD % 2]
                        p.inc(sync.dma_start(out=dst, in_=src),
                              f"ff_lq3_{FD % 2}", 16)

        yc = [0, 0]
        for tt in range(2):
            tsl = slice(tt * 512, (tt + 1) * 512)
            yc[tt] = emit_ln(p, block, xs, yb, 512, ones_bf, b1, b2, mb, rb,
                             tmp, None, xbf, xsq, ps[4][0:1, :], ps[5][0:1, :],
                             cst[0:1, 0:1], srow, sqrow, r1, r2, sd, rstd,
                             gp_wait=[("ff_lx", B["ff_lx"] + 16 * (tt + 1))],
                             sc_wait=[("ff_lx", B["ff_lx"] + 16 * (tt + 1))],
                             te_wait=[("ff_obd", B["ff_obd"] + 16 * tt)],
                             ve_wait=[("ff_q3c", B["ff_q3c"] + 2 * FCH * tt)],
                             prev=True)

            @block.tensor
            def _(te, tt=tt):
                for g in range(NGRP_B):
                    G = tt * NGRP_B + g
                    te.wait_ge(p.sem(f"ff_lw1_{G % 2}"), B[f"ff_lw1_{G % 2}"] + 16 * (G // 2 + 1))
                    te.wait_ge(p.sem(f"ff_lw2_{G % 2}"), B[f"ff_lw2_{G % 2}"] + 16 * (G // 2 + 1))
                    if g == 0:
                        te.wait_ge(p.sem("ln_y"), yc[tt])
                    for m in range(2):
                        MH = G * 2 + m
                        if MH >= 2:
                            te.wait_ge(p.sem("ff_hdone"), B["ff_hdone"] + MH - 1)
                        pa, pb_ = ps[2 * (MH % 2)], ps[2 * (MH % 2) + 1]
                        msl = slice(m * 128, (m + 1) * 128)
                        for c in range(CD):
                            te.matmul(pa[:], w1s[:, G % 2, c, msl], yb[:, c],
                                      start=(c == 0), stop=(c == CD - 1))
                        for c in range(CD):
                            ins = te.matmul(pb_[:], w2s[:, G % 2, c, msl], yb[:, c],
                                            start=(c == 0), stop=(c == CD - 1))
                        p.inc(ins, "ff_mmh", 1)

            @block.scalar
            def _(sc, tt=tt):
                for g in range(NGRP_B):
                    for m in range(2):
                        MH = (tt * NGRP_B + g) * 2 + m
                        sc.wait_ge(p.sem("ff_mmh"), B["ff_mmh"] + MH + 1)
                        if MH >= 2:
                            sc.wait_ge(p.sem("ff_hdone"), B["ff_hdone"] + MH - 1)
                        p.inc(sc.activation(sgt[:, MH % 2], ps[2 * (MH % 2)][:],
                                            AF.Sigmoid), "ff_sig", 1)

            @block.vector
            def _(ve, tt=tt):
                for g in range(NGRP_B):
                    for m in range(2):
                        MH = (tt * NGRP_B + g) * 2 + m
                        ve.wait_ge(p.sem("ff_sig"), B["ff_sig"] + MH + 1)
                        col = g * 2 + m
                        p.inc(ve.tensor_mul(hb[:, col], sgt[:, MH % 2],
                                            ps[2 * (MH % 2) + 1][:]), "ff_hdone", 1)

            @block.tensor
            def _(te, tt=tt):
                for dg in range(2):
                    DG = tt * 2 + dg
                    te.wait_ge(p.sem("ff_hdone"), B["ff_hdone"] + 44 * (tt + 1))
                    te.wait_ge(p.sem("ff_obd"), B["ff_obd"] + 8 * DG)
                    for f in range(FCH):
                        FD = DG * FCH + f
                        te.wait_ge(p.sem(f"ff_lq3_{FD % 2}"), B[f"ff_lq3_{FD % 2}"] + 16 * (FD // 2 + 1))
                        for dout in range(8):
                            ins = te.matmul(ps[dout][:],
                                            q3sf[:, FD % 2, dout * 128:(dout + 1) * 128],
                                            hb[:, f], start=(f == 0), stop=(f == FCH - 1))
                        p.inc(ins, "ff_q3c", 1)

            @block.vector
            def _(ve, tt=tt):
                for dg in range(2):
                    DG = tt * 2 + dg
                    ve.wait_ge(p.sem("ff_q3c"), B["ff_q3c"] + FCH * (DG + 1))
                    ve.wait_ge(p.sem("ff_obst"), B["ff_obst"] + 16 * DG)
                    for dout in range(8):
                        ch = dg * 8 + dout
                        p.inc(ve.scalar_tensor_tensor(ob[:, dout], ps[dout][:], 0.5,
                                                      xs[:, ch], ALU.mult, ALU.add),
                              "ff_obd", 1)

            @block.scalar
            def _(sc, tt=tt, tsl=tsl):
                for dg in range(2):
                    DG = tt * 2 + dg
                    sc.wait_ge(p.sem("ff_obd"), B["ff_obd"] + 8 * (DG + 1))
                    p.inc(sc.dma_start(out=xov[:, dg * 8:(dg + 1) * 8, tsl], in_=ob[:]),
                          "ff_obst", 16)

        @block.scalar
        def _(sc):
            sc.wait_ge(p.sem("ff_obst"), B["ff_obst"] + 16 * 4)


def _mk_consts(p):
    nc = p.nc
    ones_in = nc.dram_tensor("ones_bf", [128, 1], BF16, kind="ExternalInput").ap()
    cst_in = nc.dram_tensor("cstf", [1, 8], F32, kind="ExternalInput").ap()
    b1 = nc.dram_tensor("lnb1", [1, 512], F32).ap()
    b2 = nc.dram_tensor("lnb2", [1, 512], F32).ap()
    ones_bf = p.sbuf("ones_sb", [128, 1], BF16)
    cst = p.sbuf("cst_sb", [1, 8], F32)
    with nc.Block() as block:
        @block.sync
        def _(sync):
            p.inc(sync.dma_start(out=ones_bf[:], in_=ones_in[:]), "k_cst", 16)
            p.inc(sync.dma_start(out=cst[0:1, :], in_=cst_in[0:1, :]), "k_cst", 16)
    p.fence(["k_cst"])
    return ones_bf, cst, b1, b2


def build_l0ffn():
    p = P()
    nc = p.nc
    x = nc.dram_tensor("x", [D, T], F32, kind="ExternalInput").ap()
    q1T = nc.dram_tensor("q1T", [D, F1], BF16, kind="ExternalInput").ap()
    q2T = nc.dram_tensor("q2T", [D, F1], BF16, kind="ExternalInput").ap()
    q3T = nc.dram_tensor("q3T", [F1, D], BF16, kind="ExternalInput").ap()
    xo = nc.dram_tensor("xo", [D, T], F32, kind="ExternalOutput").ap()
    with p.stack:
        ones_bf, cst, b1, b2 = _mk_consts(p)
        ps = [p.psum(f"ps{i}", [128, 512]) for i in range(8)]
        emit_l0ffn(p, x, q1T, q2T, q3T, xo, ones_bf, cst, b1, b2, ps)
    return p.nc



ATT_KTS = {0: list(range(12)), 1: list(range(16))}


def emit_l0attn(p, x, xfh, wqkv, bqkv_cm, bqkv_row, wproj, bproj_cm, kbias_in,
                qbias_in, masks_in, ones2_in, xo, ones_bf, cst, b1, b2, ps,
                vdram, adram, xfh_gate=None):
    nc = p.nc
    p.pool_reset()
    B = p.snap()

    xv, xfv = chunked(x), chunked(xfh)
    wqv = chunked(wqkv)
    wpv = chunked(wproj)
    vdv = chunked(vdram)
    adv = chunked(adram)
    xov = chunked(xo)

    LKT = [len(ATT_KTS[hq % 2]) for hq in range(32)]
    S_BASE = [0]
    for hq in range(32):
        S_BASE.append(S_BASE[-1] + LKT[hq])
    MASK_IDX = {pos: i for i, pos in enumerate(MASK_POS)}
    MK = [sum(1 for kt in ATT_KTS[hq % 2] if (hq % 2, kt) in MASK_IDX) for hq in range(32)]
    M_BASE = [0]
    for hq in range(32):
        M_BASE.append(M_BASE[-1] + MK[hq])

    with contextlib.ExitStack() as _ats:
        _sb = lambda n, s, d: _ats.enter_context(nc.sbuf_tensor(n, s, d))
        masks = _sb("at_masks", [128, 16, 512], BF16)
        kb = _sb("at_kb", [128, NH * 16], F32)
        ones2 = _sb("at_ones2", [2, 128], BF16)
        bq = _sb("at_bq", [128, 48], F32)
        bvr = _sb("at_bvr", [1, 2048], BF16)
        bp = _sb("at_bp", [128, CD], F32)
        xln_o = _sb("at_xln_o", [128, CD, T], BF16)
        xln_f = _sb("at_xln_f", [128, CD, T], BF16)
        rows = _sb("at_rows", [1, 6, 512], F32)
        srow, sqrow, r1, r2, sd, rstd = (rows[:, i] for i in range(6))

        NCON = 6
        with nc.sbuf_tensor("at_bqr", [1, 3 * D], F32) as bqr, nc.Block() as block:
            @block.sync
            def _(sync):
                for dst, srcv in ((masks[:], masks_in.rearrange("i p f -> p i f")),
                                  (kb[:], kbias_in[:]),
                                  (ones2[:], ones2_in[:]),
                                  (bq[:], bqkv_cm[:]), (bqr[0:1, :], bqkv_row[0:1, :]),
                                  (bp[:], bproj_cm[:])):
                    p.inc(sync.dma_start(out=dst, in_=srcv), "at_lcst", 16)

            @block.vector
            def _(ve):
                ve.wait_ge(p.sem("at_lcst"), B["at_lcst"] + 16 * NCON)
                p.inc(ve.tensor_copy(bvr[0:1, :], bqr[0:1, 4096:6144]), "at_bvrd", 1)

        with (nc.sbuf_tensor("xbuf", [128, CD, 512], F32) as xbuf,
              nc.sbuf_tensor("lntmp", [128, 2, 512], F32) as lntmp,
              nc.sbuf_tensor("lnxb", [128, 2, 512], BF16) as lnxb,
              nc.sbuf_tensor("lnxq", [128, 2, 512], BF16) as lnxq,
              nc.sbuf_tensor("lnmb", [128, 512], F32) as lnmb,
              nc.sbuf_tensor("lnrb", [128, 512], F32) as lnrb):
            passes = [(xv, xln_o, 0), (xv, xln_o, 1), (xfv, xln_f, 0), (xfv, xln_f, 1)]
            ycnt = []
            with nc.Block() as block:
                for k, (srcv, dstv, tt) in enumerate(passes):
                    tsl = slice(tt * 512, (tt + 1) * 512)

                    @block.sync
                    def _(sync, k=k, srcv=srcv, tsl=tsl):
                        if k > 0:
                            sync.wait_ge(p.sem("ln_y"), ycnt[k - 1])
                        if k == 2 and xfh_gate is not None:
                            sync.wait_ge(p.sem(xfh_gate[0]), xfh_gate[1])
                        p.inc(sync.dma_start(out=xbuf[:], in_=srcv[:, :, tsl]), "at_lx", 16)
                    ycnt.append(emit_ln(p, block, xbuf, dstv[:, :, tsl], 512, ones_bf,
                            b1, b2, lnmb, lnrb, lntmp, None, lnxb, lnxq,
                            ps[4][0:1, :], ps[5][0:1, :], cst[0:1, 0:1],
                            srow, sqrow, r1, r2, sd, rstd,
                            gp_wait=[("at_lx", B["at_lx"] + 16 * (k + 1))],
                            sc_wait=[("at_lx", B["at_lx"] + 16 * (k + 1)),
                                     ("at_lcst", B["at_lcst"] + 16 * NCON)],
                            te_wait=[("at_lcst", B["at_lcst"] + 16 * NCON)],
                            prev=True))

        with (nc.sbuf_tensor("wvs", [128, 2, CD, 512], BF16) as wvs,
              nc.sbuf_tensor("vt", [128, 2, 512], BF16) as vt):
            with nc.Block() as block:
                @block.gpsimd
                def _(gp):
                    gp.wait_ge(p.sem("ln_y"), ycnt[3])
                    for gn in range(4):
                        if gn >= 2:
                            gp.wait_ge(p.sem("at_vmm"), B["at_vmm"] + 16 * (gn - 1))
                        csl = slice(4096 + gn * 512, 4096 + (gn + 1) * 512)
                        p.inc(gp.dma_start(out=wvs[:, gn % 2], in_=wqv[:, :, csl]),
                              f"at_lwv{gn % 2}", 16)

                @block.tensor
                def _(te):
                    te.wait_ge(p.sem("ln_y"), ycnt[3])
                    te.wait_ge(p.sem("at_bvrd"), B["at_bvrd"] + 1)
                    vm = 0
                    for gn in range(4):
                        te.wait_ge(p.sem(f"at_lwv{gn % 2}"), B[f"at_lwv{gn % 2}"] + 16 * (gn // 2 + 1))
                        for s in range(16):
                            pb = ps[vm % 2]
                            if vm >= 2:
                                te.wait_ge(p.sem("at_vev"), B["at_vev"] + vm - 1)
                            xsrc = xln_f if s < 8 else xln_o
                            ssl = slice((s % 8) * 128, (s % 8 + 1) * 128)
                            for c in range(CD):
                                te.matmul(pb[:], xsrc[:, c, ssl], wvs[:, gn % 2, c, :],
                                          start=(c == 0), stop=False)
                            ins = te.matmul(pb[:], ones2[0:1, :],
                                            bvr[0:1, gn * 512:(gn + 1) * 512],
                                            start=False, stop=True)
                            p.inc(ins, "at_vmm", 1)
                            vm += 1

                @block.scalar
                def _(sc):
                    vm = 0
                    for gn in range(4):
                        for s in range(16):
                            sc.wait_ge(p.sem("at_vmm"), B["at_vmm"] + vm + 1)
                            if vm >= 2:
                                sc.wait_ge(p.sem(f"at_vst{vm % 2}"), B[f"at_vst{vm % 2}"] + 16 * (vm // 2))
                            p.inc(sc.activation(vt[:, vm % 2], ps[vm % 2][:], AF.Identity),
                                  "at_vev", 1)
                            sc.wait_ge(p.sem("at_vev"), B["at_vev"] + vm + 1)
                            p.inc(sc.dma_start(out=vdv[:, s, gn * 512:(gn + 1) * 512],
                                               in_=vt[:, vm % 2]), f"at_vst{vm % 2}", 16)
                            vm += 1

        with (nc.sbuf_tensor("wks", [128, 2, CD, 128], BF16) as wks,
              nc.sbuf_tensor("wqs", [128, 2, CD, 128], BF16) as wqs,
              nc.sbuf_tensor("kh", [128, 2, 2048], BF16) as kh,
              nc.sbuf_tensor("qh", [128, 2, 1024], BF16) as qh,
              nc.sbuf_tensor("vh", [128, 1, CD, 128], BF16) as vh,
              nc.sbuf_tensor("qb", [2, 2, 512], BF16) as qb,
              nc.sbuf_tensor("pt", [128, 2, 512], BF16) as pt,
              nc.sbuf_tensor("ptm", [128, 2, 512], BF16) as ptm,
              nc.sbuf_tensor("ao", [128, 2, 512], BF16) as ao,
              nc.sbuf_tensor("avs", [128, 2, 512], F32) as avs,
              nc.sbuf_tensor("nrmb1", [1, 2, 512], BF16) as nrmb1,
              nc.sbuf_tensor("nrmb2", [1, 2, 512], BF16) as nrmb2):
            nrm = rows[:, 0:4]
            with nc.Block() as block:
                @block.gpsimd
                def _(gp):
                    gp.wait_ge(p.sem("at_vst0"), B["at_vst0"] + 16 * 32)
                    gp.wait_ge(p.sem("at_vst1"), B["at_vst1"] + 16 * 32)
                    for h in range(NH):
                        if h >= 2:
                            gp.wait_ge(p.sem("at_qev"), B["at_qev"] + 2 * (h - 1))
                        p.inc(gp.dma_start(out=wks[:, h % 2],
                                           in_=wqv[:, :, 2048 + h * 128:2048 + (h + 1) * 128]),
                              f"at_lwk{h % 2}", 16)
                        p.inc(gp.dma_start(out=wqs[:, h % 2],
                                           in_=wqv[:, :, h * 128:(h + 1) * 128]),
                              f"at_lwq{h % 2}", 16)

                @block.sync
                def _(sync):
                    sync.wait_ge(p.sem("at_vst0"), B["at_vst0"] + 16 * 32)
                    sync.wait_ge(p.sem("at_vst1"), B["at_vst1"] + 16 * 32)
                    for h in range(NH):
                        if h >= 1:
                            sync.wait_ge(p.sem("at_avmm"), B["at_avmm"] + S_BASE[2 * h])
                        p.inc(sync.dma_start(out=vh[:, 0],
                                             in_=vdv[:, :, h * 128:(h + 1) * 128]),
                              "at_lvh", 16)
                        for qt in range(2):
                            hq = h * 2 + qt
                            if hq >= 2:
                                sync.wait_ge(p.sem("at_smm"), B["at_smm"] + S_BASE[hq - 1])
                            p.inc(sync.dma_start(out=qb[:, hq % 2],
                                                 in_=qbias_in[h, qt]),
                                f"at_lqb{hq % 2}", 16)

                @block.tensor
                def _(te):
                    for h in range(NH):
                        te.wait_ge(p.sem(f"at_lwk{h % 2}"), B[f"at_lwk{h % 2}"] + 16 * (h // 2 + 1))
                        if h >= 2:
                            te.wait_ge(p.sem("at_smm"), B["at_smm"] + S_BASE[2 * (h - 1)])
                        for st in range(4):
                            if st >= 2:
                                te.wait_ge(p.sem("at_kev"), B["at_kev"] + 4 * h + st - 1)
                            xsrc = xln_f if st < 2 else xln_o
                            tslq = slice((st % 2) * 512, (st % 2 + 1) * 512)
                            for c in range(CD):
                                ins = te.matmul(ps[2 + st % 2][:], wks[:, h % 2, c, :],
                                                xsrc[:, c, tslq], start=(c == 0),
                                                stop=(c == CD - 1))
                            p.inc(ins, "at_kmm", 1)
                        te.wait_ge(p.sem(f"at_lwq{h % 2}"), B[f"at_lwq{h % 2}"] + 16 * (h // 2 + 1))
                        te.wait_ge(p.sem("at_kev"), B["at_kev"] + 4 * (h + 1))
                        for qt in range(2):
                            for c in range(CD):
                                ins = te.matmul(ps[2 + qt][:], wqs[:, h % 2, c, :],
                                                xln_o[:, c, qt * 512:(qt + 1) * 512],
                                                start=(c == 0), stop=(c == CD - 1))
                            p.inc(ins, "at_qmm", 1)
                        te.wait_ge(p.sem("at_lvh"), B["at_lvh"] + 16 * (h + 1))
                        te.wait_ge(p.sem("at_qev"), B["at_qev"] + 2 * (h + 1))
                        for qt in range(2):
                            hq = h * 2 + qt
                            te.wait_ge(p.sem(f"at_lqb{hq % 2}"), B[f"at_lqb{hq % 2}"] + 16 * (hq // 2 + 1))
                            if hq > 0:
                                te.wait_ge(p.sem("at_avev"), B["at_avev"] + hq)
                                te.wait_ge(p.sem("at_sev"), B["at_sev"] + hq)
                            kts = ATT_KTS[qt]
                            nm = 0
                            for j, kt in enumerate(kts):
                                pb = ps[j % 2]
                                if j >= 2:
                                    te.wait_ge(p.sem("at_expdone"), B["at_expdone"] + S_BASE[hq] + j - 1)
                                te.matmul(pb[:], kh[:, h % 2, kt * 128:(kt + 1) * 128],
                                          qh[:, h % 2, qt * 512:(qt + 1) * 512],
                                          start=True, stop=False)
                                ins = te.matmul(pb[:], ones2[:, :],
                                                qb[:, hq % 2], start=False, stop=True)
                                p.inc(ins, "at_smm", 1)
                                masked = (qt, kt) in MASK_IDX
                                if masked:
                                    te.wait_ge(p.sem("at_mdone"), B["at_mdone"] + M_BASE[hq] + nm + 1)
                                    nm += 1
                                    psrc = ptm
                                else:
                                    te.wait_ge(p.sem("at_expdone"), B["at_expdone"] + S_BASE[hq] + j + 1)
                                    psrc = pt
                                te.matmul(ps[4][:], vh[:, 0, kt, :], psrc[:, j % 2],
                                          start=(j == 0), stop=(j == len(kts) - 1))
                                ins = te.matmul(ps[5][0:1, :], ones_bf[:, 0:1],
                                                psrc[:, j % 2],
                                                start=(j == 0), stop=(j == len(kts) - 1))
                                p.inc(ins, "at_avmm", 1)
                            if hq >= 1:
                                q_ = hq - 1
                                te.wait_ge(p.sem("at_hilo"), B["at_hilo"] + q_ + 1)
                                if q_ >= 2:
                                    te.wait_ge(p.sem("at_avdone"), B["at_avdone"] + q_ - 1)
                                te.matmul(ps[6 + q_ % 2][:], ones2[0:1, :], nrmb1[0:1, q_ % 2],
                                          start=True, stop=False)
                                p.inc(te.matmul(ps[6 + q_ % 2][:], ones2[0:1, :], nrmb2[0:1, q_ % 2],
                                                start=False, stop=True), "at_bcmm", 1)
                    q_ = 2 * NH - 1
                    te.wait_ge(p.sem("at_hilo"), B["at_hilo"] + q_ + 1)
                    te.wait_ge(p.sem("at_avdone"), B["at_avdone"] + q_ - 1)
                    te.matmul(ps[6 + q_ % 2][:], ones2[0:1, :], nrmb1[0:1, q_ % 2],
                              start=True, stop=False)
                    p.inc(te.matmul(ps[6 + q_ % 2][:], ones2[0:1, :], nrmb2[0:1, q_ % 2],
                                    start=False, stop=True), "at_bcmm", 1)

                @block.scalar
                def _(sc):
                    for h in range(NH):
                        for st in range(4):
                            sc.wait_ge(p.sem("at_kmm"), B["at_kmm"] + 4 * h + st + 1)
                            p.inc(sc.activation(kh[:, h % 2, st * 512:(st + 1) * 512],
                                                ps[2 + st % 2][:], AF.Identity,
                                                bias=bq[:, 16 + h:17 + h]), "at_kev", 1)
                        for qt in range(2):
                            sc.wait_ge(p.sem("at_qmm"), B["at_qmm"] + 2 * h + qt + 1)
                            p.inc(sc.activation(qh[:, h % 2, qt * 512:(qt + 1) * 512],
                                                ps[2 + qt][:], AF.Identity,
                                                bias=bq[:, h:h + 1]), "at_qev", 1)
                        for qt in range(2):
                            hq = h * 2 + qt
                            kts = ATT_KTS[qt]
                            for j, kt in enumerate(kts):
                                g = S_BASE[hq] + j
                                sc.wait_ge(p.sem("at_smm"), B["at_smm"] + g + 1)
                                if g >= 2:
                                    sc.wait_ge(p.sem("at_avmm"), B["at_avmm"] + g - 1)
                                p.inc(sc.activation(pt[:, j % 2], ps[j % 2][:], AF.Exp,
                                                    scale=1.0 / 128.0,
                                                    bias=kb[:, h * 16 + kt:h * 16 + kt + 1]),
                                      "at_expdone", 1)
                            sc.wait_ge(p.sem("at_avmm"), B["at_avmm"] + S_BASE[hq + 1])
                            sc.wait_ge(p.sem("at_hilo"), B["at_hilo"] + hq)
                            p.inc(sc.activation(nrm[0:1, 0, :], ps[5][0:1, :], AF.Identity),
                                  "at_sev", 1)
                            if hq >= 1:
                                sc.wait_ge(p.sem("at_avdone"), B["at_avdone"] + hq - 1)
                            p.inc(sc.activation(avs[:, hq % 2], ps[4][:], AF.Identity), "at_avev", 1)
                            if hq >= 1:
                                sc.wait_ge(p.sem("at_avdone"), B["at_avdone"] + hq)
                                ph, pq = (hq - 1) // 2, (hq - 1) % 2
                                p.inc(sc.dma_start(
                                    out=adv[:, ph, pq * 512:(pq + 1) * 512],
                                    in_=ao[:, (hq - 1) % 2]), f"at_ast{(hq - 1) % 2}", 16)
                    sc.wait_ge(p.sem("at_avdone"), B["at_avdone"] + 32)
                    p.inc(sc.dma_start(out=adv[:, 15, 512:1024], in_=ao[:, 1]), "at_ast1", 16)

                @block.vector
                def _(ve):
                    for h in range(NH):
                        for qt in range(2):
                            hq = h * 2 + qt
                            kts = ATT_KTS[qt]
                            nm = 0
                            for j, kt in enumerate(kts):
                                if (qt, kt) in MASK_IDX:
                                    g = S_BASE[hq] + j
                                    ve.wait_ge(p.sem("at_expdone"), B["at_expdone"] + g + 1)
                                    if M_BASE[hq] + nm >= 2:
                                        ve.wait_ge(p.sem("at_avmm"), B["at_avmm"] + g - 1)
                                    p.inc(ve.tensor_tensor(ptm[:, j % 2], pt[:, j % 2],
                                                           masks[:, MASK_IDX[(qt, kt)], :],
                                                           ALU.min), "at_mdone", 1)
                                    nm += 1
                            ve.wait_ge(p.sem("at_sev"), B["at_sev"] + hq + 1)
                            p.chain(ve, ve.reciprocal(nrm[0:1, 1, :], nrm[0:1, 0, :]), "at_mch")
                            p.chain(ve, ve.tensor_copy(nrmb1[0:1, hq % 2, :], nrm[0:1, 1, :]),
                                    "at_mch")
                            p.chain(ve, ve.tensor_copy(nrm[0:1, 2, :], nrmb1[0:1, hq % 2, :]),
                                    "at_mch")
                            p.chain(ve, ve.tensor_sub(nrm[0:1, 3, :], nrm[0:1, 1, :],
                                                      nrm[0:1, 2, :]), "at_mch")
                            p.inc(ve.tensor_copy(nrmb2[0:1, hq % 2, :], nrm[0:1, 3, :]),
                                  "at_hilo", 1)
                            if hq >= 1:
                                q2_ = hq - 1
                                ve.wait_ge(p.sem("at_avev"), B["at_avev"] + q2_ + 1)
                                ve.wait_ge(p.sem("at_bcmm"), B["at_bcmm"] + q2_ + 1)
                                if q2_ >= 2:
                                    ve.wait_ge(p.sem(f"at_ast{q2_ % 2}"), B[f"at_ast{q2_ % 2}"] + 16 * (q2_ // 2))
                                p.inc(ve.tensor_tensor(ao[:, q2_ % 2], avs[:, q2_ % 2], ps[6 + q2_ % 2][:],
                                                       ALU.mult), "at_avdone", 1)
                    q2_ = 2 * NH - 1
                    ve.wait_ge(p.sem("at_avev"), B["at_avev"] + q2_ + 1)
                    ve.wait_ge(p.sem("at_bcmm"), B["at_bcmm"] + q2_ + 1)
                    ve.wait_ge(p.sem(f"at_ast{q2_ % 2}"), B[f"at_ast{q2_ % 2}"] + 16 * (q2_ // 2))
                    p.inc(ve.tensor_tensor(ao[:, q2_ % 2], avs[:, q2_ % 2], ps[6 + q2_ % 2][:],
                                           ALU.mult), "at_avdone", 1)

        with (nc.sbuf_tensor("ats", [128, CD, T], BF16) as ats,
              nc.sbuf_tensor("wps", [128, 2, CD, 128], BF16) as wps,
              nc.sbuf_tensor("xt", [128, 2, 512], F32) as xt,
              nc.sbuf_tensor("prt", [128, 2, 512], F32) as prt,
              nc.sbuf_tensor("obp", [128, 2, 512], F32) as obp):
            with nc.Block() as block:
                @block.sync
                def _(sync):
                    sync.wait_ge(p.sem("at_ast0"), B["at_ast0"] + 16 * 16)
                    sync.wait_ge(p.sem("at_ast1"), B["at_ast1"] + 16 * 16)
                    p.inc(sync.dma_start(out=ats[:], in_=adv[:]), "at_lat", 16)
                    for i in range(32):
                        dout, qt = i // 2, i % 2
                        if i >= 2:
                            sync.wait_ge(p.sem("at_prdone"), B["at_prdone"] + i - 1)
                        p.inc(sync.dma_start(out=xt[:, i % 2],
                                             in_=xv[:, dout, qt * 512:(qt + 1) * 512]),
                              f"at_lxt{i % 2}", 16)

                @block.gpsimd
                def _(gp):
                    gp.wait_ge(p.sem("at_ast0"), B["at_ast0"] + 16 * 16)
                    gp.wait_ge(p.sem("at_ast1"), B["at_ast1"] + 16 * 16)
                    for dout in range(CD):
                        if dout >= 2:
                            gp.wait_ge(p.sem("at_pjmm"), B["at_pjmm"] + 2 * (dout - 1))
                        p.inc(gp.dma_start(out=wps[:, dout % 2],
                                           in_=wpv[:, :, dout * 128:(dout + 1) * 128]),
                              f"at_lwp{dout % 2}", 16)

                @block.tensor
                def _(te):
                    te.wait_ge(p.sem("at_lat"), B["at_lat"] + 16)
                    for i in range(32):
                        dout, qt = i // 2, i % 2
                        te.wait_ge(p.sem(f"at_lwp{dout % 2}"), B[f"at_lwp{dout % 2}"] + 16 * (dout // 2 + 1))
                        if i >= 2:
                            te.wait_ge(p.sem("at_prv"), B["at_prv"] + i - 1)
                        for c in range(CD):
                            ins = te.matmul(ps[i % 2][:], wps[:, dout % 2, c, :],
                                            ats[:, c, qt * 512:(qt + 1) * 512],
                                            start=(c == 0), stop=(c == CD - 1))
                        p.inc(ins, "at_pjmm", 1)

                @block.scalar
                def _(sc):
                    for i in range(32):
                        dout = i // 2
                        sc.wait_ge(p.sem("at_pjmm"), B["at_pjmm"] + i + 1)
                        if i >= 2:
                            sc.wait_ge(p.sem("at_prdone"), B["at_prdone"] + i - 1)
                        p.inc(sc.activation(prt[:, i % 2], ps[i % 2][:], AF.Identity,
                                            bias=bp[:, dout:dout + 1]), "at_prv", 1)
                        if i >= 1:
                            sc.wait_ge(p.sem("at_prdone"), B["at_prdone"] + i)
                            pd, pq = (i - 1) // 2, (i - 1) % 2
                            p.inc(sc.dma_start(out=xov[:, pd, pq * 512:(pq + 1) * 512],
                                               in_=obp[:, (i - 1) % 2]),
                                  f"at_obst{(i - 1) % 2}", 16)
                    sc.wait_ge(p.sem("at_prdone"), B["at_prdone"] + 32)
                    p.inc(sc.dma_start(out=xov[:, 15, 512:1024], in_=obp[:, 1]), "at_obst1", 16)
                    sc.wait_ge(p.sem("at_obst0"), B["at_obst0"] + 16 * 16)
                    sc.wait_ge(p.sem("at_obst1"), B["at_obst1"] + 16 * 16)

                @block.vector
                def _(ve):
                    for i in range(32):
                        ve.wait_ge(p.sem("at_prv"), B["at_prv"] + i + 1)
                        ve.wait_ge(p.sem(f"at_lxt{i % 2}"), B[f"at_lxt{i % 2}"] + 16 * (i // 2 + 1))
                        if i >= 2:
                            ve.wait_ge(p.sem(f"at_obst{i % 2}"), B[f"at_obst{i % 2}"] + 16 * (i // 2))
                        p.inc(ve.tensor_add(obp[:, i % 2], prt[:, i % 2], xt[:, i % 2]),
                              "at_prdone", 1)



def build_l0attn():
    p = P()
    nc = p.nc
    x = nc.dram_tensor("x", [D, T], F32, kind="ExternalInput").ap()
    xfh = nc.dram_tensor("xfh", [D, T], F32, kind="ExternalInput").ap()
    wqkv = nc.dram_tensor("wqkv", [D, 3 * D], F32, kind="ExternalInput").ap()
    bqkv_cm = nc.dram_tensor("bqkv_cm", [128, 48], F32, kind="ExternalInput").ap()
    bqkv_row = nc.dram_tensor("bqkv_row", [1, 3 * D], F32, kind="ExternalInput").ap()
    wproj = nc.dram_tensor("wproj", [D, D], F32, kind="ExternalInput").ap()
    bproj_cm = nc.dram_tensor("bproj_cm", [128, CD], F32, kind="ExternalInput").ap()
    kbias_in = nc.dram_tensor("kbias", [128, NH * 16], F32, kind="ExternalInput").ap()
    qbias_in = nc.dram_tensor("qbias", [NH, 2, 2, 512], BF16, kind="ExternalInput").ap()
    masks_in = nc.dram_tensor("masks", [16, 128, 512], BF16, kind="ExternalInput").ap()
    ones2_in = nc.dram_tensor("ones2", [2, 128], BF16, kind="ExternalInput").ap()
    xo = nc.dram_tensor("xo", [D, T], F32, kind="ExternalOutput").ap()
    vdram = nc.dram_tensor("vdram", [2048, D], BF16).ap()
    adram = nc.dram_tensor("adram", [D, T], BF16).ap()
    with p.stack:
        ones_bf, cst, b1, b2 = _mk_consts(p)
        ps = [p.psum(f"ps{i}", [128, 512]) for i in range(8)]
        emit_l0attn(p, x, xfh, wqkv, bqkv_cm, bqkv_row, wproj, bproj_cm, kbias_in,
                    qbias_in, masks_in, ones2_in, xo, ones_bf, cst, b1, b2, ps,
                    vdram, adram)
    return p.nc


def _attn_inputs(inp, x_full, kb_pc, qb_pc, masks_pc):
    wqkv = np.ascontiguousarray(np.asarray(inp["L0_attn_w_qkv"], np.float32))
    bq = np.asarray(inp["L0_attn_b_qkv"], np.float32)
    wproj = np.ascontiguousarray(np.asarray(inp["L0_attn_w_proj"], np.float32))
    bp = np.asarray(inp["L0_attn_b_proj"], np.float32)
    ones_bf = np.ones((128, 1), BF)
    ones2 = np.ones((2, 128), BF)
    cstf = np.zeros((1, 8), np.float32)
    cstf[0, 0] = LN_EPS
    cstf[0, 1] = LN_EPS_IN
    maps = []
    for c in range(NC):
        even = c - (c % 2)
        maps.append({
            "x": _fm_shard(x_full, c), "xfh": _fm_shard(x_full, even),
            "wqkv": wqkv, "bqkv_cm": _cm(bq), "bqkv_row": bq[None, :].astype(np.float32),
            "wproj": wproj, "bproj_cm": _cm(bp),
            "kbias": kb_pc[c], "qbias": qb_pc[c], "masks": masks_pc[c],
            "ones_bf": ones_bf, "ones2": ones2, "cstf": cstf,
        })
    return maps



FE = 4096
FECH = FE // 128


def emit_l1ffn(p, x, wg, wa, wb, ba_cm, bb_cm, xo, ones_bf, cst, b1, b2, ps,
               mb1, mb2, w_gate=None):
    nc = p.nc
    p.pool_reset()
    B = p.snap()

    xv = chunked(x)
    wgv = chunked(wg)
    wav = [chunked(w) for w in wa]
    wbv = [chunked(w) for w in wb]
    xov = chunked(xo)
    ACTS = [AF.Gelu, AF.Relu]

    with contextlib.ExitStack() as _mos:
        def _sb(n, s, d):
            return _mos.enter_context(nc.sbuf_tensor(n, s, d))
        wgs = _sb("mo_wgs", [128, CD, 2], BF16)
        bas = [_sb(f"mo_bas{e}", [128, FECH], F32) for e in range(2)]
        bbs = [_sb(f"mo_bbs{e}", [128, CD], F32) for e in range(2)]
        xs = _sb("mo_xs", [128, 2, CD, 512], F32)
        yb = _sb("mo_yb", [128, CD, 512], BF16)
        tmp = _sb("mo_tmp", [128, 2, 512], F32)
        xbf = _sb("mo_xbf", [128, 2, 512], BF16)
        xsq = _sb("mo_xsq", [128, 2, 512], BF16)
        mb = _sb("mo_mbx", [128, 512], F32)
        rb = _sb("mo_rbx", [128, 512], F32)
        rows = _sb("mo_rows", [1, 6, 512], F32)
        grow = _sb("mo_grow", [1, 5, 512], F32)
        m0b = _sb("mo_m0b", [128, 512], F32)
        m1b = _sb("mo_m1b", [128, 512], F32)
        was = _sb("mo_was", [128, 2, CD, 512], BF16)
        wbs = _sb("mo_wbs", [128, 2, 1024], BF16)
        hb = _sb("mo_hb", [128, FECH, 512], BF16)
        prt = _sb("mo_prt", [128, 2, 512], F32)
        act = _sb("mo_act", [128, 2, 512], F32)
        ob = _sb("mo_ob", [128, 2, 512], F32)
        srow, sqrow, r1, r2, sd, rstd = (rows[:, i] for i in range(6))

        with nc.Block() as block:
            @block.sync
            def _(sync):
                for dst, srcv in ((bas[0][:], ba_cm[0][:]), (bas[1][:], ba_cm[1][:]),
                                  (bbs[0][:], bb_cm[0][:]), (bbs[1][:], bb_cm[1][:])):
                    p.inc(sync.dma_start(out=dst, in_=srcv), "mo_lcst", 16)

            @block.gpsimd
            def _(gp):
                p.inc(gp.dma_start(out=wgs[:], in_=wgv[:]), "mo_lwg", 16)

            ycnt = []
            for tt in range(2):
                tsl = slice(tt * 512, (tt + 1) * 512)

                @block.sync
                def _(sync, tt=tt):
                    if tt > 0:
                        sync.wait_ge(p.sem("mo_obst0"), B["mo_obst0"] + 16 * 8 * tt)
                        sync.wait_ge(p.sem("mo_obst1"), B["mo_obst1"] + 16 * 8 * tt)
                    p.inc(sync.dma_start(out=xs[:, 0], in_=xv[:, :, tsl]), "mo_lx", 16)
                ycnt.append(emit_ln(p, block, xs[:, 0], yb, 512, ones_bf, b1, b2,
                        mb, rb, tmp, None, xbf, xsq, ps[4][0:1, :], ps[5][0:1, :],
                        cst[0:1, 0:1], srow, sqrow, r1, r2, sd, rstd,
                        gp_wait=[("mo_lx", B["mo_lx"] + 16 * (tt + 1))],
                        sc_wait=[("mo_lx", B["mo_lx"] + 16 * (tt + 1)),
                                 ("mo_lcst", B["mo_lcst"] + 16 * 4)],
                        te_wait=[("mo_lcst", B["mo_lcst"] + 16 * 4),
                                 ("mo_lwg", B["mo_lwg"] + 16),
                                 ("mo_prv", B["mo_prv"] + 32 * tt)],
                        prev=True))

                @block.tensor
                def _(te, tt=tt, yc=ycnt[tt]):
                    te.wait_ge(p.sem("ln_y"), yc)
                    for e in range(2):
                        for c in range(CD):
                            ins = te.matmul(ps[6 + e][0:1, :], wgs[:, c, e:e + 1],
                                            yb[:, c], start=(c == 0), stop=(c == CD - 1))
                        p.inc(ins, "mo_gmm", 1)

                @block.scalar
                def _(sc, tt=tt):
                    sc.wait_ge(p.sem("mo_gmm"), B["mo_gmm"] + 2 * (tt + 1))
                    if tt > 0:
                        sc.wait_ge(p.sem("mo_mbld"), B["mo_mbld"] + 32 * tt)
                    p.inc(sc.activation(grow[0:1, 0, :], ps[6][0:1, :], AF.Identity),
                          "mo_gev", 1)
                    p.inc(sc.activation(grow[0:1, 1, :], ps[7][0:1, :], AF.Identity),
                          "mo_gev", 1)
                    sc.wait_ge(p.sem("mo_gmask"), B["mo_gmask"] + tt + 1)
                    p.inc(sc.dma_start(out=mb1[0:1, :], in_=grow[0:1, 3, :]), "mo_mbst", 16)
                    p.inc(sc.dma_start(out=mb2[0:1, :], in_=grow[0:1, 4, :]), "mo_mbst", 16)
                    sc.wait_ge(p.sem("mo_mbst"), B["mo_mbst"] + 32 * (tt + 1))
                    p.inc(sc.dma_start(out=m0b[:], in_=mb1.broadcast_to((128, 512))), "mo_mbld", 16)
                    p.inc(sc.dma_start(out=m1b[:], in_=mb2.broadcast_to((128, 512))), "mo_mbld", 16)

                @block.vector
                def _(ve, tt=tt):
                    ve.wait_ge(p.sem("mo_gev"), B["mo_gev"] + 2 * (tt + 1))
                    p.chain(ve, ve.tensor_tensor(grow[0:1, 2, :], grow[0:1, 0, :],
                                                 grow[0:1, 1, :], ALU.is_ge), "vch")
                    p.chain(ve, ve.tensor_scalar_mul(grow[0:1, 3, :], grow[0:1, 2, :], 0.5),
                            "vch")
                    p.inc(ve.tensor_scalar(grow[0:1, 4, :], grow[0:1, 2, :], -0.5, 0.5,
                                           ALU.mult, ALU.add), "mo_gmask", 1)

                for e in range(2):
                    @block.sync
                    def _(sync, tt=tt, e=e):
                        for g in range(8):
                            G = (tt * 2 + e) * 8 + g
                            if G == 0 and w_gate is not None:
                                sync.wait_ge(p.sem(w_gate[0]), w_gate[1])
                            if G >= 2:
                                sync.wait_ge(p.sem("mo_hmm"), B["mo_hmm"] + 4 * (G - 1))
                            msl = slice(g * 512, (g + 1) * 512)
                            p.inc(sync.dma_start(out=was[:, G % 2], in_=wav[e][:, :, msl]),
                                  f"mo_lwa{G % 2}", 16)
                        for dg in range(2):
                            for f in range(FECH):
                                FD = ((tt * 2 + e) * 2 + dg) * FECH + f
                                if FD >= 2:
                                    sync.wait_ge(p.sem("mo_bmm"), B["mo_bmm"] + FD - 1)
                                dsl = slice(dg * 1024, (dg + 1) * 1024)
                                p.inc(sync.dma_start(out=wbs[:, FD % 2], in_=wbv[e][:, f, dsl]),
                                      f"mo_lwb{FD % 2}", 16)

                    @block.tensor
                    def _(te, tt=tt, e=e):
                        for g in range(8):
                            G = (tt * 2 + e) * 8 + g
                            te.wait_ge(p.sem(f"mo_lwa{G % 2}"), B[f"mo_lwa{G % 2}"] + 16 * (G // 2 + 1))
                            if g == 0:
                                te.wait_ge(p.sem("ln_y"), ycnt[tt])
                                te.wait_ge(p.sem("mo_prv"), B["mo_prv"] + 16 * (tt * 2 + e))
                            for m in range(4):
                                MH = G * 4 + m
                                if MH >= 4:
                                    te.wait_ge(p.sem("mo_hev"), B["mo_hev"] + MH - 3)
                                pb = ps[MH % 4]
                                msl = slice(m * 128, (m + 1) * 128)
                                for c in range(CD):
                                    ins = te.matmul(pb[:], was[:, G % 2, c, msl], yb[:, c],
                                                    start=(c == 0), stop=(c == CD - 1))
                                p.inc(ins, "mo_hmm", 1)
                        for dg in range(2):
                            DG = (tt * 2 + e) * 2 + dg
                            te.wait_ge(p.sem("mo_hev"), B["mo_hev"] + 32 * (tt * 2 + e + 1))
                            te.wait_ge(p.sem("mo_prv"), B["mo_prv"] + 8 * DG)
                            for f in range(FECH):
                                FD = DG * FECH + f
                                te.wait_ge(p.sem(f"mo_lwb{FD % 2}"), B[f"mo_lwb{FD % 2}"] + 16 * (FD // 2 + 1))
                                for dout in range(8):
                                    ins = te.matmul(ps[dout][:],
                                                    wbs[:, FD % 2, dout * 128:(dout + 1) * 128],
                                                    hb[:, f], start=(f == 0),
                                                    stop=(f == FECH - 1))
                                p.inc(ins, "mo_bmm", 1)

                    @block.scalar
                    def _(sc, tt=tt, e=e):
                        for g in range(8):
                            for m in range(4):
                                MH = ((tt * 2 + e) * 8 + g) * 4 + m
                                sc.wait_ge(p.sem("mo_hmm"), B["mo_hmm"] + MH + 1)
                                col = g * 4 + m
                                p.inc(sc.activation(hb[:, col], ps[MH % 4][:], ACTS[e],
                                                    bias=bas[e][:, col:col + 1]), "mo_hev", 1)
                        for dg in range(2):
                            DG = (tt * 2 + e) * 2 + dg
                            for dout in range(8):
                                il = dg * 8 + dout
                                i = DG * 8 + dout
                                gi = 16 * (tt * 2 + e) + il
                                sc.wait_ge(p.sem("mo_bmm"), B["mo_bmm"] + (DG + 1) * FECH)
                                if gi >= 2:
                                    sc.wait_ge(p.sem("mo_mdn"), B["mo_mdn"] + gi - 1)
                                ch = dg * 8 + dout
                                p.inc(sc.activation(prt[:, i % 2], ps[dout][:], AF.Identity,
                                                    bias=bbs[e][:, ch:ch + 1]), "mo_prv", 1)
                                if e == 1 and il >= 1:
                                    sc.wait_ge(p.sem("mo_acc"), B["mo_acc"] + 32 * tt + 16 + il)
                                    pch = il - 1
                                    p.inc(sc.dma_start(out=xov[:, pch, tsl],
                                                       in_=ob[:, (il - 1) % 2]),
                                          f"mo_obst{(il - 1) % 2}", 16)
                        if e == 1:
                            sc.wait_ge(p.sem("mo_acc"), B["mo_acc"] + 32 * tt + 32)
                            p.inc(sc.dma_start(out=xov[:, 15, tsl], in_=ob[:, 1]),
                                  "mo_obst1", 16)

                    @block.vector
                    def _(ve, tt=tt, e=e):
                        if e == 1:
                            ve.wait_ge(p.sem("mo_acc"), B["mo_acc"] + 32 * tt + 16)
                        for dg in range(2):
                            for dout in range(8):
                                il = dg * 8 + dout
                                i = ((tt * 2 + e) * 2 + dg) * 8 + dout
                                gi = 16 * (tt * 2 + e) + il
                                ch = dg * 8 + dout
                                ve.wait_ge(p.sem("mo_prv"), B["mo_prv"] + 16 * (tt * 2 + e) + il + 1)
                                if e == 0 and il == 0:
                                    ve.wait_ge(p.sem("mo_mbld"), B["mo_mbld"] + 32 * (tt + 1))
                                mk = m0b if e == 0 else m1b
                                p.chain(ve, ve.tensor_mul(act[:, il % 2], prt[:, il % 2],
                                                          mk[:]), "mo_mdn")
                                if e == 0:
                                    ins = ve.tensor_add(xs[:, 1, ch, :], xs[:, 0, ch, :],
                                                        act[:, il % 2])
                                    p.inc(ins, "mo_acc", 1)
                                else:
                                    if 8 * tt + il // 2 >= 1:
                                        ve.wait_ge(p.sem(f"mo_obst{il % 2}"),
                                                   B[f"mo_obst{il % 2}"] + 16 * (8 * tt + il // 2))
                                    ins = ve.tensor_add(ob[:, il % 2], xs[:, 1, ch, :],
                                                        act[:, il % 2])
                                    p.inc(ins, "mo_acc", 1)



            @block.scalar
            def _(sc):
                sc.wait_ge(p.sem("mo_obst0"), B["mo_obst0"] + 16 * 16)
                sc.wait_ge(p.sem("mo_obst1"), B["mo_obst1"] + 16 * 16)


def build_l1ffn():
    p = P()
    nc = p.nc
    x = nc.dram_tensor("x", [D, T], F32, kind="ExternalInput").ap()
    wg = nc.dram_tensor("wgT", [D, 2], F32, kind="ExternalInput").ap()
    wa = [nc.dram_tensor(f"w{e}aT", [D, FE], BF16, kind="ExternalInput").ap() for e in range(2)]
    wb = [nc.dram_tensor(f"w{e}bT", [FE, D], BF16, kind="ExternalInput").ap() for e in range(2)]
    ba_cm = [nc.dram_tensor(f"b{e}a_cm", [128, FECH], F32, kind="ExternalInput").ap() for e in range(2)]
    bb_cm = [nc.dram_tensor(f"b{e}b_cm", [128, CD], F32, kind="ExternalInput").ap() for e in range(2)]
    xo = nc.dram_tensor("xo", [D, T], F32, kind="ExternalOutput").ap()
    mb1 = nc.dram_tensor("mb1", [1, 512], F32).ap()
    mb2 = nc.dram_tensor("mb2", [1, 512], F32).ap()
    with p.stack:
        ones_bf, cst, b1, b2 = _mk_consts(p)
        ps = [p.psum(f"ps{i}", [128, 512]) for i in range(8)]
        emit_l1ffn(p, x, wg, wa, wb, ba_cm, bb_cm, xo, ones_bf, cst, b1, b2, ps,
                   mb1, mb2)
    return p.nc



def emit_l1at1(p, x, wv_in, g1, bt1, gt, btt, td, rev, yo, xlno, vmpo,
               ones_bf, cst, b1, b2, ps, rvb):
    nc = p.nc
    p.pool_reset()
    B = p.snap()

    xv, yv, xlv = chunked(x), chunked(yo), chunked(xlno)
    wvv = chunked(wv_in)

    with contextlib.ExitStack() as _f1s:
        def _sb(n, s, d):
            return _f1s.enter_context(nc.sbuf_tensor(n, s, d))
        g1s = _sb("f1_g1s", [128, CD], F32)
        b1s = _sb("f1_b1s", [128, CD], F32)
        gts = _sb("f1_gts", [128, CD], F32)
        bts = _sb("f1_bts", [128, CD], F32)
        tds = _sb("f1_tds", [128, CD], F32)
        revb = _sb("f1_revb", [128, T], F32)
        xs = _sb("f1_xs", [128, CD, 512], F32)
        yb = _sb("f1_ybf", [128, CD, 512], F32)
        x1b = _sb("f1_x1b", [128, CD, 512], F32)
        xlnb = _sb("f1_xlnb", [128, CD, T], BF16)
        tmp = _sb("f1_tmp", [128, 2, 512], F32)
        tmp2 = _sb("f1_tmp2", [128, 2, 512], F32)
        xbf = _sb("f1_xbf", [128, 2, 512], BF16)
        xsq = _sb("f1_xsq", [128, 2, 512], BF16)
        mb = _sb("f1_mbx", [128, 512], F32)
        rb = _sb("f1_rbx", [128, 512], F32)
        rows = _sb("f1_rows", [1, 6, 512], F32)
        wvs = _sb("f1_wvs", [128, 2, CD, 128], BF16)
        vt = _sb("f1_vt", [128, 2, 512], F32)
        dct = _sb("f1_dct", [128, 2, 512], F32)
        vdt = _sb("f1_vdt", [128, 2, 512], F32)
        vmpt = _sb("f1_vmpt", [128, 2, CD], F32)
        vmps = _sb("f1_vmps", [128, CD], F32)
        srow, sqrow, r1, r2, sd, rstd = (rows[:, i] for i in range(6))

        NCON = 7
        with nc.Block() as block:
            @block.sync
            def _(sync):
                for dst, srcv in ((g1s[:], g1[:]), (b1s[:], bt1[:]), (gts[:], gt[:]),
                                  (bts[:], btt[:]), (tds[:], td[:])):
                    p.inc(sync.dma_start(out=dst, in_=srcv), "f1_lcst", 16)
                p.inc(sync.dma_start(out=rvb[0:1, :], in_=rev[0:1, :]), "f1_lcst", 16)
                sync.wait_ge(p.sem("f1_lcst"), B["f1_lcst"] + 16 * 6)
                p.inc(sync.dma_start(out=revb[:], in_=rvb.broadcast_to((128, T))), "f1_lcst", 16)

            ylast = [0]
            ycl = {}
            for tt in range(2):
                tsl = slice(tt * 512, (tt + 1) * 512)

                @block.sync
                def _(sync, tt=tt, tsl=tsl):
                    if tt > 0:
                        sync.wait_ge(p.sem("ln_y"), ycl[("c", tt - 1)])
                    p.inc(sync.dma_start(out=xs[:], in_=xv[:, :, tsl]), "f1_lx", 16)
                for nm, src_, dst_, eps_i, gains in (
                        ("a", xs, yb, 0, None), ("b", yb, x1b, 1, (g1s, b1s)),
                        ("c", x1b, xlnb[:, :, tsl], 1, (gts, bts))):
                    gp_w = ([("f1_lx", B["f1_lx"] + 16 * (tt + 1))] if nm == "a"
                            else [("ln_y", ylast[0])])
                    ve_w = []
                    if tt == 1 and nm == "a":
                        ve_w = [("f1_yst0", B["f1_yst0"] + 16)]
                    if tt == 1 and nm == "c":
                        ve_w = [("f1_xlst0", B["f1_xlst0"] + 16)]
                    ylast[0] = emit_ln(p, block, src_, dst_, 512, ones_bf, b1, b2,
                            mb, rb, tmp, tmp2, xbf, xsq, ps[4][0:1, :], ps[5][0:1, :],
                            cst[0:1, eps_i:eps_i + 1], srow, sqrow, r1, r2, sd, rstd,
                            gp_wait=gp_w, sc_wait=gp_w + [("f1_lcst", B["f1_lcst"] + 16 * NCON)],
                            te_wait=[("f1_lcst", B["f1_lcst"] + 16 * NCON)], ve_wait=ve_w,
                            gains=gains, prev=True)
                    ycl[(nm, tt)] = ylast[0]

                @block.scalar
                def _(sc, tt=tt, tsl=tsl):
                    sc.wait_ge(p.sem("ln_y"), ycl[("a", tt)])
                    p.inc(sc.dma_start(out=yv[:, :, tsl], in_=yb[:]), f"f1_yst{tt}", 16)
                    sc.wait_ge(p.sem("ln_y"), ycl[("c", tt)])
                    p.inc(sc.dma_start(out=xlv[:, :, tsl], in_=xlnb[:, :, tsl]),
                          f"f1_xlst{tt}", 16)

            @block.gpsimd
            def _(gp):
                for co in range(CD):
                    if co >= 2:
                        gp.wait_ge(p.sem("f1_vmm"), B["f1_vmm"] + 2 * (co - 1))
                    p.inc(gp.dma_start(out=wvs[:, co % 2],
                                       in_=wvv[:, :, co * 128:(co + 1) * 128]),
                          f"f1_lwv{co % 2}", 16)

            @block.tensor
            def _(te):
                te.wait_ge(p.sem("ln_y"), ycl[("c", 1)])
                for co in range(CD):
                    te.wait_ge(p.sem(f"f1_lwv{co % 2}"), B[f"f1_lwv{co % 2}"] + 16 * (co // 2 + 1))
                    for tt in range(2):
                        i = co * 2 + tt
                        if i >= 2:
                            te.wait_ge(p.sem("f1_vev"), B["f1_vev"] + i - 1)
                        for c in range(CD):
                            ins = te.matmul(ps[i % 2][:], wvs[:, co % 2, c, :],
                                            xlnb[:, c, tt * 512:(tt + 1) * 512],
                                            start=(c == 0), stop=(c == CD - 1))
                        p.inc(ins, "f1_vmm", 1)

            @block.scalar
            def _(sc):
                for co in range(CD):
                    for tt in range(2):
                        i = co * 2 + tt
                        sc.wait_ge(p.sem("f1_vmm"), B["f1_vmm"] + i + 1)
                        if i >= 2:
                            sc.wait_ge(p.sem("f1_vred"), B["f1_vred"] + i - 1)
                        p.inc(sc.activation(vt[:, i % 2], ps[i % 2][:], AF.Identity),
                              "f1_vev", 1)
                        sc.wait_ge(p.sem("f1_varg"), B["f1_varg"] + i + 1)
                        p.inc(sc.activation(dct[:, i % 2], vdt[:, i % 2], AF.Exp),
                              "f1_vdec", 1)

            @block.vector
            def _(ve):
                ve.wait_ge(p.sem("f1_lcst"), B["f1_lcst"] + 16 * NCON)
                for co in range(CD):
                    for tt in range(2):
                        i = co * 2 + tt
                        if i >= 2:
                            ve.wait_ge(p.sem("f1_vdec"), B["f1_vdec"] + i - 1)
                        p.inc(ve.tensor_scalar(vdt[:, i % 2],
                                               revb[:, tt * 512:(tt + 1) * 512],
                                               tds[:, co:co + 1], None, ALU.mult),
                              "f1_varg", 1)
                        ve.wait_ge(p.sem("f1_vev"), B["f1_vev"] + i + 1)
                        ve.wait_ge(p.sem("f1_vdec"), B["f1_vdec"] + i + 1)
                        p.chain(ve, ve.tensor_mul(vdt[:, i % 2], dct[:, i % 2], vt[:, i % 2])
                                if False else
                                ve.tensor_tensor(tmp[:, i % 2], dct[:, i % 2], vt[:, i % 2],
                                                 ALU.mult), "vch")
                        p.inc(ve.tensor_reduce(vmpt[:, tt, co:co + 1], tmp[:, i % 2],
                                               AX.X, ALU.add), "f1_vred", 1)
                ve.wait_ge(p.sem("f1_vred"), B["f1_vred"] + 32)
                p.inc(ve.tensor_add(vmps[:], vmpt[:, 0], vmpt[:, 1]), "f1_vfin", 1)

            @block.scalar
            def _(sc):
                sc.wait_ge(p.sem("f1_vfin"), B["f1_vfin"] + 1)
                p.inc(sc.dma_start(out=vmpo[:], in_=vmps[:]), "f1_vst", 16)
                sc.wait_ge(p.sem("f1_vst"), B["f1_vst"] + 16)
                sc.wait_ge(p.sem("f1_yst0"), B["f1_yst0"] + 16)
                sc.wait_ge(p.sem("f1_yst1"), B["f1_yst1"] + 16)
                sc.wait_ge(p.sem("f1_xlst0"), B["f1_xlst0"] + 16)
                sc.wait_ge(p.sem("f1_xlst1"), B["f1_xlst1"] + 16)


def build_l1at1():
    p = P()
    nc = p.nc
    x = nc.dram_tensor("x", [D, T], F32, kind="ExternalInput").ap()
    wv_in = nc.dram_tensor("Wv", [D, D], F32, kind="ExternalInput").ap()
    g1 = nc.dram_tensor("ln1g_cm", [128, CD], F32, kind="ExternalInput").ap()
    bt1 = nc.dram_tensor("ln1b_cm", [128, CD], F32, kind="ExternalInput").ap()
    gt = nc.dram_tensor("tmg_cm", [128, CD], F32, kind="ExternalInput").ap()
    btt = nc.dram_tensor("tmb_cm", [128, CD], F32, kind="ExternalInput").ap()
    td = nc.dram_tensor("td_cm", [128, CD], F32, kind="ExternalInput").ap()
    rev = nc.dram_tensor("rev_t", [1, T], F32, kind="ExternalInput").ap()
    yo = nc.dram_tensor("yo", [D, T], F32, kind="ExternalOutput").ap()
    xlno = nc.dram_tensor("xlno", [D, T], BF16, kind="ExternalOutput").ap()
    vmpo = nc.dram_tensor("vmpo", [128, CD], F32, kind="ExternalOutput").ap()
    rvb = nc.dram_tensor("rvb", [1, T], F32).ap()
    with p.stack:
        ones_bf, cst, b1, b2 = _mk_consts(p)
        ps = [p.psum(f"ps{i}", [128, 512]) for i in range(8)]
        emit_l1at1(p, x, wv_in, g1, bt1, gt, btt, td, rev, yo, xlno, vmpo,
                   ones_bf, cst, b1, b2, ps, rvb)
    return p.nc



def emit_l1at2(p, y_in, xln_in, xlb, vmp_o, vmp_p, tf, wr_in, wo_in, bo, g2,
               b2g, gc, bc, x2o, xco, ones_bf, cst, b1, b2, ps):
    nc = p.nc
    p.pool_reset()
    B = p.snap()

    yv, xlnv = chunked(y_in), chunked(xln_in)
    wrv, wov = chunked(wr_in), chunked(wo_in)
    x2v, xcv = chunked(x2o), chunked(xco)

    with contextlib.ExitStack() as _f2s:
        def _sb(n, s, d):
            return _f2s.enter_context(nc.sbuf_tensor(n, s, d))
        tfs = _sb("f2_tfs", [128, CD], F32)
        tfm1 = _sb("f2_f2_tfm1", [128, CD], F32)
        xlbs = _sb("f2_xlbs", [128, CD], BF16)
        bos = _sb("f2_bos", [128, CD], F32)
        g2s = _sb("f2_g2s", [128, CD], F32)
        b2s = _sb("f2_b2s", [128, CD], F32)
        gcs = _sb("f2_gcs", [128, CD], F32)
        bcs = _sb("f2_bcs", [128, CD], F32)
        vo = _sb("f2_vo", [128, CD], F32)
        vp = _sb("f2_vp", [128, CD], F32)
        vmx = _sb("f2_vmx", [128, CD], F32)
        vhl = _sb("f2_vhl", [128, CD, 2], BF16)
        vh32 = _sb("f2_vh32", [128, CD], F32)
        vlo = _sb("f2_vlo", [128, CD], F32)
        rbuf = _sb("f2_rbuf", [128, CD, T], BF16)
        zb = _sb("f2_zb", [128, CD], F32)
        zbb = _sb("f2_zbb", [128, CD], F32)
        zt = _sb("f2_zt", [128, 2, 2], F32)
        ws = _sb("f2_ws", [128, 2, CD, 128], BF16)
        tmp = _sb("f2_tmp", [128, 2, 512], F32)
        tmp2 = _sb("f2_tmp2", [128, 2, 512], F32)
        xbf = _sb("f2_xbf", [128, 2, 512], BF16)
        xsq = _sb("f2_xsq", [128, 2, 512], BF16)
        mb = _sb("f2_mbx", [128, 512], F32)
        rb = _sb("f2_rbx", [128, 512], F32)
        rows = _sb("f2_rows", [1, 6, 512], F32)
        srow, sqrow, r1, r2, sd, rstd = (rows[:, i] for i in range(6))
        consts = [(tfs[:], tf[:]), (xlbs[:], xlb[:]), (bos[:], bo[:]),
                  (g2s[:], g2[:]), (b2s[:], b2g[:]), (gcs[:], gc[:]),
                  (bcs[:], bc[:]), (vo[:], vmp_o[:])]
        if vmp_p is not None:
            consts.append((vp[:], vmp_p[:]))
        NCON = len(consts)

        with nc.Block() as block:
            @block.sync
            def _(sync):
                for dst, srcv in consts:
                    p.inc(sync.dma_start(out=dst, in_=srcv), "f2_lcst", 16)

        with (nc.sbuf_tensor("xlnb", [128, CD, T], BF16) as xlnb,
              nc.sbuf_tensor("xsb", [128, CD, T], BF16) as xsb,
              nc.sbuf_tensor("mixb", [128, CD, T], BF16) as mixb):
            with nc.Block() as block:
                @block.sync
                def _(sync):
                    p.inc(sync.dma_start(out=xlnb[:], in_=xlnv[:]), "f2_lxln", 16)

                @block.vector
                def _(ve):
                    ve.wait_ge(p.sem("f2_lcst"), B["f2_lcst"] + 16 * NCON)
                    p.inc(ve.tensor_scalar(tfm1[:], tfs[:], -1.0, 1.0, ALU.mult, ALU.add),
                          "f2_tfm", 1)
                    ve.wait_ge(p.sem("f2_lxln"), B["f2_lxln"] + 16)
                    ve.wait_ge(p.sem("f2_tfm"), B["f2_tfm"] + 1)
                    for c in range(CD):
                        p.chain(ve, ve.tensor_copy(xsb[:, c, 1:T], xlnb[:, c, 0:T - 1]), "vch")
                        p.chain(ve, ve.tensor_copy(xsb[:, c, 0:1], xlbs[:, c:c + 1]), "vch")
                        p.chain(ve, ve.tensor_scalar(tmp[:, 0, :] if False else tmp.rearrange("p a b -> p (a b)")[:, 0:T],
                                                     xlnb[:, c], tfs[:, c:c + 1], None,
                                                     ALU.mult), "vch")
                        ins = ve.scalar_tensor_tensor(mixb[:, c], xsb[:, c], tfm1[:, c:c + 1],
                                                      tmp.rearrange("p a b -> p (a b)")[:, 0:T],
                                                      ALU.mult, ALU.add)
                        p.inc(ins, "f2_mixd", 1)

                @block.gpsimd
                def _(gp):
                    for co in range(CD):
                        if co >= 2:
                            gp.wait_ge(p.sem("f2_rmm"), B["f2_rmm"] + 2 * (co - 1))
                        p.inc(gp.dma_start(out=ws[:, co % 2],
                                           in_=wrv[:, :, co * 128:(co + 1) * 128]),
                              f"f2_lwr{co % 2}", 16)

                @block.tensor
                def _(te):
                    te.wait_ge(p.sem("f2_mixd"), B["f2_mixd"] + CD)
                    for co in range(CD):
                        te.wait_ge(p.sem(f"f2_lwr{co % 2}"), B[f"f2_lwr{co % 2}"] + 16 * (co // 2 + 1))
                        for tt in range(2):
                            i = co * 2 + tt
                            if i >= 2:
                                te.wait_ge(p.sem("f2_rev_"), B["f2_rev_"] + i - 1)
                            for c in range(CD):
                                ins = te.matmul(ps[i % 2][:], ws[:, co % 2, c, :],
                                                mixb[:, c, tt * 512:(tt + 1) * 512],
                                                start=(c == 0), stop=(c == CD - 1))
                            p.inc(ins, "f2_rmm", 1)

                @block.scalar
                def _(sc):
                    for co in range(CD):
                        for tt in range(2):
                            i = co * 2 + tt
                            sc.wait_ge(p.sem("f2_rmm"), B["f2_rmm"] + i + 1)
                            p.inc(sc.activation(rbuf[:, co, tt * 512:(tt + 1) * 512],
                                                ps[i % 2][:], AF.Sigmoid), "f2_rev_", 1)

        with nc.Block() as block:
            @block.vector
            def _(ve):
                ve.wait_ge(p.sem("f2_lcst"), B["f2_lcst"] + 16 * NCON)
                if vmp_p is None:
                    p.chain(ve, ve.tensor_copy(vmx[:], vo[:]), "vch")
                else:
                    p.chain(ve, ve.tensor_add(vmx[:], vo[:], vp[:]), "vch")
                p.chain(ve, ve.tensor_copy(vhl[:, :, 0], vmx[:]), "vch")
                p.chain(ve, ve.tensor_copy(vh32[:], vhl[:, :, 0]), "vch")
                p.chain(ve, ve.tensor_sub(vlo[:], vmx[:], vh32[:]), "vch")
                p.inc(ve.tensor_copy(vhl[:, :, 1], vlo[:]), "f2_vhld", 1)

            @block.gpsimd
            def _(gp):
                gp.wait_ge(p.sem("f2_rmm"), B["f2_rmm"] + 32)
                for co in range(CD):
                    if co >= 2:
                        gp.wait_ge(p.sem("f2_zmm"), B["f2_zmm"] + co - 1)
                    p.inc(gp.dma_start(out=ws[:, co % 2],
                                       in_=wov[:, :, co * 128:(co + 1) * 128]),
                          f"f2_lwo{co % 2}", 16)

            @block.tensor
            def _(te):
                te.wait_ge(p.sem("f2_vhld"), B["f2_vhld"] + 1)
                te.wait_ge(p.sem("f2_rmm"), B["f2_rmm"] + 32)
                te.wait_ge(p.sem("f2_rev_"), B["f2_rev_"] + 32)
                for co in range(CD):
                    te.wait_ge(p.sem(f"f2_lwo{co % 2}"), B[f"f2_lwo{co % 2}"] + 16 * (co // 2 + 1))
                    if co >= 2:
                        te.wait_ge(p.sem("f2_zev"), B["f2_zev"] + co - 1)
                    for c in range(CD):
                        ins = te.matmul(ps[co % 2][:, 0:2], ws[:, co % 2, c, :],
                                        vhl[:, c, :], start=(c == 0), stop=(c == CD - 1))
                    p.inc(ins, "f2_zmm", 1)

            @block.scalar
            def _(sc):
                for co in range(CD):
                    sc.wait_ge(p.sem("f2_zmm"), B["f2_zmm"] + co + 1)
                    if co >= 2:
                        sc.wait_ge(p.sem("f2_zbd"), B["f2_zbd"] + co - 1)
                    p.inc(sc.activation(zt[:, co % 2, :], ps[co % 2][:, 0:2], AF.Identity),
                          "f2_zev", 1)

            @block.vector
            def _(ve):
                for co in range(CD):
                    ve.wait_ge(p.sem("f2_zev"), B["f2_zev"] + co + 1)
                    p.chain(ve, ve.tensor_add(zb[:, co:co + 1], zt[:, co % 2, 0:1],
                                              zt[:, co % 2, 1:2]), "vch")
                    p.inc(ve.tensor_scalar(zbb[:, co:co + 1], zb[:, co:co + 1],
                                           bos[:, co:co + 1], None, ALU.add), "f2_zbd", 1)

        with (nc.sbuf_tensor("ybA", [128, CD, 512], F32) as ybA,
              nc.sbuf_tensor("ybB", [128, CD, 512], F32) as ybB,
              nc.sbuf_tensor("xct", [128, CD, 512], BF16) as xct,
              nc.Block() as block):
            ylast = [0]
            ycl = {}
            for tt in range(2):
                tsl = slice(tt * 512, (tt + 1) * 512)
                if True:
                    @block.sync
                    def _(sync, tt=tt, tsl=tsl):
                        sync.wait_ge(p.sem("f2_zbd"), B["f2_zbd"] + CD)
                        if tt > 0:
                            sync.wait_ge(p.sem("ln_y"), ycl[("e", tt - 1)])
                        p.inc(sync.dma_start(out=ybA[:], in_=yv[:, :, tsl]), "f2_ly", 16)

                    @block.vector
                    def _(ve, tt=tt, tsl=tsl):
                        ve.wait_ge(p.sem("f2_ly"), B["f2_ly"] + 16 * (tt + 1))
                        ve.wait_ge(p.sem("f2_zbd"), B["f2_zbd"] + CD)
                        if tt > 0:
                            ve.wait_ge(p.sem(f"f2_x2st{tt - 1}"), B[f"f2_x2st{tt - 1}"] + 16)
                        for c in range(CD):
                            p.chain(ve, ve.tensor_scalar(tmp[:, c % 2], rbuf[:, c, tsl],
                                                         zbb[:, c:c + 1], None, ALU.mult),
                                    "vch")
                            ins = ve.tensor_add(ybB[:, c], ybA[:, c], tmp[:, c % 2])
                            p.inc(ins, f"f2_x2d{tt}", 1)

                    @block.scalar
                    def _(sc, tt=tt, tsl=tsl):
                        sc.wait_ge(p.sem(f"f2_x2d{tt}"), B[f"f2_x2d{tt}"] + CD)
                        p.inc(sc.dma_start(out=x2v[:, :, tsl], in_=ybB[:]), f"f2_x2st{tt}", 16)

                for nm, src_, dst_, gains in (("d", ybB, ybA, (g2s, b2s)),
                                              ("e", ybA, xct, (gcs, bcs))):
                    gpw = ([(f"f2_x2d{tt}", B[f"f2_x2d{tt}"] + CD)] if nm == "d"
                           else [("ln_y", ylast[0])])
                    vew = ([(f"f2_x2st{tt}", B[f"f2_x2st{tt}"] + 16)] if nm == "d"
                           else ([("f2_xcst0", B["f2_xcst0"] + 16)] if tt == 1 else []))
                    ylast[0] = emit_ln(p, block, src_, dst_, 512, ones_bf, b1, b2,
                            mb, rb, tmp, tmp2, xbf, xsq, ps[4][0:1, :], ps[5][0:1, :],
                            cst[0:1, 1:2], srow, sqrow, r1, r2, sd, rstd,
                            gp_wait=gpw,
                            sc_wait=gpw + [("f2_lcst", B["f2_lcst"] + 16 * NCON)],
                            te_wait=[("f2_lcst", B["f2_lcst"] + 16 * NCON),
                                     ("f2_zmm", B["f2_zmm"] + CD)], ve_wait=vew,
                            gains=gains, prev=True)
                    ycl[(nm, tt)] = ylast[0]

                @block.scalar
                def _(sc, tt=tt, tsl=tsl):
                    sc.wait_ge(p.sem("ln_y"), ycl[("e", tt)])
                    p.inc(sc.dma_start(out=xcv[:, :, tsl], in_=xct[:]), f"f2_xcst{tt}", 16)

            @block.scalar
            def _(sc):
                for s in ("f2_x2st0", "f2_x2st1", "f2_xcst0", "f2_xcst1"):
                    sc.wait_ge(p.sem(s), B[s] + 16)


def build_l1at2():
    p = P()
    nc = p.nc
    y_in = nc.dram_tensor("y", [D, T], F32, kind="ExternalInput").ap()
    xln_in = nc.dram_tensor("xln", [D, T], BF16, kind="ExternalInput").ap()
    xlb = nc.dram_tensor("xlb_cm", [128, CD], BF16, kind="ExternalInput").ap()
    vmp_o = nc.dram_tensor("vmp_own", [128, CD], F32, kind="ExternalInput").ap()
    vmp_p = nc.dram_tensor("vmp_pair", [128, CD], F32, kind="ExternalInput").ap()
    tf = nc.dram_tensor("tf_cm", [128, CD], F32, kind="ExternalInput").ap()
    wr_in = nc.dram_tensor("Wr", [D, D], F32, kind="ExternalInput").ap()
    wo_in = nc.dram_tensor("Wo", [D, D], F32, kind="ExternalInput").ap()
    bo = nc.dram_tensor("bo_cm", [128, CD], F32, kind="ExternalInput").ap()
    g2 = nc.dram_tensor("ln2g_cm", [128, CD], F32, kind="ExternalInput").ap()
    b2g = nc.dram_tensor("ln2b_cm", [128, CD], F32, kind="ExternalInput").ap()
    gc = nc.dram_tensor("cmg_cm", [128, CD], F32, kind="ExternalInput").ap()
    bc = nc.dram_tensor("cmb_cm", [128, CD], F32, kind="ExternalInput").ap()
    x2o = nc.dram_tensor("x2o", [D, T], F32, kind="ExternalOutput").ap()
    xco = nc.dram_tensor("xco", [D, T], BF16, kind="ExternalOutput").ap()
    with p.stack:
        ones_bf, cst, b1, b2 = _mk_consts(p)
        ps = [p.psum(f"ps{i}", [128, 512]) for i in range(8)]
        emit_l1at2(p, y_in, xln_in, xlb, vmp_o, vmp_p, tf, wr_in, wo_in, bo, g2,
                   b2g, gc, bc, x2o, xco, ones_bf, cst, b1, b2, ps)
    return p.nc



def emit_l1at3(p, x, x2, xc, xcb, wck, wcv, wcr, xo, ones_bf, cst, ps,
               w_gate=None):
    nc = p.nc
    p.pool_reset()
    B = p.snap()

    xv, x2v, xcv, xov = chunked(x), chunked(x2), chunked(xc), chunked(xo)
    wckv, wcvv, wcrv = chunked(wck), chunked(wcv), chunked(wcr)
    FK = 64

    with contextlib.ExitStack() as _f3s:
        def _sb(n, s, d):
            return _f3s.enter_context(nc.sbuf_tensor(n, s, d))
        xcbs = _sb("f3_xcbs", [128, CD], BF16)
        xcnb = _sb("f3_xcnb", [128, CD, T], BF16)
        xcsb = _sb("f3_xcsb", [128, CD, T], BF16)
        rc = _sb("f3_rc", [128, CD, T], BF16)
        wks = _sb("f3_wks", [128, 2, CD, 128], BF16)
        wvs = _sb("f3_wvs", [128, 2, 1024], BF16)
        kcb = _sb("f3_kcb", [128, FK, 512], BF16)
        rtb = _sb("f3_rtb", [128, 2, 512], BF16)
        vcb = _sb("f3_vcb", [128, 2, 512], BF16)
        t1b = _sb("f3_t1b", [128, 2, 512], F32)
        o1b = _sb("f3_o1b", [128, 2, 512], F32)
        x2t = _sb("f3_x2t", [128, 2, 512], F32)
        xt = _sb("f3_xt", [128, 2, 512], F32)
        ob = _sb("f3_ob", [128, 2, 512], F32)

        with nc.Block() as block:
            @block.sync
            def _(sync):
                for dst, srcv in ((xcbs[:], xcb[:]),):
                    p.inc(sync.dma_start(out=dst, in_=srcv), "f3_lcst", 16)
                p.inc(sync.dma_start(out=xcnb[:], in_=xcv[:]), "f3_lxc", 16)

            @block.vector
            def _(ve):
                ve.wait_ge(p.sem("f3_lcst"), B["f3_lcst"] + 16)
                ve.wait_ge(p.sem("f3_lxc"), B["f3_lxc"] + 16)
                for c in range(CD):
                    p.chain(ve, ve.tensor_copy(xcsb[:, c, 1:T], xcnb[:, c, 0:T - 1]), "vch")
                    ins = ve.tensor_copy(xcsb[:, c, 0:1], xcbs[:, c:c + 1])
                    p.inc(ins, "f3_xcsd", 1)

            @block.gpsimd
            def _(gp):
                for co in range(CD):
                    if co >= 2:
                        gp.wait_ge(p.sem("f3_rcmm"), B["f3_rcmm"] + 2 * (co - 1))
                    p.inc(gp.dma_start(out=wks[:, co % 2],
                                       in_=wcrv[:, :, co * 128:(co + 1) * 128]),
                          f"f3_lwcr{co % 2}", 16)

            @block.tensor
            def _(te):
                te.wait_ge(p.sem("f3_lxc"), B["f3_lxc"] + 16)
                for co in range(CD):
                    te.wait_ge(p.sem(f"f3_lwcr{co % 2}"), B[f"f3_lwcr{co % 2}"] + 16 * (co // 2 + 1))
                    for tt in range(2):
                        i = co * 2 + tt
                        if i >= 2:
                            te.wait_ge(p.sem("f3_rcev"), B["f3_rcev"] + i - 1)
                        for c in range(CD):
                            ins = te.matmul(ps[i % 2][:], wks[:, co % 2, c, :],
                                            xcnb[:, c, tt * 512:(tt + 1) * 512],
                                            start=(c == 0), stop=(c == CD - 1))
                        p.inc(ins, "f3_rcmm", 1)

            @block.scalar
            def _(sc):
                for co in range(CD):
                    for tt in range(2):
                        i = co * 2 + tt
                        sc.wait_ge(p.sem("f3_rcmm"), B["f3_rcmm"] + i + 1)
                        p.inc(sc.activation(rc[:, co, tt * 512:(tt + 1) * 512],
                                            ps[i % 2][:], AF.Sigmoid), "f3_rcev", 1)

            for tt in range(2):
                tsl = slice(tt * 512, (tt + 1) * 512)
                @block.gpsimd
                def _(gp, tt=tt):
                    for f in range(FK):
                        FG = tt * FK + f
                        if FG == 0 and w_gate is not None:
                            gp.wait_ge(p.sem(w_gate[0]), w_gate[1])
                        if FG < 2:
                            gp.wait_ge(p.sem("f3_rcmm"), B["f3_rcmm"] + 32)
                        else:
                            gp.wait_ge(p.sem("f3_kmm"), B["f3_kmm"] + FG - 1)
                        p.inc(gp.dma_start(out=wks[:, FG % 2],
                                           in_=wckv[:, :, f * 128:(f + 1) * 128]),
                              f"f3_lwck{FG % 2}", 16)

                @block.tensor
                def _(te, tt=tt, tsl=tsl):
                    if tt == 0:
                        te.wait_ge(p.sem("f3_rcmm"), B["f3_rcmm"] + 32)
                        te.wait_ge(p.sem("f3_rcev"), B["f3_rcev"] + 32)
                    else:
                        te.wait_ge(p.sem("f3_vcev"), B["f3_vcev"] + 16)
                    te.wait_ge(p.sem("f3_xcsd"), B["f3_xcsd"] + CD)
                    for f in range(FK):
                        FG = tt * FK + f
                        te.wait_ge(p.sem(f"f3_lwck{FG % 2}"), B[f"f3_lwck{FG % 2}"] + 16 * (FG // 2 + 1))
                        if FG >= 2:
                            te.wait_ge(p.sem("f3_ksq"), B["f3_ksq"] + FG - 1)
                        for c in range(CD):
                            ins = te.matmul(ps[FG % 2][:], wks[:, FG % 2, c, :],
                                            xcsb[:, c, tsl], start=(c == 0),
                                            stop=(c == CD - 1))
                        p.inc(ins, "f3_kmm", 1)

                @block.scalar
                def _(sc, tt=tt):
                    for f in range(FK):
                        FG = tt * FK + f
                        sc.wait_ge(p.sem("f3_kmm"), B["f3_kmm"] + FG + 1)
                        p.chain(sc, sc.activation(rtb[:, FG % 2], ps[FG % 2][:], AF.Relu),
                                "sch")
                        p.inc(sc.activation(kcb[:, f], rtb[:, FG % 2], AF.Square), "f3_ksq", 1)

                @block.gpsimd
                def _(gp, tt=tt):
                    for dg in range(2):
                        for f in range(FK):
                            FD = (tt * 2 + dg) * FK + f
                            if FD >= 2:
                                gp.wait_ge(p.sem("f3_vmm"), B["f3_vmm"] + FD - 1)
                            dsl = slice(dg * 1024, (dg + 1) * 1024)
                            p.inc(gp.dma_start(out=wvs[:, FD % 2], in_=wcvv[:, f, dsl]),
                                  f"f3_lwcv{FD % 2}", 16)

                @block.sync
                def _(sync, tt=tt, tsl=tsl):
                    for i in range(CD):
                        gi = tt * CD + i
                        if gi >= 2:
                            sync.wait_ge(p.sem("f3_obd"), B["f3_obd"] + gi - 1)
                        p.inc(sync.dma_start(out=x2t[:, gi % 2], in_=x2v[:, i, tsl]),
                              f"f3_lx2{gi % 2}", 16)
                        p.inc(sync.dma_start(out=xt[:, gi % 2], in_=xv[:, i, tsl]),
                              f"f3_lxr{gi % 2}", 16)

                @block.tensor
                def _(te, tt=tt):
                    te.wait_ge(p.sem("f3_ksq"), B["f3_ksq"] + FK * (tt + 1))
                    for dg in range(2):
                        DG = tt * 2 + dg
                        te.wait_ge(p.sem("f3_vcev"), B["f3_vcev"] + 8 * DG)
                        for f in range(FK):
                            FD = DG * FK + f
                            te.wait_ge(p.sem(f"f3_lwcv{FD % 2}"), B[f"f3_lwcv{FD % 2}"] + 16 * (FD // 2 + 1))
                            for dout in range(8):
                                ins = te.matmul(ps[dout][:],
                                                wvs[:, FD % 2, dout * 128:(dout + 1) * 128],
                                                kcb[:, f], start=(f == 0), stop=(f == FK - 1))
                            p.inc(ins, "f3_vmm", 1)

                @block.scalar
                def _(sc, tt=tt, tsl=tsl):
                    for dg in range(2):
                        DG = tt * 2 + dg
                        for dout in range(8):
                            gi = DG * 8 + dout
                            sc.wait_ge(p.sem("f3_vmm"), B["f3_vmm"] + (DG + 1) * FK)
                            if gi >= 2:
                                sc.wait_ge(p.sem("f3_t1d"), B["f3_t1d"] + gi - 1)
                            p.inc(sc.activation(vcb[:, gi % 2], ps[dout][:], AF.Identity),
                                  "f3_vcev", 1)
                            if gi >= 1:
                                sc.wait_ge(p.sem("f3_obd"), B["f3_obd"] + gi)
                                pch = (gi - 1) % CD
                                ptt = (gi - 1) // CD
                                psl = slice(ptt * 512, (ptt + 1) * 512)
                                p.inc(sc.dma_start(out=xov[:, pch, psl],
                                                   in_=ob[:, (gi - 1) % 2]),
                                      f"f3_obst{(gi - 1) % 2}", 16)

                @block.vector
                def _(ve, tt=tt, tsl=tsl):
                    ve.wait_ge(p.sem("f3_rcev"), B["f3_rcev"] + 32)
                    for dg in range(2):
                        DG = tt * 2 + dg
                        for dout in range(8):
                            gi = DG * 8 + dout
                            ch = dg * 8 + dout
                            ve.wait_ge(p.sem("f3_vcev"), B["f3_vcev"] + gi + 1)
                            ve.wait_ge(p.sem(f"f3_lx2{gi % 2}"), B[f"f3_lx2{gi % 2}"] + 16 * (gi // 2 + 1))
                            ve.wait_ge(p.sem(f"f3_lxr{gi % 2}"), B[f"f3_lxr{gi % 2}"] + 16 * (gi // 2 + 1))
                            if gi >= 2:
                                ve.wait_ge(p.sem(f"f3_obst{gi % 2}"), B[f"f3_obst{gi % 2}"] + 16 * (gi // 2))
                            p.chain(ve, ve.tensor_tensor(t1b[:, gi % 2], vcb[:, gi % 2],
                                                         rc[:, ch, tsl], ALU.mult), "f3_t1d")
                            p.chain(ve, ve.tensor_add(o1b[:, gi % 2], t1b[:, gi % 2],
                                                      x2t[:, gi % 2]), "vch")
                            p.inc(ve.tensor_add(ob[:, gi % 2], o1b[:, gi % 2],
                                                xt[:, gi % 2]), "f3_obd", 1)

            @block.scalar
            def _(sc):
                sc.wait_ge(p.sem("f3_obd"), B["f3_obd"] + 32)
                p.inc(sc.dma_start(out=xov[:, 15, 512:1024], in_=ob[:, 1]), "f3_obst1", 16)
                sc.wait_ge(p.sem("f3_obst0"), B["f3_obst0"] + 16 * 16)
                sc.wait_ge(p.sem("f3_obst1"), B["f3_obst1"] + 16 * 16)



def build_l1at3():
    p = P()
    nc = p.nc
    x = nc.dram_tensor("x", [D, T], F32, kind="ExternalInput").ap()
    x2 = nc.dram_tensor("x2", [D, T], F32, kind="ExternalInput").ap()
    xc = nc.dram_tensor("xc", [D, T], BF16, kind="ExternalInput").ap()
    xcb = nc.dram_tensor("xcb_cm", [128, CD], BF16, kind="ExternalInput").ap()
    wck = nc.dram_tensor("Wck16", [D, 8192], BF16, kind="ExternalInput").ap()
    wcv = nc.dram_tensor("Wcv16", [8192, D], BF16, kind="ExternalInput").ap()
    wcr = nc.dram_tensor("Wcr", [D, D], F32, kind="ExternalInput").ap()
    xo = nc.dram_tensor("xo", [D, T], F32, kind="ExternalOutput").ap()
    with p.stack:
        ones_bf, cst, b1, b2 = _mk_consts(p)
        ps = [p.psum(f"ps{i}", [128, 512]) for i in range(8)]
        emit_l1at3(p, x, x2, xc, xcb, wck, wcv, wcr, xo, ones_bf, cst, ps)
    return p.nc



PAIRS = [[0, 1], [2, 3], [4, 5], [6, 7]]
WORLD = [list(range(NC))]


def emit_cc(p, kind, op, groups, pairs):
    with p.nc.Block() as blk:
        @blk.gpsimd
        def _(gp):
            for src_ap, dst_ap in pairs:
                p.inc(gp.collective_compute(kind, op, replica_groups=groups,
                                            ins=[src_ap.opt()], outs=[dst_ap.opt()]),
                      "k_cc", 1)


def emit_lastcol_glue(p, srcT, ag_in, ag_out, parmask, eff):
    nc = p.nc
    p.pool_reset()
    B = p.snap()
    with (nc.sbuf_tensor("gl_a", [128, CD, 1], BF16) as a3,
          nc.sbuf_tensor("gl_b", [128, CD], BF16) as bb,
          nc.sbuf_tensor("gl_m", [128, CD], BF16) as mm_,
          nc.Block() as blk):
        av = a3[:].rearrange("p c o -> p (c o)")

        @blk.sync
        def _(sync):
            p.inc(sync.dma_start(out=a3[:], in_=chunked(srcT)[:, :, T - 1:T],
                                 allow_slow_non_contiguous=True), "k_gl", 16)
            sync.wait_ge(p.sem("k_gl"), B["k_gl"] + 16)
            p.inc(sync.dma_start(out=ag_in[:], in_=av), "k_gl", 16)
            p.inc(sync.dma_start(out=mm_[:], in_=parmask[:]), "k_gl", 16)

        @blk.gpsimd
        def _(gp):
            gp.wait_ge(p.sem("k_gl"), B["k_gl"] + 48)
            p.inc(gp.collective_compute("AllGather", ALU.bypass, replica_groups=PAIRS,
                                        ins=[ag_in.opt()], outs=[ag_out.opt()]),
                  "k_cc", 1)

        @blk.scalar
        def _(sc):
            sc.wait_ge(p.sem("k_cc"), p.cnt["k_cc"])
            p.inc(sc.dma_start(out=bb[:], in_=ag_out[0:128, :]), "k_gl2", 16)

        @blk.vector
        def _(ve):
            ve.wait_ge(p.sem("k_gl2"), B["k_gl2"] + 16)
            ve.wait_ge(p.sem("k_gl"), B["k_gl"] + 48)
            p.inc(ve.tensor_mul(av, bb[:], mm_[:]), "k_gld", 1)

        @blk.scalar
        def _(sc):
            sc.wait_ge(p.sem("k_gld"), p.cnt["k_gld"])
            p.inc(sc.dma_start(out=eff[:], in_=av), "k_gl2", 16)
    p.fence(["k_gl", "k_gl2", "k_cc", "k_gld"])


def build_fused():
    p = P()
    nc = p.nc

    def EI(n, s, d):
        return nc.dram_tensor(n, s, d, kind="ExternalInput").ap()

    def IN(n, s, d):
        return nc.dram_tensor(n, s, d).ap()

    def SH(n, s, d):
        return nc.dram_tensor(n, s, d, addr_space="Shared").ap()

    x0 = EI("x0", [D, T], F32)
    w1s = EI("w1T_s", [256, F1], F32)
    w2s = EI("w2T_s", [256, F1], F32)
    w3s = EI("w3T_s", [F1, 256], F32)
    wqkv = EI("wqkv", [D, 3 * D], F32)
    bqkv_cm = EI("bqkv_cm", [128, 48], F32)
    bqkv_row = EI("bqkv_row", [1, 3 * D], F32)
    wproj = EI("wproj", [D, D], F32)
    bproj_cm = EI("bproj_cm", [128, CD], F32)
    kbias = EI("kbias", [128, NH * 16], F32)
    qbias = EI("qbias", [NH, 2, 2, 512], BF16)
    masks = EI("masks", [16, 128, 512], BF16)
    ones2 = EI("ones2", [2, 128], BF16)
    wgT = EI("wgT", [D, 2], F32)
    cin, cout = {}, {}
    for nm, (r_, cd_) in CAST_SHARDS.items():
        cin[nm] = EI(nm + "_s", [r_ // NC, cd_], F32)
        cout[nm] = IN(nm + "16_s", [r_ // NC, cd_], BF16)
    ba_cm = [EI(f"b{e}a_cm", [128, FECH], F32) for e in range(2)]
    bb_cm = [EI(f"b{e}b_cm", [128, CD], F32) for e in range(2)]
    Wv = EI("Wv", [D, D], F32)
    Wr = EI("Wr", [D, D], F32)
    Wo = EI("Wo", [D, D], F32)
    Wcr = EI("Wcr", [D, D], F32)
    g1cm = EI("ln1g_cm", [128, CD], F32)
    b1cm = EI("ln1b_cm", [128, CD], F32)
    gtcm = EI("tmg_cm", [128, CD], F32)
    btcm = EI("tmb_cm", [128, CD], F32)
    tdcm = EI("td_cm", [128, CD], F32)
    tfcm = EI("tf_cm", [128, CD], F32)
    bocm = EI("bo_cm", [128, CD], F32)
    g2cm = EI("ln2g_cm", [128, CD], F32)
    b2cm = EI("ln2b_cm", [128, CD], F32)
    gccm = EI("cmg_cm", [128, CD], F32)
    bccm = EI("cmb_cm", [128, CD], F32)
    rev = EI("rev_t", [1, T], F32)
    parmask = EI("parmask", [128, CD], BF16)
    x6 = nc.dram_tensor("x6", [D, T], F32, kind="ExternalOutput").ap()

    q1sh = IN("q1sh", [256, F1], BF16)
    q2sh = IN("q2sh", [256, F1], BF16)
    q3sh = IN("q3sh", [F1, 256], BF16)
    q1T = SH("q1T", [D, F1], BF16)
    q2T = SH("q2T", [D, F1], BF16)
    q3ag = SH("q3ag", [NC * F1, 256], BF16)
    q3q = q3ag.rearrange("(r ch p) c -> p ch r c", r=NC, p=128)
    wa16 = [SH(f"w{e}aT", [D, FE], BF16) for e in range(2)]
    wb16 = [SH(f"w{e}bT", [FE, D], BF16) for e in range(2)]
    wck16 = SH("Wck16", [D, 8192], BF16)
    wcv16 = SH("Wcv16", [8192, D], BF16)
    x1 = IN("x1", [D, T], F32)
    x2 = IN("x2", [D, T], F32)
    x3 = IN("x3", [D, T], F32)
    x4 = IN("x4", [D, T], F32)
    x5 = IN("x5", [D, T], F32)
    xex = IN("xex", [2 * D, T], F32)
    yF = IN("yF", [D, T], F32)
    xlnF = IN("xlnF", [D, T], BF16)
    vmp = IN("vmp", [128, CD], F32)
    vmpsum = IN("vmpsum", [128, CD], F32)
    x2L = IN("x2L", [D, T], F32)
    xcF = IN("xcF", [D, T], BF16)
    xlc_in = IN("xlc_in", [128, CD], BF16)
    xlc_out = IN("xlc_out", [256, CD], BF16)
    xlb_eff = IN("xlb_eff", [128, CD], BF16)
    xcc_in = IN("xcc_in", [128, CD], BF16)
    xcc_out = IN("xcc_out", [256, CD], BF16)
    xcb_eff = IN("xcb_eff", [128, CD], BF16)
    cc_in = IN("cc_in", [1, 8], F32)
    cc_out = IN("cc_out", [1, 8], F32)
    colb = IN("colb", [1, 512], F32)
    rowb = IN("rowb", [1, 8], F32)
    mb1 = IN("mb1", [1, 512], F32)
    mb2 = IN("mb2", [1, 512], F32)
    rvb = IN("rvb", [1, T], F32)
    vdram = IN("vdram", [2048, D], BF16)
    adram = IN("adram", [D, T], BF16)

    with p.stack:
        ones_bf, cst, b1, b2 = _mk_consts(p)
        ps = [p.psum(f"ps{i}", [128, 512]) for i in range(8)]

        with nc.named_scope("prep"):
            emit_prep(p, w1s, w2s, w3s, q1sh, q2sh, q3sh, cin, cout, cc_in, cc_out,
                      colb, rowb, do_casts=True,
                      ag_q=((q1sh, q1T), (q2sh, q2T), (q3sh, q3ag)),
                      ag_w=((cout["W0aT"], wa16[0]), (cout["W0bT"], wb16[0]),
                            (cout["W1aT"], wa16[1]), (cout["W1bT"], wb16[1]),
                            (cout["Wck"], wck16), (cout["Wcv"], wcv16)))

        with nc.named_scope("ffn0a"):
            emit_l0ffn(p, x0, q1T, q2T, None, x1, ones_bf, cst, b1, b2, ps, q3q=q3q,
                       q_gate=("k_cc", 2, 3))
        with nc.named_scope("ag_x1"):
            emit_cc(p, "AllGather", ALU.bypass, PAIRS, [(x1, xex)])
        with nc.named_scope("attn0"):
            emit_l0attn(p, x1, xex[0:D, :], wqkv, bqkv_cm, bqkv_row, wproj, bproj_cm,
                        kbias, qbias, masks, ones2, x2, ones_bf, cst, b1, b2, ps,
                        vdram, adram, xfh_gate=("k_cc", 10))
        with nc.named_scope("ffn0b"):
            emit_l0ffn(p, x2, q1T, q2T, None, x3, ones_bf, cst, b1, b2, ps, q3q=q3q)

        with nc.named_scope("moe1"):
            emit_l1ffn(p, x3, wgT, wa16, wb16, ba_cm, bb_cm, x4, ones_bf, cst, b1, b2,
                       ps, mb1, mb2, w_gate=("k_cc", 9))
        with nc.named_scope("rwkv1"):
            emit_l1at1(p, x4, Wv, g1cm, b1cm, gtcm, btcm, tdcm, rev, yF, xlnF, vmp,
                       ones_bf, cst, b1, b2, ps, rvb)
        with nc.named_scope("glue1"):
            emit_cc(p, "AllReduce", ALU.add, PAIRS, [(vmp, vmpsum)])
            emit_lastcol_glue(p, xlnF, xlc_in, xlc_out, parmask, xlb_eff)
        with nc.named_scope("rwkv2"):
            emit_l1at2(p, yF, xlnF, xlb_eff, vmpsum, None, tfcm, Wr, Wo, bocm,
                       g2cm, b2cm, gccm, bccm, x2L, xcF, ones_bf, cst, b1, b2, ps)
        with nc.named_scope("glue2"):
            emit_lastcol_glue(p, xcF, xcc_in, xcc_out, parmask, xcb_eff)
        with nc.named_scope("rwkv3"):
            emit_l1at3(p, x4, x2L, xcF, xcb_eff, wck16, wcv16, Wcr, x5,
                       ones_bf, cst, ps, w_gate=("k_cc", 9))
        with nc.named_scope("moe2"):
            emit_l1ffn(p, x5, wgT, wa16, wb16, ba_cm, bb_cm, x6, ones_bf, cst, b1, b2,
                       ps, mb1, mb2, w_gate=("k_cc", 9))
    return p


def _fused_inputs(inp):
    x_full = np.ascontiguousarray(
        np.asarray(inp["L0_attn_hidden_states"], np.float32).reshape(NC * T, D))
    w1T = np.ascontiguousarray(np.asarray(inp["L0_ffn_w1"], np.float32).T)
    w2T = np.ascontiguousarray(np.asarray(inp["L0_ffn_w2"], np.float32).T)
    w3T = np.ascontiguousarray(np.asarray(inp["L0_ffn_w3"], np.float32).T)
    kb_pc, qb_pc, masks_pc = _attn_consts()
    ones_bf = np.ones((128, 1), BF)
    ones2 = np.ones((2, 128), BF)
    shared = {
        "wqkv": np.ascontiguousarray(np.asarray(inp["L0_attn_w_qkv"], np.float32)),
        "bqkv_cm": _cm(inp["L0_attn_b_qkv"]),
        "bqkv_row": np.asarray(inp["L0_attn_b_qkv"], np.float32)[None, :],
        "wproj": np.ascontiguousarray(np.asarray(inp["L0_attn_w_proj"], np.float32)),
        "bproj_cm": _cm(inp["L0_attn_b_proj"]),
        "ones2": ones2, "ones_bf": ones_bf, "cstf": _cstf(),
        "wgT": np.ascontiguousarray(np.asarray(inp["L1_ffn_Wg"], np.float32).T),
        "b0a_cm": _cm(inp["L1_ffn_b0a"]), "b0b_cm": _cm(inp["L1_ffn_b0b"]),
        "b1a_cm": _cm(inp["L1_ffn_b1a"]), "b1b_cm": _cm(inp["L1_ffn_b1b"]),
        "Wv": np.ascontiguousarray(np.asarray(inp["L1_attn_Wv"], np.float32)),
        "Wr": np.ascontiguousarray(np.asarray(inp["L1_attn_Wr"], np.float32)),
        "Wo": np.ascontiguousarray(np.asarray(inp["L1_attn_Wo"], np.float32)),
        "Wcr": np.ascontiguousarray(np.asarray(inp["L1_attn_Wcr"], np.float32)),
        "ln1g_cm": _cm(inp["L1_attn_ln1_g"]), "ln1b_cm": _cm(inp["L1_attn_ln1_b"]),
        "tmg_cm": _cm(inp["L1_attn_tm_g"]), "tmb_cm": _cm(inp["L1_attn_tm_b"]),
        "td_cm": _cm(inp["L1_attn_time_decay"]), "tf_cm": _cm(inp["L1_attn_time_first"]),
        "bo_cm": _cm(inp["L1_attn_bo"]),
        "ln2g_cm": _cm(inp["L1_attn_ln2_g"]), "ln2b_cm": _cm(inp["L1_attn_ln2_b"]),
        "cmg_cm": _cm(inp["L1_attn_cm_g"]), "cmb_cm": _cm(inp["L1_attn_cm_b"]),
    }
    casts = {"W0aT": np.ascontiguousarray(np.asarray(inp["L1_ffn_W0a"], np.float32).T),
             "W0bT": np.ascontiguousarray(np.asarray(inp["L1_ffn_W0b"], np.float32).T),
             "W1aT": np.ascontiguousarray(np.asarray(inp["L1_ffn_W1a"], np.float32).T),
             "W1bT": np.ascontiguousarray(np.asarray(inp["L1_ffn_W1b"], np.float32).T),
             "Wck": np.ascontiguousarray(np.asarray(inp["L1_attn_Wck"], np.float32)),
             "Wcv": np.ascontiguousarray(np.asarray(inp["L1_attn_Wcv"], np.float32))}
    maps = []
    for c in range(NC):
        m = dict(shared)
        m["x0"] = _fm_shard(x_full, c)
        for nm, arr in casts.items():
            rs = arr.shape[0] // NC
            m[nm + "_s"] = np.ascontiguousarray(arr[c * rs:(c + 1) * rs])
        m["w1T_s"] = np.ascontiguousarray(w1T[c * 256:(c + 1) * 256])
        m["w2T_s"] = np.ascontiguousarray(w2T[c * 256:(c + 1) * 256])
        m["w3T_s"] = np.ascontiguousarray(w3T[:, c * 256:(c + 1) * 256])
        m["kbias"] = kb_pc[c]
        m["qbias"] = qb_pc[c]
        m["masks"] = masks_pc[c]
        m["rev_t"] = (2047.0 - (np.arange(T) + (c % 2) * 1024)).astype(np.float32)[None, :]
        m["parmask"] = np.full((128, CD), float(c % 2), BF)
        maps.append(m)
    return maps


def kernel(**inp):
    p = build_fused()
    maps = _fused_inputs(inp)
    res = run_bass_kernel_spmd(p.nc, maps, core_ids=CORES)
    x6 = np.vstack([np.asarray(res.results[c]["x6"], np.float32).T for c in range(NC)])
    return np.ascontiguousarray(x6.reshape(4, 2048, 2048))



_CST = None


def _cstf():
    c = np.zeros((1, 8), np.float32)
    c[0, 0] = LN_EPS
    c[0, 1] = LN_EPS_IN
    return c


def _gather_fm(results, key):
    return np.vstack([np.asarray(results[c][key], np.float32).T for c in range(NC)])


def kernel_multi(**inp):
    ones_bf = np.ones((128, 1), BF)
    cst = _cstf()

    ncA = build_prep()
    resA = run_bass_kernel_spmd(ncA, _prep_inputs(inp), core_ids=CORES)
    W = _prep_gather(resA.results)

    x_full = np.ascontiguousarray(
        np.asarray(inp["L0_attn_hidden_states"], np.float32).reshape(NC * T, D))

    ncB = build_l0ffn()

    def run_l0ffn(xf):
        maps = [{"x": _fm_shard(xf, c), "q1T": W["q1T"], "q2T": W["q2T"],
                 "q3T": W["q3T"], "ones_bf": ones_bf, "cstf": cst} for c in range(NC)]
        res = run_bass_kernel_spmd(ncB, maps, core_ids=CORES)
        return _gather_fm(res.results, "xo")

    x1 = run_l0ffn(x_full)

    kb_pc, qb_pc, masks_pc = _attn_consts()
    ncC = build_l0attn()
    res = run_bass_kernel_spmd(ncC, _attn_inputs(inp, x1, kb_pc, qb_pc, masks_pc),
                               core_ids=CORES)
    x2 = _gather_fm(res.results, "xo")

    x3 = run_l0ffn(x2)

    ncE = build_l1ffn()
    wgT = np.ascontiguousarray(np.asarray(inp["L1_ffn_Wg"], np.float32).T)
    eb = {"wgT": wgT,
          "w0aT": np.ascontiguousarray(W["W0aT16"]), "w0bT": np.ascontiguousarray(W["W0bT16"]),
          "w1aT": np.ascontiguousarray(W["W1aT16"]), "w1bT": np.ascontiguousarray(W["W1bT16"]),
          "b0a_cm": _cm(inp["L1_ffn_b0a"]), "b0b_cm": _cm(inp["L1_ffn_b0b"]),
          "b1a_cm": _cm(inp["L1_ffn_b1a"]), "b1b_cm": _cm(inp["L1_ffn_b1b"]),
          "ones_bf": ones_bf, "cstf": cst}

    def run_l1ffn(xf):
        maps = [dict(eb, x=_fm_shard(xf, c)) for c in range(NC)]
        res = run_bass_kernel_spmd(ncE, maps, core_ids=CORES)
        return _gather_fm(res.results, "xo")

    x4 = run_l1ffn(x3)

    ncF1 = build_l1at1()
    maps = []
    for c in range(NC):
        rev = (2047.0 - (np.arange(T) + (c % 2) * 1024)).astype(np.float32)[None, :]
        maps.append({"x": _fm_shard(x4, c),
                     "Wv": np.ascontiguousarray(np.asarray(inp["L1_attn_Wv"], np.float32)),
                     "ln1g_cm": _cm(inp["L1_attn_ln1_g"]), "ln1b_cm": _cm(inp["L1_attn_ln1_b"]),
                     "tmg_cm": _cm(inp["L1_attn_tm_g"]), "tmb_cm": _cm(inp["L1_attn_tm_b"]),
                     "td_cm": _cm(inp["L1_attn_time_decay"]), "rev_t": rev,
                     "ones_bf": ones_bf, "cstf": cst})
    resF1 = run_bass_kernel_spmd(ncF1, maps, core_ids=CORES)
    r1 = resF1.results

    ncF2 = build_l1at2()
    maps = []
    for c in range(NC):
        even = c - (c % 2)
        if c % 2 == 0:
            xlb = np.zeros((128, CD), BF)
        else:
            xlb = np.ascontiguousarray(
                np.asarray(r1[even]["xlno"])[:, T - 1].reshape(CD, 128).T)
        maps.append({"y": np.asarray(r1[c]["yo"]), "xln": np.asarray(r1[c]["xlno"]),
                     "xlb_cm": xlb, "vmp_own": np.asarray(r1[c]["vmpo"]),
                     "vmp_pair": np.asarray(r1[c ^ 1]["vmpo"]),
                     "tf_cm": _cm(inp["L1_attn_time_first"]),
                     "Wr": np.ascontiguousarray(np.asarray(inp["L1_attn_Wr"], np.float32)),
                     "Wo": np.ascontiguousarray(np.asarray(inp["L1_attn_Wo"], np.float32)),
                     "bo_cm": _cm(inp["L1_attn_bo"]),
                     "ln2g_cm": _cm(inp["L1_attn_ln2_g"]), "ln2b_cm": _cm(inp["L1_attn_ln2_b"]),
                     "cmg_cm": _cm(inp["L1_attn_cm_g"]), "cmb_cm": _cm(inp["L1_attn_cm_b"]),
                     "ones_bf": ones_bf, "cstf": cst})
    resF2 = run_bass_kernel_spmd(ncF2, maps, core_ids=CORES)
    r2 = resF2.results

    ncF3 = build_l1at3()
    wck16 = np.ascontiguousarray(W["Wck16"])
    wcv16 = np.ascontiguousarray(W["Wcv16"])
    wcr = np.ascontiguousarray(np.asarray(inp["L1_attn_Wcr"], np.float32))
    maps = []
    for c in range(NC):
        even = c - (c % 2)
        if c % 2 == 0:
            xcb = np.zeros((128, CD), BF)
        else:
            xcb = np.ascontiguousarray(
                np.asarray(r2[even]["xco"])[:, T - 1].reshape(CD, 128).T)
        maps.append({"x": _fm_shard(x4, c), "x2": np.asarray(r2[c]["x2o"]),
                     "xc": np.asarray(r2[c]["xco"]), "xcb_cm": xcb,
                     "Wck16": wck16, "Wcv16": wcv16, "Wcr": wcr,
                     "ones_bf": ones_bf, "cstf": cst})
    resF3 = run_bass_kernel_spmd(ncF3, maps, core_ids=CORES)
    x5 = _gather_fm(resF3.results, "xo")

    x6 = run_l1ffn(x5)

    return np.ascontiguousarray(x6.reshape(4, 2048, 2048))


# version 06 (final)
# speedup vs baseline: 1.1283x; 1.0083x over previous
import contextlib
import numpy as np
import ml_dtypes
import concourse.bass as bass
import concourse.mybir as mybir
from concourse.bass_utils import run_bass_kernel_spmd

F32 = mybir.dt.float32
BF16 = mybir.dt.bfloat16
AF = mybir.ActivationFunctionType
ALU = mybir.AluOpType
AX = mybir.AxisListType
BF = ml_dtypes.bfloat16

NC = 8
D = 2048
CD = D // 128
T = 1024
F1 = 5632
NH, HD = 16, 128
FBIG = 3.0e38
KILL = -1.0e12
LN_EPS = 1e-5
LN_EPS_IN = 1e-12

CORES = list(range(NC))



def _alibi_slopes(n):
    import math
    def pow2_slopes(n):
        start = 2.0 ** (-2.0 ** (-(math.log2(n) - 3)))
        return [start * start ** i for i in range(n)]
    if math.log2(n).is_integer():
        return pow2_slopes(n)
    cp2 = 2 ** math.floor(math.log2(n))
    return pow2_slopes(cp2) + _alibi_slopes(2 * cp2)[0::2][:n - cp2]


def _cm(v):
    return np.ascontiguousarray(np.asarray(v, np.float32).reshape(-1, 128).T)


def _fm_shard(flat, c):
    return np.ascontiguousarray(flat[c * T:(c + 1) * T].T)


MASK_POS = ([(0, kt) for kt in range(4)] + [(1, kt) for kt in range(4, 8)]
            + [(0, kt) for kt in range(8, 12)] + [(1, kt) for kt in range(12, 16)])
SKIP_POS = {(0, kt) for kt in range(12, 16)}


def _attn_consts():
    slopes = np.array(_alibi_slopes(NH), np.float64)
    dk = np.arange(128, dtype=np.float64)
    kb_pc, qb_pc, masks_pc = [], [], []
    for c in range(NC):
        odd = c % 2 == 1
        off = 1024.0 if odd else 0.0
        kb = np.zeros((128, NH * 16), np.float64)
        for h in range(NH):
            for kt in range(16):
                kg = kt * 128.0 + dk if kt < 8 else (kt - 8) * 128.0 + off + dk
                v = slopes[h] * kg
                if (not odd) and kt < 8:
                    v = np.full(128, KILL)
                kb[:, h * 16 + kt] = v
        kb_pc.append(kb.astype(np.float32))

        q_glob = np.arange(T, dtype=np.float64) + off
        qb = np.zeros((NH, 2, 2, 512), np.float64)
        for h in range(NH):
            for qt in range(2):
                v = -128.0 * slopes[h] * (q_glob[qt * 512:(qt + 1) * 512] + 64.0)
                hi = v.astype(BF).astype(np.float64)
                lo = (v - hi).astype(BF).astype(np.float64)
                qb[h, qt, 0] = hi
                qb[h, qt, 1] = lo
        qb_pc.append(qb.astype(BF))

        q_own = np.arange(T, dtype=np.float64)
        masks = np.zeros((16, 128, 512), np.float64)
        for i, (qt, kt) in enumerate(MASK_POS):
            q = q_own[qt * 512:(qt + 1) * 512]
            if kt < 8:
                masks[i] = FBIG if odd else np.where(
                    (kt * 128 + np.arange(128.0))[:, None] <= q[None, :], FBIG, -FBIG)
            else:
                k = (kt - 8) * 128 + np.arange(128.0)
                masks[i] = np.where(k[:, None] <= q[None, :], FBIG, -FBIG)
        masks_pc.append(masks.astype(BF))
    return kb_pc, qb_pc, masks_pc



class P:

    def __init__(self):
        self.nc = bass.Bass("TRN2", target_bir_lowering=False)
        self.stack = contextlib.ExitStack()
        self.sems = {}
        self.cnt = {}
        orig_sbuf = self.nc.sbuf_tensor
        seen = {}

        def _uniq_sbuf(name, shape, dt, **kw):
            n = seen.get(name, 0)
            seen[name] = n + 1
            return orig_sbuf(name if n == 0 else f"{name}_u{n}", shape, dt, **kw)
        self.nc.sbuf_tensor = _uniq_sbuf
        self.alias = {}
        self.pool_next = 0
        self.sw_next = 0

    _POOLED = ("pr_", "ld1_", "ld3_", "qst_", "cst_", "cld_", "ff_", "at_",
               "mo_", "f1_", "f2_", "f3_")
    _SW = {"at_lwv0", "at_lwv1", "at_lwk0", "at_lwk1", "at_lwq0", "at_lwq1",
           "at_lwp0", "at_lwp1", "f1_lwv0", "f1_lwv1", "f2_lwr0", "f2_lwr1",
           "f2_lwo0", "f2_lwo1", "f3_lwcr0", "f3_lwcr1", "f3_lwck0",
           "f3_lwck1", "f3_lwcv0", "f3_lwcv1", "mo_lwg", "cld_0", "cld_1"}

    def _phys(self, name):
        if not name.startswith(self._POOLED):
            return name
        ph = self.alias.get(name)
        if ph is None:
            if name in self._SW:
                ph = f"h{self.sw_next}"
                self.sw_next += 1
            else:
                ph = f"g{self.pool_next}"
                self.pool_next += 1
            self.alias[name] = ph
        return ph

    def pool_reset(self):
        self.alias = {}
        self.pool_next = 0
        self.sw_next = 0

    def sem(self, name):
        name = self._phys(name)
        if name not in self.sems:
            self.sems[name] = self.stack.enter_context(self.nc.semaphore(name))
            self.cnt[name] = 0
        return self.sems[name]

    def inc(self, ins, name, by=1):
        name = self._phys(name)
        ins.then_inc(self.sem(name), by)
        self.cnt[name] += by
        return self.cnt[name]

    def sbuf(self, name, shape, dt):
        return self.stack.enter_context(self.nc.sbuf_tensor(name, shape, dt))

    def psum(self, name, shape, dt=F32):
        return self.stack.enter_context(self.nc.psum_tensor(name, shape, dt))

    def scope(self):
        return self.stack.enter_context(contextlib.ExitStack())

    def sbuf_in(self, scope, name, shape, dt):
        return scope.enter_context(self.nc.sbuf_tensor(name, shape, dt))

    def chain(self, eng, ins, name="chn"):
        v = self.inc(ins, name, 1)
        eng.wait_ge(self.sem(name), v)

    def snap(self):
        snap = dict(self.cnt)
        pp = self

        class _S:
            def __getitem__(_, n):
                return snap.get(pp._phys(n), 0)
        return _S()

    def fence(self, names=None):
        nms = list(self.cnt) if names is None else names
        vals = [(n, self.cnt[n]) for n in nms if self.cnt.get(n, 0) > 0]
        with self.nc.Block() as blk:
            @blk.sync
            def _(sync):
                for n, v in vals:
                    sync.wait_ge(self.sem(n), v)


def chunked(ap, p=128):
    return ap.rearrange("(c p) f -> p c f", p=p)



CAST_SHARDS = {"W0aT": (D, 4096), "W0bT": (4096, D), "W1aT": (D, 4096),
               "W1bT": (4096, D), "Wck": (D, 8192), "Wcv": (8192, D)}
QN_TOT = float(D * F1)


def emit_prep(p, w1, w2, w3, q1, q2, q3, cin, cout, cc_in, cc_out, colb, rowb,
              do_casts=True, ag_q=None, ag_w=None):
    nc = p.nc
    p.pool_reset()
    B = p.snap()

    wv = {"w1": (chunked(w1), chunked(q1), F1), "w2": (chunked(w2), chunked(q2), F1),
          "w3": (chunked(w3), chunked(q3), 256)}
    TILES = []
    for nm in ("w1", "w2", "w3"):
        iv, ov, wid = wv[nm]
        tw = wid // 4
        for i in range(4):
            TILES.append((nm, iv[:, :, i * tw:(i + 1) * tw], ov[:, :, i * tw:(i + 1) * tw]))

    CT = []
    for nm, (r, cdim) in (CAST_SHARDS.items() if do_casts else []):
        nch = (r // NC) // 128
        na = cdim // 2048
        ivf = chunked(cin[nm]).rearrange("p c (a b) -> p c a b", b=2048)
        ovf = chunked(cout[nm]).rearrange("p c (a b) -> p c a b", b=2048)
        if nch * na == 4:
            CT.append((ivf, ovf, (nch, na)))
        elif na == 4:
            for i in range(2):
                CT.append((ivf[:, :, i * 2:(i + 1) * 2], ovf[:, :, i * 2:(i + 1) * 2], (nch, 2)))
        else:
            for i in range(2):
                CT.append((ivf[:, i * 4:(i + 1) * 4], ovf[:, i * 4:(i + 1) * 4], (4, 1)))

    with contextlib.ExitStack() as _prs:
        def _sb(n, s, d):
            return _prs.enter_context(nc.sbuf_tensor(n, s, d))
        wt = _sb("pr_wt", [128, 2, 2, 1408], F32)
        wt3 = _sb("pr_wt3", [128, 2, 44, 64], F32)
        acc = _sb("pr_acc", [128, 12], F32)
        accw = _sb("pr_pr_accw", [128, 4], F32)
        ta = _sb("pr_ta", [1, 8], F32)
        rr = _sb("pr_rr", [1, 512], F32)
        prow = _sb("pr_prow", [1, 8], F32)
        trow = _sb("pr_trow", [1, 8], F32)
        t2row = _sb("pr_t2row", [1, 8], F32)
        thr = _sb("pr_thr", [128, 8], F32)
        qa = _sb("pr_qa", [128, 2, 1408], F32)
        qa3 = _sb("pr_qa3", [128, 44, 64], F32)
        qo = _sb("pr_qo", [128, 2, 2, 1408], BF16)
        qo3 = _sb("pr_qo3", [128, 2, 44, 64], BF16)
        cb = _sb("pr_cb", [128, 2, 4, 2048], BF16)

        def buf(nm, i):
            return wt3[:, i % 2] if nm == "w3" else wt[:, i % 2]

        def qbuf(nm, i):
            return qo3[:, i % 2] if nm == "w3" else qo[:, i % 2]

        def abuf(nm):
            return qa3 if nm == "w3" else qa

        with nc.Block() as block:

            @block.sync
            def _(sync):
                for i, (nm, iv, ov) in enumerate(TILES):
                    if i >= 2:
                        sync.wait_ge(p.sem("pr_gcons"), B["pr_gcons"] + i - 1)
                    p.inc(sync.dma_start(out=buf(nm, i), in_=iv), f"ld1_{i % 2}", 16)
                sync.wait_ge(p.sem("pr_gcons"), B["pr_gcons"] + 12)
                for i, (nm, iv, ov) in enumerate(TILES):
                    if i >= 2:
                        sync.wait_ge(p.sem("pr_qdone"), B["pr_qdone"] + i - 1)
                    p.inc(sync.dma_start(out=buf(nm, i), in_=iv), f"ld3_{i % 2}", 16)

            @block.vector
            def _(vector):
                for i, (nm, iv, ov) in enumerate(TILES):
                    vector.wait_ge(p.sem(f"ld1_{i % 2}"), 16 * (i // 2 + 1))
                    ins = vector.tensor_reduce(acc[:, i:i + 1], buf(nm, i)[:], AX.XY, ALU.add,
                                               apply_absolute_value=True)
                    p.inc(ins, "pr_gcons", 1)
                vector.wait_ge(p.sem("pr_gcons"), 12)
                for wi in range(3):
                    vector.tensor_reduce(accw[:, wi:wi + 1], acc[:, 4 * wi:4 * wi + 4], AX.X, ALU.add)
                p.inc(vector.tensor_copy(accw[:, 3:4], acc[:, 0:1]), "pr_accw", 1)
                vector.wait_ge(p.sem("pr_rowld"), B["pr_rowld"] + 16)
                for wi in range(3):
                    ins = vector.tensor_reduce(prow[0:1, wi:wi + 1], rr[0:1, wi:512:4], AX.X, ALU.add)
                    p.inc(ins, "vch", 1)
                    vector.wait_ge(p.sem("vch"), p.cnt["vch"])
                p.inc(vector.tensor_copy(prow[0:1, 3:8], rr[0:1, 3:8]), "pr_partials", 1)
                vector.wait_ge(p.sem("pr_trld"), B["pr_trld"] + 16)
                ins = vector.tensor_scalar(ta[0:1, 0:3], trow[0:1, 0:3], 1.0 / QN_TOT, 1e-8, ALU.mult, ALU.add)
                p.inc(ins, "vch", 1)
                vector.wait_ge(p.sem("vch"), p.cnt["vch"])
                ins = vector.tensor_scalar(t2row[0:1, 0:3], ta[0:1, 0:3], 0.5, None, ALU.mult)
                p.inc(ins, "vch", 1)
                vector.wait_ge(p.sem("vch"), p.cnt["vch"])
                vector.tensor_scalar(t2row[0:1, 4:7], t2row[0:1, 0:3], -1.0, None, ALU.mult)
                ins = vector.tensor_copy(t2row[0:1, 3:4], t2row[0:1, 0:1])
                p.inc(ins, "vch", 1)
                vector.wait_ge(p.sem("vch"), p.cnt["vch"])
                p.inc(vector.tensor_copy(t2row[0:1, 7:8], t2row[0:1, 0:1]), "pr_thrmath", 1)
                for i, (nm, iv, ov) in enumerate(TILES):
                    vector.wait_ge(p.sem(f"ld3_{i % 2}"), 16 * (i // 2 + 1))
                    if i == 0:
                        vector.wait_ge(p.sem("pr_thrld"), B["pr_thrld"] + 16)
                    if i >= 2:
                        vector.wait_ge(p.sem(f"qst_{i % 2}"), 16 * (i // 2))
                    wi = i // 4
                    src, dst, sc = buf(nm, i), qbuf(nm, i), abuf(nm)
                    if i >= 1:
                        vector.wait_ge(p.sem("pr_qdone"), i)
                    p.chain(vector, vector.tensor_scalar(sc[:], src[:], thr[:, 4 + wi:5 + wi],
                                                         None, ALU.is_lt), "vch")
                    ins = vector.scalar_tensor_tensor(dst[:], src[:], thr[:, wi:wi + 1], sc[:],
                                                      ALU.is_gt, ALU.subtract)
                    p.inc(ins, "pr_qdone", 1)

            @block.scalar
            def _(scalar):
                scalar.wait_ge(p.sem("pr_accw"), B["pr_accw"] + 1)
                p.inc(scalar.dma_start(out=colb.rearrange("o (pp w) -> pp (o w)", w=4), in_=accw[:]), "pr_cold", 16)
                scalar.wait_ge(p.sem("pr_cold"), B["pr_cold"] + 16)
                p.inc(scalar.dma_start(out=rr[0:1, :], in_=colb[0:1, :]), "pr_rowld", 16)
                scalar.wait_ge(p.sem("pr_partials"), B["pr_partials"] + 1)
                p.inc(scalar.dma_start(out=cc_in[0:1, :], in_=prow[0:1, :]), "pr_ccin", 16)
                scalar.wait_ge(p.sem("pr_ccdone"), B["pr_ccdone"] + 1)
                p.inc(scalar.dma_start(out=trow[0:1, :], in_=cc_out[0:1, :]), "pr_trld", 16)
                scalar.wait_ge(p.sem("pr_thrmath"), B["pr_thrmath"] + 1)
                p.inc(scalar.dma_start(out=rowb[0:1, :], in_=t2row[0:1, :]), "pr_rbst", 16)
                scalar.wait_ge(p.sem("pr_rbst"), B["pr_rbst"] + 16)
                p.inc(scalar.dma_start(out=thr[:], in_=rowb.broadcast_to((128, 8))), "pr_thrld", 16)
                for i, (nm, iv, ov) in enumerate(TILES):
                    scalar.wait_ge(p.sem("pr_qdone"), B["pr_qdone"] + i + 1)
                    p.inc(scalar.dma_start(out=ov, in_=qbuf(nm, i)), f"qst_{i % 2}", 16)
                for j, (ivf, ovf, (cc, aa)) in enumerate(CT):
                    scalar.wait_ge(p.sem(f"cld_{j % 2}"), 16 * (j // 2 + 1))
                    cbv = cb[:, j % 2].rearrange("p (c a) b -> p c a b", c=cc)
                    p.inc(scalar.dma_start(out=ovf, in_=cbv), f"cst_{j % 2}", 16)
                scalar.wait_ge(p.sem("qst_0"), 16 * 6)
                scalar.wait_ge(p.sem("qst_1"), 16 * 6)
                scalar.wait_ge(p.sem("cst_0"), 16 * ((len(CT) + 1) // 2))
                scalar.wait_ge(p.sem("cst_1"), 16 * (len(CT) // 2))

            @block.gpsimd
            def _(gpsimd):
                gpsimd.wait_ge(p.sem("pr_ccin"), B["pr_ccin"] + 16)
                p.inc(gpsimd.collective_compute(
                    "AllReduce", ALU.add, replica_groups=[CORES],
                    ins=[cc_in.opt()], outs=[cc_out.opt()]), "pr_ccdone", 1)
                for j, (ivf, ovf, (cc, aa)) in enumerate(CT):
                    if j >= 2:
                        gpsimd.wait_ge(p.sem(f"cst_{j % 2}"), 16 * (j // 2))
                    cbv = cb[:, j % 2].rearrange("p (c a) b -> p c a b", c=cc)
                    p.inc(gpsimd.dma_start(out=cbv, in_=ivf), f"cld_{j % 2}", 16)
                if ag_q is not None:
                    for idx, (src, dst) in enumerate(ag_q):
                        gpsimd.wait_ge(p.sem("qst_0"), 32 * (idx + 1))
                        gpsimd.wait_ge(p.sem("qst_1"), 32 * (idx + 1))
                        p.inc(gpsimd.collective_compute(
                            "AllGather", ALU.bypass, replica_groups=[CORES],
                            ins=[src.opt()], outs=[dst.opt()]), "k_cc", 1)
                if ag_w is not None:
                    gpsimd.wait_ge(p.sem("cst_0"), 16 * ((len(CT) + 1) // 2))
                    gpsimd.wait_ge(p.sem("cst_1"), 16 * (len(CT) // 2))
                    for src, dst in ag_w:
                        p.inc(gpsimd.collective_compute(
                            "AllGather", ALU.bypass, replica_groups=[CORES],
                            ins=[src.opt()], outs=[dst.opt()]), "k_cc", 1)



def build_prep():
    p = P()
    nc = p.nc
    w1 = nc.dram_tensor("w1T_s", [256, F1], F32, kind="ExternalInput").ap()
    w2 = nc.dram_tensor("w2T_s", [256, F1], F32, kind="ExternalInput").ap()
    w3 = nc.dram_tensor("w3T_s", [F1, 256], F32, kind="ExternalInput").ap()
    q1 = nc.dram_tensor("q1T_s", [256, F1], BF16, kind="ExternalOutput").ap()
    q2 = nc.dram_tensor("q2T_s", [256, F1], BF16, kind="ExternalOutput").ap()
    q3 = nc.dram_tensor("q3T_s", [F1, 256], BF16, kind="ExternalOutput").ap()
    cin, cout = {}, {}
    for nm, (r, cdim) in CAST_SHARDS.items():
        cin[nm] = nc.dram_tensor(nm + "_s", [r // NC, cdim], F32, kind="ExternalInput").ap()
        cout[nm] = nc.dram_tensor(nm + "16_s", [r // NC, cdim], BF16, kind="ExternalOutput").ap()
    cc_in = nc.dram_tensor("cc_in", [1, 8], F32).ap()
    cc_out = nc.dram_tensor("cc_out", [1, 8], F32).ap()
    colb = nc.dram_tensor("colb", [1, 512], F32).ap()
    rowb = nc.dram_tensor("rowb", [1, 8], F32).ap()
    with p.stack:
        emit_prep(p, w1, w2, w3, q1, q2, q3, cin, cout, cc_in, cc_out, colb, rowb)
    return p.nc


def _prep_inputs(inp):
    w1T = np.ascontiguousarray(inp["L0_ffn_w1"].T)
    w2T = np.ascontiguousarray(inp["L0_ffn_w2"].T)
    w3T = np.ascontiguousarray(inp["L0_ffn_w3"].T)
    casts = {"W0aT": np.ascontiguousarray(inp["L1_ffn_W0a"].T),
             "W0bT": np.ascontiguousarray(inp["L1_ffn_W0b"].T),
             "W1aT": np.ascontiguousarray(inp["L1_ffn_W1a"].T),
             "W1bT": np.ascontiguousarray(inp["L1_ffn_W1b"].T),
             "Wck": np.ascontiguousarray(inp["L1_attn_Wck"]),
             "Wcv": np.ascontiguousarray(inp["L1_attn_Wcv"])}
    maps = []
    for c in range(NC):
        m = {
            "w1T_s": np.ascontiguousarray(w1T[c * 256:(c + 1) * 256]),
            "w2T_s": np.ascontiguousarray(w2T[c * 256:(c + 1) * 256]),
            "w3T_s": np.ascontiguousarray(w3T[:, c * 256:(c + 1) * 256]),
        }
        for nm, arr in casts.items():
            rs = arr.shape[0] // NC
            m[nm + "_s"] = np.ascontiguousarray(arr[c * rs:(c + 1) * rs])
        maps.append(m)
    return maps


def _prep_gather(results):
    out = {}
    out["q1T"] = np.vstack([results[c]["q1T_s"] for c in range(NC)])
    out["q2T"] = np.vstack([results[c]["q2T_s"] for c in range(NC)])
    out["q3T"] = np.ascontiguousarray(np.hstack([results[c]["q3T_s"] for c in range(NC)]))
    for nm in CAST_SHARDS:
        out[nm + "16"] = np.vstack([results[c][nm + "16_s"] for c in range(NC)])
    return out



def emit_ln(p, block, xs, y, ttw, ones_bf, b1, b2, mb, rb, tmp, tmp2, xbf, xsq,
            ps_s, ps_q, eps_ap, srow, sqrow, r1, r2, sd, rstd,
            gp_wait=(), sc_wait=(), te_wait=(), ve_wait=(), gains=None, nch=CD,
            prev=False):
    B = p.snap()

    @block.gpsimd
    def _(gp):
        for s, v in gp_wait:
            gp.wait_ge(p.sem(s), v)
        for c in range(nch):
            if c >= 2:
                gp.wait_ge(p.sem("ln_mm"), B["ln_mm"] + c - 1)
            p.inc(gp.tensor_copy(xbf[:, c % 2], xs[:, c]), "ln_xb", 1)

    @block.scalar
    def _(sc):
        for s, v in sc_wait:
            sc.wait_ge(p.sem(s), v)
        for c in range(nch):
            if c >= 2:
                sc.wait_ge(p.sem("ln_mm"), B["ln_mm"] + c - 1)
            p.inc(sc.activation(xsq[:, c % 2], xs[:, c], AF.Square), "ln_xq", 1)
        sc.wait_ge(p.sem("ln_mm"), B["ln_mm"] + nch)
        if prev:
            sc.wait_ge(p.sem("ln_bst"), B["ln_bst"])
        p.inc(sc.activation(srow[0:1, :], ps_s[0:1, :], AF.Identity, scale=1.0 / D), "ln_sr", 1)
        p.inc(sc.activation(sqrow[0:1, :], ps_q[0:1, :], AF.Identity, scale=1.0 / D), "ln_sr", 1)
        sc.wait_ge(p.sem("ln_var"), B["ln_var"] + 1)
        p.inc(sc.activation(sd[0:1, :], r2[0:1, :], AF.Sqrt, bias=eps_ap), "ln_sd", 1)
        sc.wait_ge(p.sem("ln_rstd"), B["ln_rstd"] + 1)
        if prev:
            sc.wait_ge(p.sem("ln_bld"), B["ln_bld"])
        p.inc(sc.dma_start(out=b1[0:1, :], in_=srow[0:1, :]), "ln_bst", 16)
        p.inc(sc.dma_start(out=b2[0:1, :], in_=rstd[0:1, :]), "ln_bst", 16)
        sc.wait_ge(p.sem("ln_bst"), B["ln_bst"] + 32)
        p.inc(sc.dma_start(out=mb[:], in_=b1.broadcast_to((128, ttw))), "ln_bld", 16)
        p.inc(sc.dma_start(out=rb[:], in_=b2.broadcast_to((128, ttw))), "ln_bld", 16)

    @block.tensor
    def _(te):
        for s, v in te_wait:
            te.wait_ge(p.sem(s), v)
        for c in range(nch):
            te.wait_ge(p.sem("ln_xb"), B["ln_xb"] + c + 1)
            te.wait_ge(p.sem("ln_xq"), B["ln_xq"] + c + 1)
            te.matmul(ps_s[0:1, :], ones_bf[:, 0:1], xbf[:, c % 2],
                      start=(c == 0), stop=(c == nch - 1))
            ins = te.matmul(ps_q[0:1, :], ones_bf[:, 0:1], xsq[:, c % 2],
                            start=(c == 0), stop=(c == nch - 1))
            p.inc(ins, "ln_mm", 1)

    @block.vector
    def _(ve):
        for s, v in ve_wait:
            ve.wait_ge(p.sem(s), v)
        ve.wait_ge(p.sem("ln_sr"), B["ln_sr"] + 2)
        p.inc(ve.tensor_mul(r1[0:1, :], srow[0:1, :], srow[0:1, :]), "ln_m2", 1)
        ve.wait_ge(p.sem("ln_m2"), B["ln_m2"] + 1)
        p.inc(ve.tensor_sub(r2[0:1, :], sqrow[0:1, :], r1[0:1, :]), "ln_var", 1)
        ve.wait_ge(p.sem("ln_sd"), B["ln_sd"] + 1)
        p.inc(ve.reciprocal(rstd[0:1, :], sd[0:1, :]), "ln_rstd", 1)
        ve.wait_ge(p.sem("ln_bld"), B["ln_bld"] + 32)
        for c in range(nch):
            p.chain(ve, ve.tensor_sub(tmp[:, c % 2], xs[:, c], mb[:]), "ln_ch")
            if gains is None:
                ins = ve.tensor_mul(y[:, c], tmp[:, c % 2], rb[:])
            else:
                g_cm, b_cm = gains
                p.chain(ve, ve.tensor_mul(tmp2[:, c % 2], tmp[:, c % 2], rb[:]), "ln_ch")
                ins = ve.tensor_scalar(y[:, c], tmp2[:, c % 2], g_cm[:, c:c + 1],
                                       b_cm[:, c:c + 1], ALU.mult, ALU.add)
            p.inc(ins, "ln_y", 1)
    return p.cnt["ln_y"]



MG_B = 256
NGRP_B = F1 // MG_B
FCH = F1 // 128


def emit_l0ffn(p, x, q1T, q2T, q3T, xo, ones_bf, cst, b1, b2, ps, q3q=None,
               sw_jobs=(), q_gate=None):
    nc = p.nc
    p.pool_reset()
    B = p.snap()
    xv, xov = chunked(x), chunked(xo)
    q1v, q2v = chunked(q1T), chunked(q2T)
    q3v = chunked(q3T) if q3q is None else None

    with (nc.sbuf_tensor("ff_xs", [128, CD, 512], F32) as xs,
          nc.sbuf_tensor("ff_yb", [128, CD, 512], BF16) as yb,
          nc.sbuf_tensor("ff_tmp", [128, 2, 512], F32) as tmp,
          nc.sbuf_tensor("ff_xbf", [128, 2, 512], BF16) as xbf,
          nc.sbuf_tensor("ff_xsq", [128, 2, 512], BF16) as xsq,
          nc.sbuf_tensor("ff_mb", [128, 512], F32) as mb,
          nc.sbuf_tensor("ff_rb", [128, 512], F32) as rb,
          nc.sbuf_tensor("ff_rows", [1, 6, 512], F32) as rows,
          nc.sbuf_tensor("ff_w1s", [128, 2, CD, MG_B], BF16) as w1s,
          nc.sbuf_tensor("ff_w2s", [128, 2, CD, MG_B], BF16) as w2s,
          nc.sbuf_tensor("ff_hb", [128, FCH, 512], BF16) as hb,
          nc.sbuf_tensor("ff_sgt", [128, 2, 512], BF16) as sgt,
          nc.sbuf_tensor("ff_q3s", [128, 2, 4, 256], BF16) as q3s,
          nc.sbuf_tensor("ff_ob", [128, 8, 512], F32) as ob,
          nc.Block() as block):
        srow, sqrow, r1, r2, sd, rstd = (rows[:, i] for i in range(6))
        q3sf = q3s[:].rearrange("p h r c -> p h (r c)")

        if sw_jobs:
            @block.gpsimd
            def _(gp):
                for src, dst in sw_jobs:
                    p.inc(gp.dma_start(out=dst[:], in_=src[:]), "k_d2c", 16)

        @block.sync
        def _(sync):
            for tt in range(2):
                tsl = slice(tt * 512, (tt + 1) * 512)
                if tt > 0:
                    sync.wait_ge(p.sem("ff_obd"), B["ff_obd"] + 16 * tt)
                p.inc(sync.dma_start(out=xs[:], in_=xv[:, :, tsl]), "ff_lx", 16)
                for g in range(NGRP_B):
                    G = tt * NGRP_B + g
                    if G == 0 and q_gate is not None:
                        sync.wait_ge(p.sem(q_gate[0]), q_gate[1])
                    if G >= 2:
                        sync.wait_ge(p.sem("ff_mmh"), B["ff_mmh"] + 2 * (G - 1))
                    msl = slice(g * MG_B, (g + 1) * MG_B)
                    p.inc(sync.dma_start(out=w1s[:, G % 2], in_=q1v[:, :, msl]),
                          f"ff_lw1_{G % 2}", 16)
                    p.inc(sync.dma_start(out=w2s[:, G % 2], in_=q2v[:, :, msl]),
                          f"ff_lw2_{G % 2}", 16)
                for dg in range(2):
                    DG = tt * 2 + dg
                    for f in range(FCH):
                        FD = DG * FCH + f
                        if FD == 0 and q_gate is not None:
                            sync.wait_ge(p.sem(q_gate[0]), q_gate[2])
                        if FD >= 2:
                            sync.wait_ge(p.sem("ff_q3c"), B["ff_q3c"] + FD - 1)
                        dsl = slice(dg * 1024, (dg + 1) * 1024)
                        src = (q3v[:, f, dsl] if q3q is None
                               else q3q[:, f, dg * 4:(dg + 1) * 4, :])
                        dst = q3sf[:, FD % 2] if q3q is None else q3s[:, FD % 2]
                        p.inc(sync.dma_start(out=dst, in_=src),
                              f"ff_lq3_{FD % 2}", 16)

        yc = [0, 0]
        for tt in range(2):
            tsl = slice(tt * 512, (tt + 1) * 512)
            yc[tt] = emit_ln(p, block, xs, yb, 512, ones_bf, b1, b2, mb, rb,
                             tmp, None, xbf, xsq, ps[4][0:1, :], ps[5][0:1, :],
                             cst[0:1, 0:1], srow, sqrow, r1, r2, sd, rstd,
                             gp_wait=[("ff_lx", B["ff_lx"] + 16 * (tt + 1))],
                             sc_wait=[("ff_lx", B["ff_lx"] + 16 * (tt + 1))],
                             te_wait=[("ff_obd", B["ff_obd"] + 16 * tt)],
                             ve_wait=[("ff_q3c", B["ff_q3c"] + 2 * FCH * tt)],
                             prev=True)

            @block.tensor
            def _(te, tt=tt):
                for g in range(NGRP_B):
                    G = tt * NGRP_B + g
                    te.wait_ge(p.sem(f"ff_lw1_{G % 2}"), B[f"ff_lw1_{G % 2}"] + 16 * (G // 2 + 1))
                    te.wait_ge(p.sem(f"ff_lw2_{G % 2}"), B[f"ff_lw2_{G % 2}"] + 16 * (G // 2 + 1))
                    if g == 0:
                        te.wait_ge(p.sem("ln_y"), yc[tt])
                    for m in range(2):
                        MH = G * 2 + m
                        if MH >= 2:
                            te.wait_ge(p.sem("ff_hdone"), B["ff_hdone"] + MH - 1)
                        pa, pb_ = ps[2 * (MH % 2)], ps[2 * (MH % 2) + 1]
                        msl = slice(m * 128, (m + 1) * 128)
                        for c in range(CD):
                            te.matmul(pa[:], w1s[:, G % 2, c, msl], yb[:, c],
                                      start=(c == 0), stop=(c == CD - 1))
                        for c in range(CD):
                            ins = te.matmul(pb_[:], w2s[:, G % 2, c, msl], yb[:, c],
                                            start=(c == 0), stop=(c == CD - 1))
                        p.inc(ins, "ff_mmh", 1)

            @block.scalar
            def _(sc, tt=tt):
                for g in range(NGRP_B):
                    for m in range(2):
                        MH = (tt * NGRP_B + g) * 2 + m
                        sc.wait_ge(p.sem("ff_mmh"), B["ff_mmh"] + MH + 1)
                        if MH >= 2:
                            sc.wait_ge(p.sem("ff_hdone"), B["ff_hdone"] + MH - 1)
                        p.inc(sc.activation(sgt[:, MH % 2], ps[2 * (MH % 2)][:],
                                            AF.Sigmoid), "ff_sig", 1)

            @block.vector
            def _(ve, tt=tt):
                for g in range(NGRP_B):
                    for m in range(2):
                        MH = (tt * NGRP_B + g) * 2 + m
                        ve.wait_ge(p.sem("ff_sig"), B["ff_sig"] + MH + 1)
                        col = g * 2 + m
                        p.inc(ve.tensor_mul(hb[:, col], sgt[:, MH % 2],
                                            ps[2 * (MH % 2) + 1][:]), "ff_hdone", 1)

            @block.tensor
            def _(te, tt=tt):
                for dg in range(2):
                    DG = tt * 2 + dg
                    te.wait_ge(p.sem("ff_hdone"), B["ff_hdone"] + 44 * (tt + 1))
                    te.wait_ge(p.sem("ff_obd"), B["ff_obd"] + 8 * DG)
                    for f in range(FCH):
                        FD = DG * FCH + f
                        te.wait_ge(p.sem(f"ff_lq3_{FD % 2}"), B[f"ff_lq3_{FD % 2}"] + 16 * (FD // 2 + 1))
                        for dout in range(8):
                            ins = te.matmul(ps[dout][:],
                                            q3sf[:, FD % 2, dout * 128:(dout + 1) * 128],
                                            hb[:, f], start=(f == 0), stop=(f == FCH - 1))
                        p.inc(ins, "ff_q3c", 1)

            @block.vector
            def _(ve, tt=tt):
                for dg in range(2):
                    DG = tt * 2 + dg
                    ve.wait_ge(p.sem("ff_q3c"), B["ff_q3c"] + FCH * (DG + 1))
                    ve.wait_ge(p.sem("ff_obst"), B["ff_obst"] + 16 * DG)
                    for dout in range(8):
                        ch = dg * 8 + dout
                        p.inc(ve.scalar_tensor_tensor(ob[:, dout], ps[dout][:], 0.5,
                                                      xs[:, ch], ALU.mult, ALU.add),
                              "ff_obd", 1)

            @block.scalar
            def _(sc, tt=tt, tsl=tsl):
                for dg in range(2):
                    DG = tt * 2 + dg
                    sc.wait_ge(p.sem("ff_obd"), B["ff_obd"] + 8 * (DG + 1))
                    p.inc(sc.dma_start(out=xov[:, dg * 8:(dg + 1) * 8, tsl], in_=ob[:]),
                          "ff_obst", 16)

        @block.scalar
        def _(sc):
            sc.wait_ge(p.sem("ff_obst"), B["ff_obst"] + 16 * 4)


def _mk_consts(p):
    nc = p.nc
    ones_in = nc.dram_tensor("ones_bf", [128, 1], BF16, kind="ExternalInput").ap()
    cst_in = nc.dram_tensor("cstf", [1, 8], F32, kind="ExternalInput").ap()
    b1 = nc.dram_tensor("lnb1", [1, 512], F32).ap()
    b2 = nc.dram_tensor("lnb2", [1, 512], F32).ap()
    ones_bf = p.sbuf("ones_sb", [128, 1], BF16)
    cst = p.sbuf("cst_sb", [1, 8], F32)
    with nc.Block() as block:
        @block.sync
        def _(sync):
            p.inc(sync.dma_start(out=ones_bf[:], in_=ones_in[:]), "k_cst", 16)
            p.inc(sync.dma_start(out=cst[0:1, :], in_=cst_in[0:1, :]), "k_cst", 16)
    p.fence(["k_cst"])
    return ones_bf, cst, b1, b2


def build_l0ffn():
    p = P()
    nc = p.nc
    x = nc.dram_tensor("x", [D, T], F32, kind="ExternalInput").ap()
    q1T = nc.dram_tensor("q1T", [D, F1], BF16, kind="ExternalInput").ap()
    q2T = nc.dram_tensor("q2T", [D, F1], BF16, kind="ExternalInput").ap()
    q3T = nc.dram_tensor("q3T", [F1, D], BF16, kind="ExternalInput").ap()
    xo = nc.dram_tensor("xo", [D, T], F32, kind="ExternalOutput").ap()
    with p.stack:
        ones_bf, cst, b1, b2 = _mk_consts(p)
        ps = [p.psum(f"ps{i}", [128, 512]) for i in range(8)]
        emit_l0ffn(p, x, q1T, q2T, q3T, xo, ones_bf, cst, b1, b2, ps)
    return p.nc



ATT_KTS = {0: list(range(12)), 1: list(range(16))}


def emit_l0attn(p, x, xfh, wqkv, bqkv_cm, bqkv_row, wproj, bproj_cm, kbias_in,
                qbias_in, masks_in, ones2_in, xo, ones_bf, cst, b1, b2, ps,
                vdram, adram, xfh_gate=None):
    nc = p.nc
    p.pool_reset()
    B = p.snap()

    xv, xfv = chunked(x), chunked(xfh)
    wqv = chunked(wqkv)
    wpv = chunked(wproj)
    vdv = chunked(vdram)
    adv = chunked(adram)
    xov = chunked(xo)

    LKT = [len(ATT_KTS[hq % 2]) for hq in range(32)]
    S_BASE = [0]
    for hq in range(32):
        S_BASE.append(S_BASE[-1] + LKT[hq])
    MASK_IDX = {pos: i for i, pos in enumerate(MASK_POS)}
    MK = [sum(1 for kt in ATT_KTS[hq % 2] if (hq % 2, kt) in MASK_IDX) for hq in range(32)]
    M_BASE = [0]
    for hq in range(32):
        M_BASE.append(M_BASE[-1] + MK[hq])
    MG_OF, GM_LIST = {}, []
    for hq in range(32):
        qt_ = hq % 2
        for j, kt in enumerate(ATT_KTS[qt_]):
            if (qt_, kt) in MASK_IDX:
                MG_OF[(hq, j)] = len(GM_LIST)
                GM_LIST.append(S_BASE[hq] + j)

    with contextlib.ExitStack() as _ats:
        _sb = lambda n, s, d: _ats.enter_context(nc.sbuf_tensor(n, s, d))
        masks = _sb("at_masks", [128, 16, 512], BF16)
        kb = _sb("at_kb", [128, NH * 16], F32)
        ones2 = _sb("at_ones2", [2, 128], BF16)
        bq = _sb("at_bq", [128, 48], F32)
        bvr = _sb("at_bvr", [1, 2048], BF16)
        bp = _sb("at_bp", [128, CD], F32)
        xln_o = _sb("at_xln_o", [128, CD, T], BF16)
        xln_f = _sb("at_xln_f", [128, CD, T], BF16)
        rows = _sb("at_rows", [1, 6, 512], F32)
        srow, sqrow, r1, r2, sd, rstd = (rows[:, i] for i in range(6))

        NCON = 6
        with nc.sbuf_tensor("at_bqr", [1, 3 * D], F32) as bqr, nc.Block() as block:
            @block.sync
            def _(sync):
                for dst, srcv in ((masks[:], masks_in.rearrange("i p f -> p i f")),
                                  (kb[:], kbias_in[:]),
                                  (ones2[:], ones2_in[:]),
                                  (bq[:], bqkv_cm[:]), (bqr[0:1, :], bqkv_row[0:1, :]),
                                  (bp[:], bproj_cm[:])):
                    p.inc(sync.dma_start(out=dst, in_=srcv), "at_lcst", 16)

            @block.vector
            def _(ve):
                ve.wait_ge(p.sem("at_lcst"), B["at_lcst"] + 16 * NCON)
                p.inc(ve.tensor_copy(bvr[0:1, :], bqr[0:1, 4096:6144]), "at_bvrd", 1)

        with (nc.sbuf_tensor("xbuf", [128, CD, 512], F32) as xbuf,
              nc.sbuf_tensor("lntmp", [128, 2, 512], F32) as lntmp,
              nc.sbuf_tensor("lnxb", [128, 2, 512], BF16) as lnxb,
              nc.sbuf_tensor("lnxq", [128, 2, 512], BF16) as lnxq,
              nc.sbuf_tensor("lnmb", [128, 512], F32) as lnmb,
              nc.sbuf_tensor("lnrb", [128, 512], F32) as lnrb):
            passes = [(xv, xln_o, 0), (xv, xln_o, 1), (xfv, xln_f, 0), (xfv, xln_f, 1)]
            ycnt = []
            with nc.Block() as block:
                for k, (srcv, dstv, tt) in enumerate(passes):
                    tsl = slice(tt * 512, (tt + 1) * 512)

                    @block.sync
                    def _(sync, k=k, srcv=srcv, tsl=tsl):
                        if k > 0:
                            sync.wait_ge(p.sem("ln_y"), ycnt[k - 1])
                        if k == 2 and xfh_gate is not None:
                            sync.wait_ge(p.sem(xfh_gate[0]), xfh_gate[1])
                        p.inc(sync.dma_start(out=xbuf[:], in_=srcv[:, :, tsl]), "at_lx", 16)
                    ycnt.append(emit_ln(p, block, xbuf, dstv[:, :, tsl], 512, ones_bf,
                            b1, b2, lnmb, lnrb, lntmp, None, lnxb, lnxq,
                            ps[4][0:1, :], ps[5][0:1, :], cst[0:1, 0:1],
                            srow, sqrow, r1, r2, sd, rstd,
                            gp_wait=[("at_lx", B["at_lx"] + 16 * (k + 1))],
                            sc_wait=[("at_lx", B["at_lx"] + 16 * (k + 1)),
                                     ("at_lcst", B["at_lcst"] + 16 * NCON)],
                            te_wait=[("at_lcst", B["at_lcst"] + 16 * NCON)],
                            prev=True))

        with (nc.sbuf_tensor("wvs", [128, 2, CD, 512], BF16) as wvs,
              nc.sbuf_tensor("vt", [128, 2, 512], BF16) as vt):
            with nc.Block() as block:
                @block.gpsimd
                def _(gp):
                    gp.wait_ge(p.sem("ln_y"), ycnt[3])
                    for gn in range(4):
                        if gn >= 2:
                            gp.wait_ge(p.sem("at_vmm"), B["at_vmm"] + 16 * (gn - 1))
                        csl = slice(4096 + gn * 512, 4096 + (gn + 1) * 512)
                        p.inc(gp.dma_start(out=wvs[:, gn % 2], in_=wqv[:, :, csl]),
                              f"at_lwv{gn % 2}", 16)

                @block.tensor
                def _(te):
                    te.wait_ge(p.sem("ln_y"), ycnt[3])
                    te.wait_ge(p.sem("at_bvrd"), B["at_bvrd"] + 1)
                    vm = 0
                    for gn in range(4):
                        te.wait_ge(p.sem(f"at_lwv{gn % 2}"), B[f"at_lwv{gn % 2}"] + 16 * (gn // 2 + 1))
                        for s in range(16):
                            pb = ps[vm % 2]
                            if vm >= 2:
                                te.wait_ge(p.sem("at_vev"), B["at_vev"] + vm - 1)
                            xsrc = xln_f if s < 8 else xln_o
                            ssl = slice((s % 8) * 128, (s % 8 + 1) * 128)
                            for c in range(CD):
                                te.matmul(pb[:], xsrc[:, c, ssl], wvs[:, gn % 2, c, :],
                                          start=(c == 0), stop=False)
                            ins = te.matmul(pb[:], ones2[0:1, :],
                                            bvr[0:1, gn * 512:(gn + 1) * 512],
                                            start=False, stop=True)
                            p.inc(ins, "at_vmm", 1)
                            vm += 1

                @block.scalar
                def _(sc):
                    vm = 0
                    for gn in range(4):
                        for s in range(16):
                            sc.wait_ge(p.sem("at_vmm"), B["at_vmm"] + vm + 1)
                            if vm >= 2:
                                sc.wait_ge(p.sem(f"at_vst{vm % 2}"), B[f"at_vst{vm % 2}"] + 16 * (vm // 2))
                            p.inc(sc.activation(vt[:, vm % 2], ps[vm % 2][:], AF.Identity),
                                  "at_vev", 1)
                            sc.wait_ge(p.sem("at_vev"), B["at_vev"] + vm + 1)
                            p.inc(sc.dma_start(out=vdv[:, s, gn * 512:(gn + 1) * 512],
                                               in_=vt[:, vm % 2]), f"at_vst{vm % 2}", 16)
                            vm += 1

        with (nc.sbuf_tensor("wks", [128, 2, CD, 128], BF16) as wks,
              nc.sbuf_tensor("wqs", [128, 2, CD, 128], BF16) as wqs,
              nc.sbuf_tensor("kh", [128, 2, 2048], BF16) as kh,
              nc.sbuf_tensor("qh", [128, 2, 1024], BF16) as qh,
              nc.sbuf_tensor("vh", [128, 1, CD, 128], BF16) as vh,
              nc.sbuf_tensor("qb", [2, 2, 512], BF16) as qb,
              nc.sbuf_tensor("pt", [128, 2, 512], BF16) as pt,
              nc.sbuf_tensor("ptm", [128, 2, 512], BF16) as ptm,
              nc.sbuf_tensor("ptp", [128, 2, 512], F32) as ptp,
              nc.sbuf_tensor("ao", [128, 2, 512], BF16) as ao,
              nc.sbuf_tensor("avs", [128, 2, 512], F32) as avs,
              nc.sbuf_tensor("nrmb1", [1, 2, 512], BF16) as nrmb1,
              nc.sbuf_tensor("nrmb2", [1, 2, 512], BF16) as nrmb2):
            nrm = rows[:, 0:4]
            with nc.Block() as block:
                @block.gpsimd
                def _(gp):
                    gp.wait_ge(p.sem("at_vst0"), B["at_vst0"] + 16 * 32)
                    gp.wait_ge(p.sem("at_vst1"), B["at_vst1"] + 16 * 32)
                    for h in range(NH):
                        if h >= 2:
                            gp.wait_ge(p.sem("at_qev"), B["at_qev"] + 2 * (h - 1))
                        p.inc(gp.dma_start(out=wks[:, h % 2],
                                           in_=wqv[:, :, 2048 + h * 128:2048 + (h + 1) * 128]),
                              f"at_lwk{h % 2}", 16)
                        p.inc(gp.dma_start(out=wqs[:, h % 2],
                                           in_=wqv[:, :, h * 128:(h + 1) * 128]),
                              f"at_lwq{h % 2}", 16)

                @block.sync
                def _(sync):
                    sync.wait_ge(p.sem("at_vst0"), B["at_vst0"] + 16 * 32)
                    sync.wait_ge(p.sem("at_vst1"), B["at_vst1"] + 16 * 32)
                    for h in range(NH):
                        if h >= 1:
                            sync.wait_ge(p.sem("at_avmm"), B["at_avmm"] + S_BASE[2 * h])
                        p.inc(sync.dma_start(out=vh[:, 0],
                                             in_=vdv[:, :, h * 128:(h + 1) * 128]),
                              "at_lvh", 16)
                        for qt in range(2):
                            hq = h * 2 + qt
                            if hq >= 2:
                                sync.wait_ge(p.sem("at_smm"), B["at_smm"] + S_BASE[hq - 1])
                            p.inc(sync.dma_start(out=qb[:, hq % 2],
                                                 in_=qbias_in[h, qt]),
                                f"at_lqb{hq % 2}", 16)

                @block.tensor
                def _(te):
                    for h in range(NH):
                        te.wait_ge(p.sem(f"at_lwk{h % 2}"), B[f"at_lwk{h % 2}"] + 16 * (h // 2 + 1))
                        if h >= 2:
                            te.wait_ge(p.sem("at_smm"), B["at_smm"] + S_BASE[2 * (h - 1)])
                        for st in range(4):
                            if st >= 2:
                                te.wait_ge(p.sem("at_kev"), B["at_kev"] + 4 * h + st - 1)
                            xsrc = xln_f if st < 2 else xln_o
                            tslq = slice((st % 2) * 512, (st % 2 + 1) * 512)
                            for c in range(CD):
                                ins = te.matmul(ps[2 + st % 2][:], wks[:, h % 2, c, :],
                                                xsrc[:, c, tslq], start=(c == 0),
                                                stop=(c == CD - 1))
                            p.inc(ins, "at_kmm", 1)
                        te.wait_ge(p.sem(f"at_lwq{h % 2}"), B[f"at_lwq{h % 2}"] + 16 * (h // 2 + 1))
                        te.wait_ge(p.sem("at_kev"), B["at_kev"] + 4 * (h + 1))
                        for qt in range(2):
                            for c in range(CD):
                                ins = te.matmul(ps[2 + qt][:], wqs[:, h % 2, c, :],
                                                xln_o[:, c, qt * 512:(qt + 1) * 512],
                                                start=(c == 0), stop=(c == CD - 1))
                            p.inc(ins, "at_qmm", 1)
                        te.wait_ge(p.sem("at_lvh"), B["at_lvh"] + 16 * (h + 1))
                        te.wait_ge(p.sem("at_qev"), B["at_qev"] + 2 * (h + 1))
                        for qt in range(2):
                            hq = h * 2 + qt
                            te.wait_ge(p.sem(f"at_lqb{hq % 2}"), B[f"at_lqb{hq % 2}"] + 16 * (hq // 2 + 1))
                            if hq > 0:
                                te.wait_ge(p.sem("at_avev"), B["at_avev"] + hq)
                                te.wait_ge(p.sem("at_sev"), B["at_sev"] + hq)
                                te.wait_ge(p.sem("at_mdone"), B["at_mdone"] + M_BASE[hq])
                            kts = ATT_KTS[qt]
                            for j, kt in enumerate(kts):
                                pb = ps[j % 2]
                                if j >= 2:
                                    if (qt, kts[j - 2]) in MASK_IDX:
                                        te.wait_ge(p.sem("at_mdone"), B["at_mdone"] + MG_OF[(hq, j - 2)] + 1)
                                    else:
                                        te.wait_ge(p.sem("at_expdone"), B["at_expdone"] + S_BASE[hq] + j - 1)
                                te.matmul(pb[:], kh[:, h % 2, kt * 128:(kt + 1) * 128],
                                          qh[:, h % 2, qt * 512:(qt + 1) * 512],
                                          start=True, stop=False)
                                ins = te.matmul(pb[:], ones2[:, :],
                                                qb[:, hq % 2], start=False, stop=True)
                                p.inc(ins, "at_smm", 1)
                                masked = (qt, kt) in MASK_IDX
                                te.wait_ge(p.sem("at_expdone"), B["at_expdone"] + S_BASE[hq] + j + 1)
                                psrc = ptm if masked else pt
                                te.matmul(ps[4][:], vh[:, 0, kt, :], psrc[:, j % 2],
                                          start=(j == 0), stop=(j == len(kts) - 1))
                                ins = te.matmul(ps[5][0:1, :], ones_bf[:, 0:1],
                                                psrc[:, j % 2],
                                                start=(j == 0), stop=(j == len(kts) - 1))
                                p.inc(ins, "at_avmm", 1)
                            if hq >= 1:
                                q_ = hq - 1
                                te.wait_ge(p.sem("at_hilo"), B["at_hilo"] + q_ + 1)
                                if q_ >= 2:
                                    te.wait_ge(p.sem("at_avdone"), B["at_avdone"] + q_ - 1)
                                te.matmul(ps[6 + q_ % 2][:], ones2[0:1, :], nrmb1[0:1, q_ % 2],
                                          start=True, stop=False)
                                p.inc(te.matmul(ps[6 + q_ % 2][:], ones2[0:1, :], nrmb2[0:1, q_ % 2],
                                                start=False, stop=True), "at_bcmm", 1)
                    q_ = 2 * NH - 1
                    te.wait_ge(p.sem("at_hilo"), B["at_hilo"] + q_ + 1)
                    te.wait_ge(p.sem("at_avdone"), B["at_avdone"] + q_ - 1)
                    te.matmul(ps[6 + q_ % 2][:], ones2[0:1, :], nrmb1[0:1, q_ % 2],
                              start=True, stop=False)
                    p.inc(te.matmul(ps[6 + q_ % 2][:], ones2[0:1, :], nrmb2[0:1, q_ % 2],
                                    start=False, stop=True), "at_bcmm", 1)

                @block.scalar
                def _(sc):
                    for h in range(NH):
                        for st in range(4):
                            sc.wait_ge(p.sem("at_kmm"), B["at_kmm"] + 4 * h + st + 1)
                            p.inc(sc.activation(kh[:, h % 2, st * 512:(st + 1) * 512],
                                                ps[2 + st % 2][:], AF.Identity,
                                                bias=bq[:, 16 + h:17 + h]), "at_kev", 1)
                        for qt in range(2):
                            sc.wait_ge(p.sem("at_qmm"), B["at_qmm"] + 2 * h + qt + 1)
                            p.inc(sc.activation(qh[:, h % 2, qt * 512:(qt + 1) * 512],
                                                ps[2 + qt][:], AF.Identity,
                                                bias=bq[:, h:h + 1]), "at_qev", 1)
                        for qt in range(2):
                            hq = h * 2 + qt
                            kts = ATT_KTS[qt]
                            for j, kt in enumerate(kts):
                                g = S_BASE[hq] + j
                                if g >= 2:
                                    sc.wait_ge(p.sem("at_avmm"), B["at_avmm"] + g - 1)
                                if (qt, kt) in MASK_IDX:
                                    mg = MG_OF[(hq, j)]
                                    sc.wait_ge(p.sem("at_mdone"), B["at_mdone"] + mg + 1)
                                    p.inc(sc.activation(ptm[:, j % 2], ptp[:, mg % 2], AF.Exp,
                                                        scale=1.0 / 128.0,
                                                        bias=kb[:, h * 16 + kt:h * 16 + kt + 1]),
                                          "at_expdone", 1)
                                else:
                                    sc.wait_ge(p.sem("at_smm"), B["at_smm"] + g + 1)
                                    p.inc(sc.activation(pt[:, j % 2], ps[j % 2][:], AF.Exp,
                                                        scale=1.0 / 128.0,
                                                        bias=kb[:, h * 16 + kt:h * 16 + kt + 1]),
                                          "at_expdone", 1)
                            sc.wait_ge(p.sem("at_avmm"), B["at_avmm"] + S_BASE[hq + 1])
                            sc.wait_ge(p.sem("at_hilo"), B["at_hilo"] + hq)
                            p.inc(sc.activation(nrm[0:1, 0, :], ps[5][0:1, :], AF.Identity),
                                  "at_sev", 1)
                            if hq >= 1:
                                sc.wait_ge(p.sem("at_avdone"), B["at_avdone"] + hq - 1)
                            p.inc(sc.activation(avs[:, hq % 2], ps[4][:], AF.Identity), "at_avev", 1)
                            if hq >= 1:
                                sc.wait_ge(p.sem("at_avdone"), B["at_avdone"] + hq)
                                ph, pq = (hq - 1) // 2, (hq - 1) % 2
                                p.inc(sc.dma_start(
                                    out=adv[:, ph, pq * 512:(pq + 1) * 512],
                                    in_=ao[:, (hq - 1) % 2]), f"at_ast{(hq - 1) % 2}", 16)
                    sc.wait_ge(p.sem("at_avdone"), B["at_avdone"] + 32)
                    p.inc(sc.dma_start(out=adv[:, 15, 512:1024], in_=ao[:, 1]), "at_ast1", 16)

                @block.vector
                def _(ve):
                    for h in range(NH):
                        for qt in range(2):
                            hq = h * 2 + qt
                            kts = ATT_KTS[qt]
                            for j, kt in enumerate(kts):
                                if (qt, kt) in MASK_IDX:
                                    g = S_BASE[hq] + j
                                    mg = MG_OF[(hq, j)]
                                    ve.wait_ge(p.sem("at_smm"), B["at_smm"] + g + 1)
                                    if mg >= 2:
                                        ve.wait_ge(p.sem("at_expdone"),
                                                   B["at_expdone"] + GM_LIST[mg - 2] + 1)
                                    p.inc(ve.tensor_tensor(ptp[:, mg % 2], ps[j % 2][:],
                                                           masks[:, MASK_IDX[(qt, kt)], :],
                                                           ALU.min), "at_mdone", 1)
                            ve.wait_ge(p.sem("at_sev"), B["at_sev"] + hq + 1)
                            p.chain(ve, ve.reciprocal(nrm[0:1, 1, :], nrm[0:1, 0, :]), "at_mch")
                            p.chain(ve, ve.tensor_copy(nrmb1[0:1, hq % 2, :], nrm[0:1, 1, :]),
                                    "at_mch")
                            p.chain(ve, ve.tensor_copy(nrm[0:1, 2, :], nrmb1[0:1, hq % 2, :]),
                                    "at_mch")
                            p.chain(ve, ve.tensor_sub(nrm[0:1, 3, :], nrm[0:1, 1, :],
                                                      nrm[0:1, 2, :]), "at_mch")
                            p.inc(ve.tensor_copy(nrmb2[0:1, hq % 2, :], nrm[0:1, 3, :]),
                                  "at_hilo", 1)
                            if hq >= 1:
                                q2_ = hq - 1
                                ve.wait_ge(p.sem("at_avev"), B["at_avev"] + q2_ + 1)
                                ve.wait_ge(p.sem("at_bcmm"), B["at_bcmm"] + q2_ + 1)
                                if q2_ >= 2:
                                    ve.wait_ge(p.sem(f"at_ast{q2_ % 2}"), B[f"at_ast{q2_ % 2}"] + 16 * (q2_ // 2))
                                p.inc(ve.tensor_tensor(ao[:, q2_ % 2], avs[:, q2_ % 2], ps[6 + q2_ % 2][:],
                                                       ALU.mult), "at_avdone", 1)
                    q2_ = 2 * NH - 1
                    ve.wait_ge(p.sem("at_avev"), B["at_avev"] + q2_ + 1)
                    ve.wait_ge(p.sem("at_bcmm"), B["at_bcmm"] + q2_ + 1)
                    ve.wait_ge(p.sem(f"at_ast{q2_ % 2}"), B[f"at_ast{q2_ % 2}"] + 16 * (q2_ // 2))
                    p.inc(ve.tensor_tensor(ao[:, q2_ % 2], avs[:, q2_ % 2], ps[6 + q2_ % 2][:],
                                           ALU.mult), "at_avdone", 1)

        with (nc.sbuf_tensor("ats", [128, CD, T], BF16) as ats,
              nc.sbuf_tensor("wps", [128, 2, CD, 128], BF16) as wps,
              nc.sbuf_tensor("xt", [128, 2, 512], F32) as xt,
              nc.sbuf_tensor("prt", [128, 2, 512], F32) as prt,
              nc.sbuf_tensor("obp", [128, 2, 512], F32) as obp):
            with nc.Block() as block:
                @block.sync
                def _(sync):
                    sync.wait_ge(p.sem("at_ast0"), B["at_ast0"] + 16 * 16)
                    sync.wait_ge(p.sem("at_ast1"), B["at_ast1"] + 16 * 16)
                    p.inc(sync.dma_start(out=ats[:], in_=adv[:]), "at_lat", 16)
                    for i in range(32):
                        dout, qt = i // 2, i % 2
                        if i >= 2:
                            sync.wait_ge(p.sem("at_prdone"), B["at_prdone"] + i - 1)
                        p.inc(sync.dma_start(out=xt[:, i % 2],
                                             in_=xv[:, dout, qt * 512:(qt + 1) * 512]),
                              f"at_lxt{i % 2}", 16)

                @block.gpsimd
                def _(gp):
                    gp.wait_ge(p.sem("at_ast0"), B["at_ast0"] + 16 * 16)
                    gp.wait_ge(p.sem("at_ast1"), B["at_ast1"] + 16 * 16)
                    for dout in range(CD):
                        if dout >= 2:
                            gp.wait_ge(p.sem("at_pjmm"), B["at_pjmm"] + 2 * (dout - 1))
                        p.inc(gp.dma_start(out=wps[:, dout % 2],
                                           in_=wpv[:, :, dout * 128:(dout + 1) * 128]),
                              f"at_lwp{dout % 2}", 16)

                @block.tensor
                def _(te):
                    te.wait_ge(p.sem("at_lat"), B["at_lat"] + 16)
                    for i in range(32):
                        dout, qt = i // 2, i % 2
                        te.wait_ge(p.sem(f"at_lwp{dout % 2}"), B[f"at_lwp{dout % 2}"] + 16 * (dout // 2 + 1))
                        if i >= 2:
                            te.wait_ge(p.sem("at_prv"), B["at_prv"] + i - 1)
                        for c in range(CD):
                            ins = te.matmul(ps[i % 2][:], wps[:, dout % 2, c, :],
                                            ats[:, c, qt * 512:(qt + 1) * 512],
                                            start=(c == 0), stop=(c == CD - 1))
                        p.inc(ins, "at_pjmm", 1)

                @block.scalar
                def _(sc):
                    for i in range(32):
                        dout = i // 2
                        sc.wait_ge(p.sem("at_pjmm"), B["at_pjmm"] + i + 1)
                        if i >= 2:
                            sc.wait_ge(p.sem("at_prdone"), B["at_prdone"] + i - 1)
                        p.inc(sc.activation(prt[:, i % 2], ps[i % 2][:], AF.Identity,
                                            bias=bp[:, dout:dout + 1]), "at_prv", 1)
                        if i >= 1:
                            sc.wait_ge(p.sem("at_prdone"), B["at_prdone"] + i)
                            pd, pq = (i - 1) // 2, (i - 1) % 2
                            p.inc(sc.dma_start(out=xov[:, pd, pq * 512:(pq + 1) * 512],
                                               in_=obp[:, (i - 1) % 2]),
                                  f"at_obst{(i - 1) % 2}", 16)
                    sc.wait_ge(p.sem("at_prdone"), B["at_prdone"] + 32)
                    p.inc(sc.dma_start(out=xov[:, 15, 512:1024], in_=obp[:, 1]), "at_obst1", 16)
                    sc.wait_ge(p.sem("at_obst0"), B["at_obst0"] + 16 * 16)
                    sc.wait_ge(p.sem("at_obst1"), B["at_obst1"] + 16 * 16)

                @block.vector
                def _(ve):
                    for i in range(32):
                        ve.wait_ge(p.sem("at_prv"), B["at_prv"] + i + 1)
                        ve.wait_ge(p.sem(f"at_lxt{i % 2}"), B[f"at_lxt{i % 2}"] + 16 * (i // 2 + 1))
                        if i >= 2:
                            ve.wait_ge(p.sem(f"at_obst{i % 2}"), B[f"at_obst{i % 2}"] + 16 * (i // 2))
                        p.inc(ve.tensor_add(obp[:, i % 2], prt[:, i % 2], xt[:, i % 2]),
                              "at_prdone", 1)



def build_l0attn():
    p = P()
    nc = p.nc
    x = nc.dram_tensor("x", [D, T], F32, kind="ExternalInput").ap()
    xfh = nc.dram_tensor("xfh", [D, T], F32, kind="ExternalInput").ap()
    wqkv = nc.dram_tensor("wqkv", [D, 3 * D], F32, kind="ExternalInput").ap()
    bqkv_cm = nc.dram_tensor("bqkv_cm", [128, 48], F32, kind="ExternalInput").ap()
    bqkv_row = nc.dram_tensor("bqkv_row", [1, 3 * D], F32, kind="ExternalInput").ap()
    wproj = nc.dram_tensor("wproj", [D, D], F32, kind="ExternalInput").ap()
    bproj_cm = nc.dram_tensor("bproj_cm", [128, CD], F32, kind="ExternalInput").ap()
    kbias_in = nc.dram_tensor("kbias", [128, NH * 16], F32, kind="ExternalInput").ap()
    qbias_in = nc.dram_tensor("qbias", [NH, 2, 2, 512], BF16, kind="ExternalInput").ap()
    masks_in = nc.dram_tensor("masks", [16, 128, 512], BF16, kind="ExternalInput").ap()
    ones2_in = nc.dram_tensor("ones2", [2, 128], BF16, kind="ExternalInput").ap()
    xo = nc.dram_tensor("xo", [D, T], F32, kind="ExternalOutput").ap()
    vdram = nc.dram_tensor("vdram", [2048, D], BF16).ap()
    adram = nc.dram_tensor("adram", [D, T], BF16).ap()
    with p.stack:
        ones_bf, cst, b1, b2 = _mk_consts(p)
        ps = [p.psum(f"ps{i}", [128, 512]) for i in range(8)]
        emit_l0attn(p, x, xfh, wqkv, bqkv_cm, bqkv_row, wproj, bproj_cm, kbias_in,
                    qbias_in, masks_in, ones2_in, xo, ones_bf, cst, b1, b2, ps,
                    vdram, adram)
    return p.nc


def _attn_inputs(inp, x_full, kb_pc, qb_pc, masks_pc):
    wqkv = np.ascontiguousarray(np.asarray(inp["L0_attn_w_qkv"], np.float32))
    bq = np.asarray(inp["L0_attn_b_qkv"], np.float32)
    wproj = np.ascontiguousarray(np.asarray(inp["L0_attn_w_proj"], np.float32))
    bp = np.asarray(inp["L0_attn_b_proj"], np.float32)
    ones_bf = np.ones((128, 1), BF)
    ones2 = np.ones((2, 128), BF)
    cstf = np.zeros((1, 8), np.float32)
    cstf[0, 0] = LN_EPS
    cstf[0, 1] = LN_EPS_IN
    maps = []
    for c in range(NC):
        even = c - (c % 2)
        maps.append({
            "x": _fm_shard(x_full, c), "xfh": _fm_shard(x_full, even),
            "wqkv": wqkv, "bqkv_cm": _cm(bq), "bqkv_row": bq[None, :].astype(np.float32),
            "wproj": wproj, "bproj_cm": _cm(bp),
            "kbias": kb_pc[c], "qbias": qb_pc[c], "masks": masks_pc[c],
            "ones_bf": ones_bf, "ones2": ones2, "cstf": cstf,
        })
    return maps



FE = 4096
FECH = FE // 128


def emit_l1ffn(p, x, wg, wa, wb, ba_cm, bb_cm, xo, ones_bf, cst, b1, b2, ps,
               mb1, mb2, w_gate=None):
    nc = p.nc
    p.pool_reset()
    B = p.snap()

    xv = chunked(x)
    wgv = chunked(wg)
    wav = [chunked(w) for w in wa]
    wbv = [chunked(w) for w in wb]
    xov = chunked(xo)
    ACTS = [AF.Gelu, AF.Relu]

    with contextlib.ExitStack() as _mos:
        def _sb(n, s, d):
            return _mos.enter_context(nc.sbuf_tensor(n, s, d))
        wgs = _sb("mo_wgs", [128, CD, 2], BF16)
        bas = [_sb(f"mo_bas{e}", [128, FECH], F32) for e in range(2)]
        bbs = [_sb(f"mo_bbs{e}", [128, CD], F32) for e in range(2)]
        xs = _sb("mo_xs", [128, 2, CD, 512], F32)
        yb = _sb("mo_yb", [128, CD, 512], BF16)
        tmp = _sb("mo_tmp", [128, 2, 512], F32)
        xbf = _sb("mo_xbf", [128, 2, 512], BF16)
        xsq = _sb("mo_xsq", [128, 2, 512], BF16)
        mb = _sb("mo_mbx", [128, 512], F32)
        rb = _sb("mo_rbx", [128, 512], F32)
        rows = _sb("mo_rows", [1, 6, 512], F32)
        grow = _sb("mo_grow", [1, 5, 512], F32)
        m0b = _sb("mo_m0b", [128, 512], F32)
        m1b = _sb("mo_m1b", [128, 512], F32)
        was = _sb("mo_was", [128, 2, CD, 512], BF16)
        wbs = _sb("mo_wbs", [128, 2, 1024], BF16)
        hb = _sb("mo_hb", [128, FECH, 512], BF16)
        prt = _sb("mo_prt", [128, 2, 512], F32)
        act = _sb("mo_act", [128, 2, 512], F32)
        ob = _sb("mo_ob", [128, 2, 512], F32)
        srow, sqrow, r1, r2, sd, rstd = (rows[:, i] for i in range(6))

        with nc.Block() as block:
            @block.sync
            def _(sync):
                for dst, srcv in ((bas[0][:], ba_cm[0][:]), (bas[1][:], ba_cm[1][:]),
                                  (bbs[0][:], bb_cm[0][:]), (bbs[1][:], bb_cm[1][:])):
                    p.inc(sync.dma_start(out=dst, in_=srcv), "mo_lcst", 16)

            @block.gpsimd
            def _(gp):
                p.inc(gp.dma_start(out=wgs[:], in_=wgv[:]), "mo_lwg", 16)

            ycnt = []
            for tt in range(2):
                tsl = slice(tt * 512, (tt + 1) * 512)

                @block.sync
                def _(sync, tt=tt):
                    if tt > 0:
                        sync.wait_ge(p.sem("mo_obst0"), B["mo_obst0"] + 16 * 8 * tt)
                        sync.wait_ge(p.sem("mo_obst1"), B["mo_obst1"] + 16 * 8 * tt)
                    p.inc(sync.dma_start(out=xs[:, 0], in_=xv[:, :, tsl]), "mo_lx", 16)
                ycnt.append(emit_ln(p, block, xs[:, 0], yb, 512, ones_bf, b1, b2,
                        mb, rb, tmp, None, xbf, xsq, ps[4][0:1, :], ps[5][0:1, :],
                        cst[0:1, 0:1], srow, sqrow, r1, r2, sd, rstd,
                        gp_wait=[("mo_lx", B["mo_lx"] + 16 * (tt + 1))],
                        sc_wait=[("mo_lx", B["mo_lx"] + 16 * (tt + 1)),
                                 ("mo_lcst", B["mo_lcst"] + 16 * 4)],
                        te_wait=[("mo_lcst", B["mo_lcst"] + 16 * 4),
                                 ("mo_lwg", B["mo_lwg"] + 16),
                                 ("mo_prv", B["mo_prv"] + 32 * tt)],
                        prev=True))

                @block.tensor
                def _(te, tt=tt, yc=ycnt[tt]):
                    te.wait_ge(p.sem("ln_y"), yc)
                    for e in range(2):
                        for c in range(CD):
                            ins = te.matmul(ps[6 + e][0:1, :], wgs[:, c, e:e + 1],
                                            yb[:, c], start=(c == 0), stop=(c == CD - 1))
                        p.inc(ins, "mo_gmm", 1)

                @block.scalar
                def _(sc, tt=tt):
                    sc.wait_ge(p.sem("mo_gmm"), B["mo_gmm"] + 2 * (tt + 1))
                    if tt > 0:
                        sc.wait_ge(p.sem("mo_mbld"), B["mo_mbld"] + 32 * tt)
                    p.inc(sc.activation(grow[0:1, 0, :], ps[6][0:1, :], AF.Identity),
                          "mo_gev", 1)
                    p.inc(sc.activation(grow[0:1, 1, :], ps[7][0:1, :], AF.Identity),
                          "mo_gev", 1)
                    sc.wait_ge(p.sem("mo_gmask"), B["mo_gmask"] + tt + 1)
                    p.inc(sc.dma_start(out=mb1[0:1, :], in_=grow[0:1, 3, :]), "mo_mbst", 16)
                    p.inc(sc.dma_start(out=mb2[0:1, :], in_=grow[0:1, 4, :]), "mo_mbst", 16)
                    sc.wait_ge(p.sem("mo_mbst"), B["mo_mbst"] + 32 * (tt + 1))
                    p.inc(sc.dma_start(out=m0b[:], in_=mb1.broadcast_to((128, 512))), "mo_mbld", 16)
                    p.inc(sc.dma_start(out=m1b[:], in_=mb2.broadcast_to((128, 512))), "mo_mbld", 16)

                @block.vector
                def _(ve, tt=tt):
                    ve.wait_ge(p.sem("mo_gev"), B["mo_gev"] + 2 * (tt + 1))
                    p.chain(ve, ve.tensor_tensor(grow[0:1, 2, :], grow[0:1, 0, :],
                                                 grow[0:1, 1, :], ALU.is_ge), "vch")
                    p.chain(ve, ve.tensor_scalar_mul(grow[0:1, 3, :], grow[0:1, 2, :], 0.5),
                            "vch")
                    p.inc(ve.tensor_scalar(grow[0:1, 4, :], grow[0:1, 2, :], -0.5, 0.5,
                                           ALU.mult, ALU.add), "mo_gmask", 1)

                for e in range(2):
                    @block.sync
                    def _(sync, tt=tt, e=e):
                        for g in range(8):
                            G = (tt * 2 + e) * 8 + g
                            if G == 0 and w_gate is not None:
                                sync.wait_ge(p.sem(w_gate[0]), w_gate[1])
                            if G >= 2:
                                sync.wait_ge(p.sem("mo_hmm"), B["mo_hmm"] + 4 * (G - 1))
                            msl = slice(g * 512, (g + 1) * 512)
                            p.inc(sync.dma_start(out=was[:, G % 2], in_=wav[e][:, :, msl]),
                                  f"mo_lwa{G % 2}", 16)
                        for dg in range(2):
                            for f in range(FECH):
                                FD = ((tt * 2 + e) * 2 + dg) * FECH + f
                                if FD >= 2:
                                    sync.wait_ge(p.sem("mo_bmm"), B["mo_bmm"] + FD - 1)
                                dsl = slice(dg * 1024, (dg + 1) * 1024)
                                p.inc(sync.dma_start(out=wbs[:, FD % 2], in_=wbv[e][:, f, dsl]),
                                      f"mo_lwb{FD % 2}", 16)

                    @block.tensor
                    def _(te, tt=tt, e=e):
                        for g in range(8):
                            G = (tt * 2 + e) * 8 + g
                            te.wait_ge(p.sem(f"mo_lwa{G % 2}"), B[f"mo_lwa{G % 2}"] + 16 * (G // 2 + 1))
                            if g == 0:
                                te.wait_ge(p.sem("ln_y"), ycnt[tt])
                                te.wait_ge(p.sem("mo_prv"), B["mo_prv"] + 16 * (tt * 2 + e))
                            for m in range(4):
                                MH = G * 4 + m
                                if MH >= 4:
                                    te.wait_ge(p.sem("mo_hev"), B["mo_hev"] + MH - 3)
                                pb = ps[MH % 4]
                                msl = slice(m * 128, (m + 1) * 128)
                                for c in range(CD):
                                    ins = te.matmul(pb[:], was[:, G % 2, c, msl], yb[:, c],
                                                    start=(c == 0), stop=(c == CD - 1))
                                p.inc(ins, "mo_hmm", 1)
                        for dg in range(2):
                            DG = (tt * 2 + e) * 2 + dg
                            te.wait_ge(p.sem("mo_hev"), B["mo_hev"] + 32 * (tt * 2 + e + 1))
                            te.wait_ge(p.sem("mo_prv"), B["mo_prv"] + 8 * DG)
                            for f in range(FECH):
                                FD = DG * FECH + f
                                te.wait_ge(p.sem(f"mo_lwb{FD % 2}"), B[f"mo_lwb{FD % 2}"] + 16 * (FD // 2 + 1))
                                for dout in range(8):
                                    ins = te.matmul(ps[dout][:],
                                                    wbs[:, FD % 2, dout * 128:(dout + 1) * 128],
                                                    hb[:, f], start=(f == 0),
                                                    stop=(f == FECH - 1))
                                p.inc(ins, "mo_bmm", 1)

                    @block.scalar
                    def _(sc, tt=tt, e=e):
                        for g in range(8):
                            for m in range(4):
                                MH = ((tt * 2 + e) * 8 + g) * 4 + m
                                sc.wait_ge(p.sem("mo_hmm"), B["mo_hmm"] + MH + 1)
                                col = g * 4 + m
                                p.inc(sc.activation(hb[:, col], ps[MH % 4][:], ACTS[e],
                                                    bias=bas[e][:, col:col + 1]), "mo_hev", 1)
                        for dg in range(2):
                            DG = (tt * 2 + e) * 2 + dg
                            for dout in range(8):
                                il = dg * 8 + dout
                                i = DG * 8 + dout
                                gi = 16 * (tt * 2 + e) + il
                                sc.wait_ge(p.sem("mo_bmm"), B["mo_bmm"] + (DG + 1) * FECH)
                                if gi >= 2:
                                    sc.wait_ge(p.sem("mo_mdn"), B["mo_mdn"] + gi - 1)
                                ch = dg * 8 + dout
                                p.inc(sc.activation(prt[:, i % 2], ps[dout][:], AF.Identity,
                                                    bias=bbs[e][:, ch:ch + 1]), "mo_prv", 1)
                                if e == 1 and il >= 1:
                                    sc.wait_ge(p.sem("mo_acc"), B["mo_acc"] + 32 * tt + 16 + il)
                                    pch = il - 1
                                    p.inc(sc.dma_start(out=xov[:, pch, tsl],
                                                       in_=ob[:, (il - 1) % 2]),
                                          f"mo_obst{(il - 1) % 2}", 16)
                        if e == 1:
                            sc.wait_ge(p.sem("mo_acc"), B["mo_acc"] + 32 * tt + 32)
                            p.inc(sc.dma_start(out=xov[:, 15, tsl], in_=ob[:, 1]),
                                  "mo_obst1", 16)

                    @block.vector
                    def _(ve, tt=tt, e=e):
                        if e == 1:
                            ve.wait_ge(p.sem("mo_acc"), B["mo_acc"] + 32 * tt + 16)
                        for dg in range(2):
                            for dout in range(8):
                                il = dg * 8 + dout
                                i = ((tt * 2 + e) * 2 + dg) * 8 + dout
                                gi = 16 * (tt * 2 + e) + il
                                ch = dg * 8 + dout
                                ve.wait_ge(p.sem("mo_prv"), B["mo_prv"] + 16 * (tt * 2 + e) + il + 1)
                                if e == 0 and il == 0:
                                    ve.wait_ge(p.sem("mo_mbld"), B["mo_mbld"] + 32 * (tt + 1))
                                mk = m0b if e == 0 else m1b
                                p.chain(ve, ve.tensor_mul(act[:, il % 2], prt[:, il % 2],
                                                          mk[:]), "mo_mdn")
                                if e == 0:
                                    ins = ve.tensor_add(xs[:, 1, ch, :], xs[:, 0, ch, :],
                                                        act[:, il % 2])
                                    p.inc(ins, "mo_acc", 1)
                                else:
                                    if 8 * tt + il // 2 >= 1:
                                        ve.wait_ge(p.sem(f"mo_obst{il % 2}"),
                                                   B[f"mo_obst{il % 2}"] + 16 * (8 * tt + il // 2))
                                    ins = ve.tensor_add(ob[:, il % 2], xs[:, 1, ch, :],
                                                        act[:, il % 2])
                                    p.inc(ins, "mo_acc", 1)



            @block.scalar
            def _(sc):
                sc.wait_ge(p.sem("mo_obst0"), B["mo_obst0"] + 16 * 16)
                sc.wait_ge(p.sem("mo_obst1"), B["mo_obst1"] + 16 * 16)


def build_l1ffn():
    p = P()
    nc = p.nc
    x = nc.dram_tensor("x", [D, T], F32, kind="ExternalInput").ap()
    wg = nc.dram_tensor("wgT", [D, 2], F32, kind="ExternalInput").ap()
    wa = [nc.dram_tensor(f"w{e}aT", [D, FE], BF16, kind="ExternalInput").ap() for e in range(2)]
    wb = [nc.dram_tensor(f"w{e}bT", [FE, D], BF16, kind="ExternalInput").ap() for e in range(2)]
    ba_cm = [nc.dram_tensor(f"b{e}a_cm", [128, FECH], F32, kind="ExternalInput").ap() for e in range(2)]
    bb_cm = [nc.dram_tensor(f"b{e}b_cm", [128, CD], F32, kind="ExternalInput").ap() for e in range(2)]
    xo = nc.dram_tensor("xo", [D, T], F32, kind="ExternalOutput").ap()
    mb1 = nc.dram_tensor("mb1", [1, 512], F32).ap()
    mb2 = nc.dram_tensor("mb2", [1, 512], F32).ap()
    with p.stack:
        ones_bf, cst, b1, b2 = _mk_consts(p)
        ps = [p.psum(f"ps{i}", [128, 512]) for i in range(8)]
        emit_l1ffn(p, x, wg, wa, wb, ba_cm, bb_cm, xo, ones_bf, cst, b1, b2, ps,
                   mb1, mb2)
    return p.nc



def emit_l1at1(p, x, wv_in, g1, bt1, gt, btt, td, rev, yo, xlno, vmpo,
               ones_bf, cst, b1, b2, ps, rvb):
    nc = p.nc
    p.pool_reset()
    B = p.snap()

    xv, yv, xlv = chunked(x), chunked(yo), chunked(xlno)
    wvv = chunked(wv_in)

    with contextlib.ExitStack() as _f1s:
        def _sb(n, s, d):
            return _f1s.enter_context(nc.sbuf_tensor(n, s, d))
        g1s = _sb("f1_g1s", [128, CD], F32)
        b1s = _sb("f1_b1s", [128, CD], F32)
        gts = _sb("f1_gts", [128, CD], F32)
        bts = _sb("f1_bts", [128, CD], F32)
        tds = _sb("f1_tds", [128, CD], F32)
        revb = _sb("f1_revb", [128, T], F32)
        xs = _sb("f1_xs", [128, CD, 512], F32)
        yb = _sb("f1_ybf", [128, CD, 512], F32)
        x1b = _sb("f1_x1b", [128, CD, 512], F32)
        xlnb = _sb("f1_xlnb", [128, CD, T], BF16)
        tmp = _sb("f1_tmp", [128, 2, 512], F32)
        tmp2 = _sb("f1_tmp2", [128, 2, 512], F32)
        xbf = _sb("f1_xbf", [128, 2, 512], BF16)
        xsq = _sb("f1_xsq", [128, 2, 512], BF16)
        mb = _sb("f1_mbx", [128, 512], F32)
        rb = _sb("f1_rbx", [128, 512], F32)
        rows = _sb("f1_rows", [1, 6, 512], F32)
        wvs = _sb("f1_wvs", [128, 2, CD, 128], BF16)
        vt = _sb("f1_vt", [128, 2, 512], F32)
        dct = _sb("f1_dct", [128, 2, 512], F32)
        vdt = _sb("f1_vdt", [128, 2, 512], F32)
        vmpt = _sb("f1_vmpt", [128, 2, CD], F32)
        vmps = _sb("f1_vmps", [128, CD], F32)
        srow, sqrow, r1, r2, sd, rstd = (rows[:, i] for i in range(6))

        NCON = 7
        with nc.Block() as block:
            @block.sync
            def _(sync):
                for dst, srcv in ((g1s[:], g1[:]), (b1s[:], bt1[:]), (gts[:], gt[:]),
                                  (bts[:], btt[:]), (tds[:], td[:])):
                    p.inc(sync.dma_start(out=dst, in_=srcv), "f1_lcst", 16)
                p.inc(sync.dma_start(out=rvb[0:1, :], in_=rev[0:1, :]), "f1_lcst", 16)
                sync.wait_ge(p.sem("f1_lcst"), B["f1_lcst"] + 16 * 6)
                p.inc(sync.dma_start(out=revb[:], in_=rvb.broadcast_to((128, T))), "f1_lcst", 16)

            ylast = [0]
            ycl = {}
            for tt in range(2):
                tsl = slice(tt * 512, (tt + 1) * 512)

                @block.sync
                def _(sync, tt=tt, tsl=tsl):
                    if tt > 0:
                        sync.wait_ge(p.sem("ln_y"), ycl[("c", tt - 1)])
                    p.inc(sync.dma_start(out=xs[:], in_=xv[:, :, tsl]), "f1_lx", 16)
                for nm, src_, dst_, eps_i, gains in (
                        ("a", xs, yb, 0, None), ("b", yb, x1b, 1, (g1s, b1s)),
                        ("c", x1b, xlnb[:, :, tsl], 1, (gts, bts))):
                    gp_w = ([("f1_lx", B["f1_lx"] + 16 * (tt + 1))] if nm == "a"
                            else [("ln_y", ylast[0])])
                    ve_w = []
                    if tt == 1 and nm == "a":
                        ve_w = [("f1_yst0", B["f1_yst0"] + 16)]
                    if tt == 1 and nm == "c":
                        ve_w = [("f1_xlst0", B["f1_xlst0"] + 16)]
                    ylast[0] = emit_ln(p, block, src_, dst_, 512, ones_bf, b1, b2,
                            mb, rb, tmp, tmp2, xbf, xsq, ps[4][0:1, :], ps[5][0:1, :],
                            cst[0:1, eps_i:eps_i + 1], srow, sqrow, r1, r2, sd, rstd,
                            gp_wait=gp_w, sc_wait=gp_w + [("f1_lcst", B["f1_lcst"] + 16 * NCON)],
                            te_wait=[("f1_lcst", B["f1_lcst"] + 16 * NCON)], ve_wait=ve_w,
                            gains=gains, prev=True)
                    ycl[(nm, tt)] = ylast[0]

                @block.scalar
                def _(sc, tt=tt, tsl=tsl):
                    sc.wait_ge(p.sem("ln_y"), ycl[("a", tt)])
                    p.inc(sc.dma_start(out=yv[:, :, tsl], in_=yb[:]), f"f1_yst{tt}", 16)
                    sc.wait_ge(p.sem("ln_y"), ycl[("c", tt)])
                    p.inc(sc.dma_start(out=xlv[:, :, tsl], in_=xlnb[:, :, tsl]),
                          f"f1_xlst{tt}", 16)

            @block.gpsimd
            def _(gp):
                for co in range(CD):
                    if co >= 2:
                        gp.wait_ge(p.sem("f1_vmm"), B["f1_vmm"] + 2 * (co - 1))
                    p.inc(gp.dma_start(out=wvs[:, co % 2],
                                       in_=wvv[:, :, co * 128:(co + 1) * 128]),
                          f"f1_lwv{co % 2}", 16)

            @block.tensor
            def _(te):
                te.wait_ge(p.sem("ln_y"), ycl[("c", 1)])
                for co in range(CD):
                    te.wait_ge(p.sem(f"f1_lwv{co % 2}"), B[f"f1_lwv{co % 2}"] + 16 * (co // 2 + 1))
                    for tt in range(2):
                        i = co * 2 + tt
                        if i >= 2:
                            te.wait_ge(p.sem("f1_vev"), B["f1_vev"] + i - 1)
                        for c in range(CD):
                            ins = te.matmul(ps[i % 2][:], wvs[:, co % 2, c, :],
                                            xlnb[:, c, tt * 512:(tt + 1) * 512],
                                            start=(c == 0), stop=(c == CD - 1))
                        p.inc(ins, "f1_vmm", 1)

            @block.scalar
            def _(sc):
                for co in range(CD):
                    for tt in range(2):
                        i = co * 2 + tt
                        sc.wait_ge(p.sem("f1_vmm"), B["f1_vmm"] + i + 1)
                        if i >= 2:
                            sc.wait_ge(p.sem("f1_vred"), B["f1_vred"] + i - 1)
                        p.inc(sc.activation(vt[:, i % 2], ps[i % 2][:], AF.Identity),
                              "f1_vev", 1)
                        sc.wait_ge(p.sem("f1_varg"), B["f1_varg"] + i + 1)
                        p.inc(sc.activation(dct[:, i % 2], vdt[:, i % 2], AF.Exp),
                              "f1_vdec", 1)

            @block.vector
            def _(ve):
                ve.wait_ge(p.sem("f1_lcst"), B["f1_lcst"] + 16 * NCON)
                for co in range(CD):
                    for tt in range(2):
                        i = co * 2 + tt
                        if i >= 2:
                            ve.wait_ge(p.sem("f1_vdec"), B["f1_vdec"] + i - 1)
                        p.inc(ve.tensor_scalar(vdt[:, i % 2],
                                               revb[:, tt * 512:(tt + 1) * 512],
                                               tds[:, co:co + 1], None, ALU.mult),
                              "f1_varg", 1)
                        ve.wait_ge(p.sem("f1_vev"), B["f1_vev"] + i + 1)
                        ve.wait_ge(p.sem("f1_vdec"), B["f1_vdec"] + i + 1)
                        p.chain(ve, ve.tensor_mul(vdt[:, i % 2], dct[:, i % 2], vt[:, i % 2])
                                if False else
                                ve.tensor_tensor(tmp[:, i % 2], dct[:, i % 2], vt[:, i % 2],
                                                 ALU.mult), "vch")
                        p.inc(ve.tensor_reduce(vmpt[:, tt, co:co + 1], tmp[:, i % 2],
                                               AX.X, ALU.add), "f1_vred", 1)
                ve.wait_ge(p.sem("f1_vred"), B["f1_vred"] + 32)
                p.inc(ve.tensor_add(vmps[:], vmpt[:, 0], vmpt[:, 1]), "f1_vfin", 1)

            @block.scalar
            def _(sc):
                sc.wait_ge(p.sem("f1_vfin"), B["f1_vfin"] + 1)
                p.inc(sc.dma_start(out=vmpo[:], in_=vmps[:]), "f1_vst", 16)
                sc.wait_ge(p.sem("f1_vst"), B["f1_vst"] + 16)
                sc.wait_ge(p.sem("f1_yst0"), B["f1_yst0"] + 16)
                sc.wait_ge(p.sem("f1_yst1"), B["f1_yst1"] + 16)
                sc.wait_ge(p.sem("f1_xlst0"), B["f1_xlst0"] + 16)
                sc.wait_ge(p.sem("f1_xlst1"), B["f1_xlst1"] + 16)


def build_l1at1():
    p = P()
    nc = p.nc
    x = nc.dram_tensor("x", [D, T], F32, kind="ExternalInput").ap()
    wv_in = nc.dram_tensor("Wv", [D, D], F32, kind="ExternalInput").ap()
    g1 = nc.dram_tensor("ln1g_cm", [128, CD], F32, kind="ExternalInput").ap()
    bt1 = nc.dram_tensor("ln1b_cm", [128, CD], F32, kind="ExternalInput").ap()
    gt = nc.dram_tensor("tmg_cm", [128, CD], F32, kind="ExternalInput").ap()
    btt = nc.dram_tensor("tmb_cm", [128, CD], F32, kind="ExternalInput").ap()
    td = nc.dram_tensor("td_cm", [128, CD], F32, kind="ExternalInput").ap()
    rev = nc.dram_tensor("rev_t", [1, T], F32, kind="ExternalInput").ap()
    yo = nc.dram_tensor("yo", [D, T], F32, kind="ExternalOutput").ap()
    xlno = nc.dram_tensor("xlno", [D, T], BF16, kind="ExternalOutput").ap()
    vmpo = nc.dram_tensor("vmpo", [128, CD], F32, kind="ExternalOutput").ap()
    rvb = nc.dram_tensor("rvb", [1, T], F32).ap()
    with p.stack:
        ones_bf, cst, b1, b2 = _mk_consts(p)
        ps = [p.psum(f"ps{i}", [128, 512]) for i in range(8)]
        emit_l1at1(p, x, wv_in, g1, bt1, gt, btt, td, rev, yo, xlno, vmpo,
                   ones_bf, cst, b1, b2, ps, rvb)
    return p.nc



def emit_l1at2(p, y_in, xln_in, xlb, vmp_o, vmp_p, tf, wr_in, wo_in, bo, g2,
               b2g, gc, bc, x2o, xco, ones_bf, cst, b1, b2, ps):
    nc = p.nc
    p.pool_reset()
    B = p.snap()

    yv, xlnv = chunked(y_in), chunked(xln_in)
    wrv, wov = chunked(wr_in), chunked(wo_in)
    x2v, xcv = chunked(x2o), chunked(xco)

    with contextlib.ExitStack() as _f2s:
        def _sb(n, s, d):
            return _f2s.enter_context(nc.sbuf_tensor(n, s, d))
        tfs = _sb("f2_tfs", [128, CD], F32)
        tfm1 = _sb("f2_f2_tfm1", [128, CD], F32)
        xlbs = _sb("f2_xlbs", [128, CD], BF16)
        bos = _sb("f2_bos", [128, CD], F32)
        g2s = _sb("f2_g2s", [128, CD], F32)
        b2s = _sb("f2_b2s", [128, CD], F32)
        gcs = _sb("f2_gcs", [128, CD], F32)
        bcs = _sb("f2_bcs", [128, CD], F32)
        vo = _sb("f2_vo", [128, CD], F32)
        vp = _sb("f2_vp", [128, CD], F32)
        vmx = _sb("f2_vmx", [128, CD], F32)
        vhl = _sb("f2_vhl", [128, CD, 2], BF16)
        vh32 = _sb("f2_vh32", [128, CD], F32)
        vlo = _sb("f2_vlo", [128, CD], F32)
        rbuf = _sb("f2_rbuf", [128, CD, T], BF16)
        zb = _sb("f2_zb", [128, CD], F32)
        zbb = _sb("f2_zbb", [128, CD], F32)
        zt = _sb("f2_zt", [128, 2, 2], F32)
        ws = _sb("f2_ws", [128, 2, CD, 128], BF16)
        tmp = _sb("f2_tmp", [128, 2, 512], F32)
        tmp2 = _sb("f2_tmp2", [128, 2, 512], F32)
        xbf = _sb("f2_xbf", [128, 2, 512], BF16)
        xsq = _sb("f2_xsq", [128, 2, 512], BF16)
        mb = _sb("f2_mbx", [128, 512], F32)
        rb = _sb("f2_rbx", [128, 512], F32)
        rows = _sb("f2_rows", [1, 6, 512], F32)
        srow, sqrow, r1, r2, sd, rstd = (rows[:, i] for i in range(6))
        consts = [(tfs[:], tf[:]), (xlbs[:], xlb[:]), (bos[:], bo[:]),
                  (g2s[:], g2[:]), (b2s[:], b2g[:]), (gcs[:], gc[:]),
                  (bcs[:], bc[:]), (vo[:], vmp_o[:])]
        if vmp_p is not None:
            consts.append((vp[:], vmp_p[:]))
        NCON = len(consts)

        with nc.Block() as block:
            @block.sync
            def _(sync):
                for dst, srcv in consts:
                    p.inc(sync.dma_start(out=dst, in_=srcv), "f2_lcst", 16)

        with (nc.sbuf_tensor("xlnb", [128, CD, T], BF16) as xlnb,
              nc.sbuf_tensor("xsb", [128, CD, T], BF16) as xsb,
              nc.sbuf_tensor("mixb", [128, CD, T], BF16) as mixb):
            with nc.Block() as block:
                @block.sync
                def _(sync):
                    p.inc(sync.dma_start(out=xlnb[:], in_=xlnv[:]), "f2_lxln", 16)

                @block.vector
                def _(ve):
                    ve.wait_ge(p.sem("f2_lcst"), B["f2_lcst"] + 16 * NCON)
                    p.inc(ve.tensor_scalar(tfm1[:], tfs[:], -1.0, 1.0, ALU.mult, ALU.add),
                          "f2_tfm", 1)
                    ve.wait_ge(p.sem("f2_lxln"), B["f2_lxln"] + 16)
                    ve.wait_ge(p.sem("f2_tfm"), B["f2_tfm"] + 1)
                    for c in range(CD):
                        p.chain(ve, ve.tensor_copy(xsb[:, c, 1:T], xlnb[:, c, 0:T - 1]), "vch")
                        p.chain(ve, ve.tensor_copy(xsb[:, c, 0:1], xlbs[:, c:c + 1]), "vch")
                        p.chain(ve, ve.tensor_scalar(tmp[:, 0, :] if False else tmp.rearrange("p a b -> p (a b)")[:, 0:T],
                                                     xlnb[:, c], tfs[:, c:c + 1], None,
                                                     ALU.mult), "vch")
                        ins = ve.scalar_tensor_tensor(mixb[:, c], xsb[:, c], tfm1[:, c:c + 1],
                                                      tmp.rearrange("p a b -> p (a b)")[:, 0:T],
                                                      ALU.mult, ALU.add)
                        p.inc(ins, "f2_mixd", 1)

                @block.gpsimd
                def _(gp):
                    for co in range(CD):
                        if co >= 2:
                            gp.wait_ge(p.sem("f2_rmm"), B["f2_rmm"] + 2 * (co - 1))
                        p.inc(gp.dma_start(out=ws[:, co % 2],
                                           in_=wrv[:, :, co * 128:(co + 1) * 128]),
                              f"f2_lwr{co % 2}", 16)

                @block.tensor
                def _(te):
                    te.wait_ge(p.sem("f2_mixd"), B["f2_mixd"] + CD)
                    for co in range(CD):
                        te.wait_ge(p.sem(f"f2_lwr{co % 2}"), B[f"f2_lwr{co % 2}"] + 16 * (co // 2 + 1))
                        for tt in range(2):
                            i = co * 2 + tt
                            if i >= 2:
                                te.wait_ge(p.sem("f2_rev_"), B["f2_rev_"] + i - 1)
                            for c in range(CD):
                                ins = te.matmul(ps[i % 2][:], ws[:, co % 2, c, :],
                                                mixb[:, c, tt * 512:(tt + 1) * 512],
                                                start=(c == 0), stop=(c == CD - 1))
                            p.inc(ins, "f2_rmm", 1)

                @block.scalar
                def _(sc):
                    for co in range(CD):
                        for tt in range(2):
                            i = co * 2 + tt
                            sc.wait_ge(p.sem("f2_rmm"), B["f2_rmm"] + i + 1)
                            p.inc(sc.activation(rbuf[:, co, tt * 512:(tt + 1) * 512],
                                                ps[i % 2][:], AF.Sigmoid), "f2_rev_", 1)

        with nc.Block() as block:
            @block.vector
            def _(ve):
                ve.wait_ge(p.sem("f2_lcst"), B["f2_lcst"] + 16 * NCON)
                if vmp_p is None:
                    p.chain(ve, ve.tensor_copy(vmx[:], vo[:]), "vch")
                else:
                    p.chain(ve, ve.tensor_add(vmx[:], vo[:], vp[:]), "vch")
                p.chain(ve, ve.tensor_copy(vhl[:, :, 0], vmx[:]), "vch")
                p.chain(ve, ve.tensor_copy(vh32[:], vhl[:, :, 0]), "vch")
                p.chain(ve, ve.tensor_sub(vlo[:], vmx[:], vh32[:]), "vch")
                p.inc(ve.tensor_copy(vhl[:, :, 1], vlo[:]), "f2_vhld", 1)

            @block.gpsimd
            def _(gp):
                gp.wait_ge(p.sem("f2_rmm"), B["f2_rmm"] + 32)
                for co in range(CD):
                    if co >= 2:
                        gp.wait_ge(p.sem("f2_zmm"), B["f2_zmm"] + co - 1)
                    p.inc(gp.dma_start(out=ws[:, co % 2],
                                       in_=wov[:, :, co * 128:(co + 1) * 128]),
                          f"f2_lwo{co % 2}", 16)

            @block.tensor
            def _(te):
                te.wait_ge(p.sem("f2_vhld"), B["f2_vhld"] + 1)
                te.wait_ge(p.sem("f2_rmm"), B["f2_rmm"] + 32)
                te.wait_ge(p.sem("f2_rev_"), B["f2_rev_"] + 32)
                for co in range(CD):
                    te.wait_ge(p.sem(f"f2_lwo{co % 2}"), B[f"f2_lwo{co % 2}"] + 16 * (co // 2 + 1))
                    if co >= 2:
                        te.wait_ge(p.sem("f2_zev"), B["f2_zev"] + co - 1)
                    for c in range(CD):
                        ins = te.matmul(ps[co % 2][:, 0:2], ws[:, co % 2, c, :],
                                        vhl[:, c, :], start=(c == 0), stop=(c == CD - 1))
                    p.inc(ins, "f2_zmm", 1)

            @block.scalar
            def _(sc):
                for co in range(CD):
                    sc.wait_ge(p.sem("f2_zmm"), B["f2_zmm"] + co + 1)
                    if co >= 2:
                        sc.wait_ge(p.sem("f2_zbd"), B["f2_zbd"] + co - 1)
                    p.inc(sc.activation(zt[:, co % 2, :], ps[co % 2][:, 0:2], AF.Identity),
                          "f2_zev", 1)

            @block.vector
            def _(ve):
                for co in range(CD):
                    ve.wait_ge(p.sem("f2_zev"), B["f2_zev"] + co + 1)
                    p.chain(ve, ve.tensor_add(zb[:, co:co + 1], zt[:, co % 2, 0:1],
                                              zt[:, co % 2, 1:2]), "vch")
                    p.inc(ve.tensor_scalar(zbb[:, co:co + 1], zb[:, co:co + 1],
                                           bos[:, co:co + 1], None, ALU.add), "f2_zbd", 1)

        with (nc.sbuf_tensor("ybA", [128, CD, 512], F32) as ybA,
              nc.sbuf_tensor("ybB", [128, CD, 512], F32) as ybB,
              nc.sbuf_tensor("xct", [128, CD, 512], BF16) as xct,
              nc.Block() as block):
            ylast = [0]
            ycl = {}
            for tt in range(2):
                tsl = slice(tt * 512, (tt + 1) * 512)
                if True:
                    @block.sync
                    def _(sync, tt=tt, tsl=tsl):
                        sync.wait_ge(p.sem("f2_zbd"), B["f2_zbd"] + CD)
                        if tt > 0:
                            sync.wait_ge(p.sem("ln_y"), ycl[("e", tt - 1)])
                        p.inc(sync.dma_start(out=ybA[:], in_=yv[:, :, tsl]), "f2_ly", 16)

                    @block.vector
                    def _(ve, tt=tt, tsl=tsl):
                        ve.wait_ge(p.sem("f2_ly"), B["f2_ly"] + 16 * (tt + 1))
                        ve.wait_ge(p.sem("f2_zbd"), B["f2_zbd"] + CD)
                        if tt > 0:
                            ve.wait_ge(p.sem(f"f2_x2st{tt - 1}"), B[f"f2_x2st{tt - 1}"] + 16)
                        for c in range(CD):
                            p.chain(ve, ve.tensor_scalar(tmp[:, c % 2], rbuf[:, c, tsl],
                                                         zbb[:, c:c + 1], None, ALU.mult),
                                    "vch")
                            ins = ve.tensor_add(ybB[:, c], ybA[:, c], tmp[:, c % 2])
                            p.inc(ins, f"f2_x2d{tt}", 1)

                    @block.scalar
                    def _(sc, tt=tt, tsl=tsl):
                        sc.wait_ge(p.sem(f"f2_x2d{tt}"), B[f"f2_x2d{tt}"] + CD)
                        p.inc(sc.dma_start(out=x2v[:, :, tsl], in_=ybB[:]), f"f2_x2st{tt}", 16)

                for nm, src_, dst_, gains in (("d", ybB, ybA, (g2s, b2s)),
                                              ("e", ybA, xct, (gcs, bcs))):
                    gpw = ([(f"f2_x2d{tt}", B[f"f2_x2d{tt}"] + CD)] if nm == "d"
                           else [("ln_y", ylast[0])])
                    vew = ([(f"f2_x2st{tt}", B[f"f2_x2st{tt}"] + 16)] if nm == "d"
                           else ([("f2_xcst0", B["f2_xcst0"] + 16)] if tt == 1 else []))
                    ylast[0] = emit_ln(p, block, src_, dst_, 512, ones_bf, b1, b2,
                            mb, rb, tmp, tmp2, xbf, xsq, ps[4][0:1, :], ps[5][0:1, :],
                            cst[0:1, 1:2], srow, sqrow, r1, r2, sd, rstd,
                            gp_wait=gpw,
                            sc_wait=gpw + [("f2_lcst", B["f2_lcst"] + 16 * NCON)],
                            te_wait=[("f2_lcst", B["f2_lcst"] + 16 * NCON),
                                     ("f2_zmm", B["f2_zmm"] + CD)], ve_wait=vew,
                            gains=gains, prev=True)
                    ycl[(nm, tt)] = ylast[0]

                @block.scalar
                def _(sc, tt=tt, tsl=tsl):
                    sc.wait_ge(p.sem("ln_y"), ycl[("e", tt)])
                    p.inc(sc.dma_start(out=xcv[:, :, tsl], in_=xct[:]), f"f2_xcst{tt}", 16)

            @block.scalar
            def _(sc):
                for s in ("f2_x2st0", "f2_x2st1", "f2_xcst0", "f2_xcst1"):
                    sc.wait_ge(p.sem(s), B[s] + 16)


def build_l1at2():
    p = P()
    nc = p.nc
    y_in = nc.dram_tensor("y", [D, T], F32, kind="ExternalInput").ap()
    xln_in = nc.dram_tensor("xln", [D, T], BF16, kind="ExternalInput").ap()
    xlb = nc.dram_tensor("xlb_cm", [128, CD], BF16, kind="ExternalInput").ap()
    vmp_o = nc.dram_tensor("vmp_own", [128, CD], F32, kind="ExternalInput").ap()
    vmp_p = nc.dram_tensor("vmp_pair", [128, CD], F32, kind="ExternalInput").ap()
    tf = nc.dram_tensor("tf_cm", [128, CD], F32, kind="ExternalInput").ap()
    wr_in = nc.dram_tensor("Wr", [D, D], F32, kind="ExternalInput").ap()
    wo_in = nc.dram_tensor("Wo", [D, D], F32, kind="ExternalInput").ap()
    bo = nc.dram_tensor("bo_cm", [128, CD], F32, kind="ExternalInput").ap()
    g2 = nc.dram_tensor("ln2g_cm", [128, CD], F32, kind="ExternalInput").ap()
    b2g = nc.dram_tensor("ln2b_cm", [128, CD], F32, kind="ExternalInput").ap()
    gc = nc.dram_tensor("cmg_cm", [128, CD], F32, kind="ExternalInput").ap()
    bc = nc.dram_tensor("cmb_cm", [128, CD], F32, kind="ExternalInput").ap()
    x2o = nc.dram_tensor("x2o", [D, T], F32, kind="ExternalOutput").ap()
    xco = nc.dram_tensor("xco", [D, T], BF16, kind="ExternalOutput").ap()
    with p.stack:
        ones_bf, cst, b1, b2 = _mk_consts(p)
        ps = [p.psum(f"ps{i}", [128, 512]) for i in range(8)]
        emit_l1at2(p, y_in, xln_in, xlb, vmp_o, vmp_p, tf, wr_in, wo_in, bo, g2,
                   b2g, gc, bc, x2o, xco, ones_bf, cst, b1, b2, ps)
    return p.nc



def emit_l1at3(p, x, x2, xc, xcb, wck, wcv, wcr, xo, ones_bf, cst, ps,
               w_gate=None):
    nc = p.nc
    p.pool_reset()
    B = p.snap()

    xv, x2v, xcv, xov = chunked(x), chunked(x2), chunked(xc), chunked(xo)
    wckv, wcvv, wcrv = chunked(wck), chunked(wcv), chunked(wcr)
    FK = 64

    with contextlib.ExitStack() as _f3s:
        def _sb(n, s, d):
            return _f3s.enter_context(nc.sbuf_tensor(n, s, d))
        xcbs = _sb("f3_xcbs", [128, CD], BF16)
        xcnb = _sb("f3_xcnb", [128, CD, T], BF16)
        xcsb = _sb("f3_xcsb", [128, CD, T], BF16)
        rc = _sb("f3_rc", [128, CD, T], BF16)
        wks = _sb("f3_wks", [128, 2, CD, 128], BF16)
        wvs = _sb("f3_wvs", [128, 2, 1024], BF16)
        kcb = _sb("f3_kcb", [128, FK, 512], BF16)
        rtb = _sb("f3_rtb", [128, 2, 512], BF16)
        vcb = _sb("f3_vcb", [128, 2, 512], BF16)
        t1b = _sb("f3_t1b", [128, 2, 512], F32)
        o1b = _sb("f3_o1b", [128, 2, 512], F32)
        x2t = _sb("f3_x2t", [128, 2, 512], F32)
        xt = _sb("f3_xt", [128, 2, 512], F32)
        ob = _sb("f3_ob", [128, 2, 512], F32)

        with nc.Block() as block:
            @block.sync
            def _(sync):
                for dst, srcv in ((xcbs[:], xcb[:]),):
                    p.inc(sync.dma_start(out=dst, in_=srcv), "f3_lcst", 16)
                p.inc(sync.dma_start(out=xcnb[:], in_=xcv[:]), "f3_lxc", 16)

            @block.vector
            def _(ve):
                ve.wait_ge(p.sem("f3_lcst"), B["f3_lcst"] + 16)
                ve.wait_ge(p.sem("f3_lxc"), B["f3_lxc"] + 16)
                for c in range(CD):
                    p.chain(ve, ve.tensor_copy(xcsb[:, c, 1:T], xcnb[:, c, 0:T - 1]), "vch")
                    ins = ve.tensor_copy(xcsb[:, c, 0:1], xcbs[:, c:c + 1])
                    p.inc(ins, "f3_xcsd", 1)

            @block.gpsimd
            def _(gp):
                for co in range(CD):
                    if co >= 2:
                        gp.wait_ge(p.sem("f3_rcmm"), B["f3_rcmm"] + 2 * (co - 1))
                    p.inc(gp.dma_start(out=wks[:, co % 2],
                                       in_=wcrv[:, :, co * 128:(co + 1) * 128]),
                          f"f3_lwcr{co % 2}", 16)

            @block.tensor
            def _(te):
                te.wait_ge(p.sem("f3_lxc"), B["f3_lxc"] + 16)
                for co in range(CD):
                    te.wait_ge(p.sem(f"f3_lwcr{co % 2}"), B[f"f3_lwcr{co % 2}"] + 16 * (co // 2 + 1))
                    for tt in range(2):
                        i = co * 2 + tt
                        if i >= 2:
                            te.wait_ge(p.sem("f3_rcev"), B["f3_rcev"] + i - 1)
                        for c in range(CD):
                            ins = te.matmul(ps[i % 2][:], wks[:, co % 2, c, :],
                                            xcnb[:, c, tt * 512:(tt + 1) * 512],
                                            start=(c == 0), stop=(c == CD - 1))
                        p.inc(ins, "f3_rcmm", 1)

            @block.scalar
            def _(sc):
                for co in range(CD):
                    for tt in range(2):
                        i = co * 2 + tt
                        sc.wait_ge(p.sem("f3_rcmm"), B["f3_rcmm"] + i + 1)
                        p.inc(sc.activation(rc[:, co, tt * 512:(tt + 1) * 512],
                                            ps[i % 2][:], AF.Sigmoid), "f3_rcev", 1)

            for tt in range(2):
                tsl = slice(tt * 512, (tt + 1) * 512)
                @block.gpsimd
                def _(gp, tt=tt):
                    for f in range(FK):
                        FG = tt * FK + f
                        if FG == 0 and w_gate is not None:
                            gp.wait_ge(p.sem(w_gate[0]), w_gate[1])
                        if FG < 2:
                            gp.wait_ge(p.sem("f3_rcmm"), B["f3_rcmm"] + 32)
                        else:
                            gp.wait_ge(p.sem("f3_kmm"), B["f3_kmm"] + FG - 1)
                        p.inc(gp.dma_start(out=wks[:, FG % 2],
                                           in_=wckv[:, :, f * 128:(f + 1) * 128]),
                              f"f3_lwck{FG % 2}", 16)

                @block.tensor
                def _(te, tt=tt, tsl=tsl):
                    if tt == 0:
                        te.wait_ge(p.sem("f3_rcmm"), B["f3_rcmm"] + 32)
                        te.wait_ge(p.sem("f3_rcev"), B["f3_rcev"] + 32)
                    else:
                        te.wait_ge(p.sem("f3_vcev"), B["f3_vcev"] + 16)
                    te.wait_ge(p.sem("f3_xcsd"), B["f3_xcsd"] + CD)
                    for f in range(FK):
                        FG = tt * FK + f
                        te.wait_ge(p.sem(f"f3_lwck{FG % 2}"), B[f"f3_lwck{FG % 2}"] + 16 * (FG // 2 + 1))
                        if FG >= 2:
                            te.wait_ge(p.sem("f3_ksq"), B["f3_ksq"] + FG - 1)
                        for c in range(CD):
                            ins = te.matmul(ps[FG % 2][:], wks[:, FG % 2, c, :],
                                            xcsb[:, c, tsl], start=(c == 0),
                                            stop=(c == CD - 1))
                        p.inc(ins, "f3_kmm", 1)

                @block.scalar
                def _(sc, tt=tt):
                    for f in range(FK):
                        FG = tt * FK + f
                        sc.wait_ge(p.sem("f3_kmm"), B["f3_kmm"] + FG + 1)
                        p.chain(sc, sc.activation(rtb[:, FG % 2], ps[FG % 2][:], AF.Relu),
                                "sch")
                        p.inc(sc.activation(kcb[:, f], rtb[:, FG % 2], AF.Square), "f3_ksq", 1)

                @block.gpsimd
                def _(gp, tt=tt):
                    for dg in range(2):
                        for f in range(FK):
                            FD = (tt * 2 + dg) * FK + f
                            if FD >= 2:
                                gp.wait_ge(p.sem("f3_vmm"), B["f3_vmm"] + FD - 1)
                            dsl = slice(dg * 1024, (dg + 1) * 1024)
                            p.inc(gp.dma_start(out=wvs[:, FD % 2], in_=wcvv[:, f, dsl]),
                                  f"f3_lwcv{FD % 2}", 16)

                @block.sync
                def _(sync, tt=tt, tsl=tsl):
                    for i in range(CD):
                        gi = tt * CD + i
                        if gi >= 2:
                            sync.wait_ge(p.sem("f3_obd"), B["f3_obd"] + gi - 1)
                        p.inc(sync.dma_start(out=x2t[:, gi % 2], in_=x2v[:, i, tsl]),
                              f"f3_lx2{gi % 2}", 16)
                        p.inc(sync.dma_start(out=xt[:, gi % 2], in_=xv[:, i, tsl]),
                              f"f3_lxr{gi % 2}", 16)

                @block.tensor
                def _(te, tt=tt):
                    te.wait_ge(p.sem("f3_ksq"), B["f3_ksq"] + FK * (tt + 1))
                    for dg in range(2):
                        DG = tt * 2 + dg
                        te.wait_ge(p.sem("f3_vcev"), B["f3_vcev"] + 8 * DG)
                        for f in range(FK):
                            FD = DG * FK + f
                            te.wait_ge(p.sem(f"f3_lwcv{FD % 2}"), B[f"f3_lwcv{FD % 2}"] + 16 * (FD // 2 + 1))
                            for dout in range(8):
                                ins = te.matmul(ps[dout][:],
                                                wvs[:, FD % 2, dout * 128:(dout + 1) * 128],
                                                kcb[:, f], start=(f == 0), stop=(f == FK - 1))
                            p.inc(ins, "f3_vmm", 1)

                @block.scalar
                def _(sc, tt=tt, tsl=tsl):
                    for dg in range(2):
                        DG = tt * 2 + dg
                        for dout in range(8):
                            gi = DG * 8 + dout
                            sc.wait_ge(p.sem("f3_vmm"), B["f3_vmm"] + (DG + 1) * FK)
                            if gi >= 2:
                                sc.wait_ge(p.sem("f3_t1d"), B["f3_t1d"] + gi - 1)
                            p.inc(sc.activation(vcb[:, gi % 2], ps[dout][:], AF.Identity),
                                  "f3_vcev", 1)
                            if gi >= 1:
                                sc.wait_ge(p.sem("f3_obd"), B["f3_obd"] + gi)
                                pch = (gi - 1) % CD
                                ptt = (gi - 1) // CD
                                psl = slice(ptt * 512, (ptt + 1) * 512)
                                p.inc(sc.dma_start(out=xov[:, pch, psl],
                                                   in_=ob[:, (gi - 1) % 2]),
                                      f"f3_obst{(gi - 1) % 2}", 16)

                @block.vector
                def _(ve, tt=tt, tsl=tsl):
                    ve.wait_ge(p.sem("f3_rcev"), B["f3_rcev"] + 32)
                    for dg in range(2):
                        DG = tt * 2 + dg
                        for dout in range(8):
                            gi = DG * 8 + dout
                            ch = dg * 8 + dout
                            ve.wait_ge(p.sem("f3_vcev"), B["f3_vcev"] + gi + 1)
                            ve.wait_ge(p.sem(f"f3_lx2{gi % 2}"), B[f"f3_lx2{gi % 2}"] + 16 * (gi // 2 + 1))
                            ve.wait_ge(p.sem(f"f3_lxr{gi % 2}"), B[f"f3_lxr{gi % 2}"] + 16 * (gi // 2 + 1))
                            if gi >= 2:
                                ve.wait_ge(p.sem(f"f3_obst{gi % 2}"), B[f"f3_obst{gi % 2}"] + 16 * (gi // 2))
                            p.chain(ve, ve.tensor_tensor(t1b[:, gi % 2], vcb[:, gi % 2],
                                                         rc[:, ch, tsl], ALU.mult), "f3_t1d")
                            p.chain(ve, ve.tensor_add(o1b[:, gi % 2], t1b[:, gi % 2],
                                                      x2t[:, gi % 2]), "vch")
                            p.inc(ve.tensor_add(ob[:, gi % 2], o1b[:, gi % 2],
                                                xt[:, gi % 2]), "f3_obd", 1)

            @block.scalar
            def _(sc):
                sc.wait_ge(p.sem("f3_obd"), B["f3_obd"] + 32)
                p.inc(sc.dma_start(out=xov[:, 15, 512:1024], in_=ob[:, 1]), "f3_obst1", 16)
                sc.wait_ge(p.sem("f3_obst0"), B["f3_obst0"] + 16 * 16)
                sc.wait_ge(p.sem("f3_obst1"), B["f3_obst1"] + 16 * 16)



def build_l1at3():
    p = P()
    nc = p.nc
    x = nc.dram_tensor("x", [D, T], F32, kind="ExternalInput").ap()
    x2 = nc.dram_tensor("x2", [D, T], F32, kind="ExternalInput").ap()
    xc = nc.dram_tensor("xc", [D, T], BF16, kind="ExternalInput").ap()
    xcb = nc.dram_tensor("xcb_cm", [128, CD], BF16, kind="ExternalInput").ap()
    wck = nc.dram_tensor("Wck16", [D, 8192], BF16, kind="ExternalInput").ap()
    wcv = nc.dram_tensor("Wcv16", [8192, D], BF16, kind="ExternalInput").ap()
    wcr = nc.dram_tensor("Wcr", [D, D], F32, kind="ExternalInput").ap()
    xo = nc.dram_tensor("xo", [D, T], F32, kind="ExternalOutput").ap()
    with p.stack:
        ones_bf, cst, b1, b2 = _mk_consts(p)
        ps = [p.psum(f"ps{i}", [128, 512]) for i in range(8)]
        emit_l1at3(p, x, x2, xc, xcb, wck, wcv, wcr, xo, ones_bf, cst, ps)
    return p.nc



PAIRS = [[0, 1], [2, 3], [4, 5], [6, 7]]
WORLD = [list(range(NC))]


def emit_cc(p, kind, op, groups, pairs):
    with p.nc.Block() as blk:
        @blk.gpsimd
        def _(gp):
            for src_ap, dst_ap in pairs:
                p.inc(gp.collective_compute(kind, op, replica_groups=groups,
                                            ins=[src_ap.opt()], outs=[dst_ap.opt()]),
                      "k_cc", 1)


def emit_lastcol_glue(p, srcT, ag_in, ag_out, parmask, eff):
    nc = p.nc
    p.pool_reset()
    B = p.snap()
    with (nc.sbuf_tensor("gl_a", [128, CD, 1], BF16) as a3,
          nc.sbuf_tensor("gl_b", [128, CD], BF16) as bb,
          nc.sbuf_tensor("gl_m", [128, CD], BF16) as mm_,
          nc.Block() as blk):
        av = a3[:].rearrange("p c o -> p (c o)")

        @blk.sync
        def _(sync):
            p.inc(sync.dma_start(out=a3[:], in_=chunked(srcT)[:, :, T - 1:T],
                                 allow_slow_non_contiguous=True), "k_gl", 16)
            sync.wait_ge(p.sem("k_gl"), B["k_gl"] + 16)
            p.inc(sync.dma_start(out=ag_in[:], in_=av), "k_gl", 16)
            p.inc(sync.dma_start(out=mm_[:], in_=parmask[:]), "k_gl", 16)

        @blk.gpsimd
        def _(gp):
            gp.wait_ge(p.sem("k_gl"), B["k_gl"] + 48)
            p.inc(gp.collective_compute("AllGather", ALU.bypass, replica_groups=PAIRS,
                                        ins=[ag_in.opt()], outs=[ag_out.opt()]),
                  "k_cc", 1)

        @blk.scalar
        def _(sc):
            sc.wait_ge(p.sem("k_cc"), p.cnt["k_cc"])
            p.inc(sc.dma_start(out=bb[:], in_=ag_out[0:128, :]), "k_gl2", 16)

        @blk.vector
        def _(ve):
            ve.wait_ge(p.sem("k_gl2"), B["k_gl2"] + 16)
            ve.wait_ge(p.sem("k_gl"), B["k_gl"] + 48)
            p.inc(ve.tensor_mul(av, bb[:], mm_[:]), "k_gld", 1)

        @blk.scalar
        def _(sc):
            sc.wait_ge(p.sem("k_gld"), p.cnt["k_gld"])
            p.inc(sc.dma_start(out=eff[:], in_=av), "k_gl2", 16)
    p.fence(["k_gl", "k_gl2", "k_cc", "k_gld"])


def build_fused():
    p = P()
    nc = p.nc

    def EI(n, s, d):
        return nc.dram_tensor(n, s, d, kind="ExternalInput").ap()

    def IN(n, s, d):
        return nc.dram_tensor(n, s, d).ap()

    def SH(n, s, d):
        return nc.dram_tensor(n, s, d, addr_space="Shared").ap()

    x0 = EI("x0", [D, T], F32)
    w1s = EI("w1T_s", [256, F1], F32)
    w2s = EI("w2T_s", [256, F1], F32)
    w3s = EI("w3T_s", [F1, 256], F32)
    wqkv = EI("wqkv", [D, 3 * D], F32)
    bqkv_cm = EI("bqkv_cm", [128, 48], F32)
    bqkv_row = EI("bqkv_row", [1, 3 * D], F32)
    wproj = EI("wproj", [D, D], F32)
    bproj_cm = EI("bproj_cm", [128, CD], F32)
    kbias = EI("kbias", [128, NH * 16], F32)
    qbias = EI("qbias", [NH, 2, 2, 512], BF16)
    masks = EI("masks", [16, 128, 512], BF16)
    ones2 = EI("ones2", [2, 128], BF16)
    wgT = EI("wgT", [D, 2], F32)
    cin, cout = {}, {}
    for nm, (r_, cd_) in CAST_SHARDS.items():
        cin[nm] = EI(nm + "_s", [r_ // NC, cd_], F32)
        cout[nm] = IN(nm + "16_s", [r_ // NC, cd_], BF16)
    ba_cm = [EI(f"b{e}a_cm", [128, FECH], F32) for e in range(2)]
    bb_cm = [EI(f"b{e}b_cm", [128, CD], F32) for e in range(2)]
    Wv = EI("Wv", [D, D], F32)
    Wr = EI("Wr", [D, D], F32)
    Wo = EI("Wo", [D, D], F32)
    Wcr = EI("Wcr", [D, D], F32)
    g1cm = EI("ln1g_cm", [128, CD], F32)
    b1cm = EI("ln1b_cm", [128, CD], F32)
    gtcm = EI("tmg_cm", [128, CD], F32)
    btcm = EI("tmb_cm", [128, CD], F32)
    tdcm = EI("td_cm", [128, CD], F32)
    tfcm = EI("tf_cm", [128, CD], F32)
    bocm = EI("bo_cm", [128, CD], F32)
    g2cm = EI("ln2g_cm", [128, CD], F32)
    b2cm = EI("ln2b_cm", [128, CD], F32)
    gccm = EI("cmg_cm", [128, CD], F32)
    bccm = EI("cmb_cm", [128, CD], F32)
    rev = EI("rev_t", [1, T], F32)
    parmask = EI("parmask", [128, CD], BF16)
    x6 = nc.dram_tensor("x6", [D, T], F32, kind="ExternalOutput").ap()

    q1sh = IN("q1sh", [256, F1], BF16)
    q2sh = IN("q2sh", [256, F1], BF16)
    q3sh = IN("q3sh", [F1, 256], BF16)
    q1T = SH("q1T", [D, F1], BF16)
    q2T = SH("q2T", [D, F1], BF16)
    q3ag = SH("q3ag", [NC * F1, 256], BF16)
    q3q = q3ag.rearrange("(r ch p) c -> p ch r c", r=NC, p=128)
    wa16 = [SH(f"w{e}aT", [D, FE], BF16) for e in range(2)]
    wb16 = [SH(f"w{e}bT", [FE, D], BF16) for e in range(2)]
    wck16 = SH("Wck16", [D, 8192], BF16)
    wcv16 = SH("Wcv16", [8192, D], BF16)
    x1 = IN("x1", [D, T], F32)
    x2 = IN("x2", [D, T], F32)
    x3 = IN("x3", [D, T], F32)
    x4 = IN("x4", [D, T], F32)
    x5 = IN("x5", [D, T], F32)
    xex = IN("xex", [2 * D, T], F32)
    yF = IN("yF", [D, T], F32)
    xlnF = IN("xlnF", [D, T], BF16)
    vmp = IN("vmp", [128, CD], F32)
    vmpsum = IN("vmpsum", [128, CD], F32)
    x2L = IN("x2L", [D, T], F32)
    xcF = IN("xcF", [D, T], BF16)
    xlc_in = IN("xlc_in", [128, CD], BF16)
    xlc_out = IN("xlc_out", [256, CD], BF16)
    xlb_eff = IN("xlb_eff", [128, CD], BF16)
    xcc_in = IN("xcc_in", [128, CD], BF16)
    xcc_out = IN("xcc_out", [256, CD], BF16)
    xcb_eff = IN("xcb_eff", [128, CD], BF16)
    cc_in = IN("cc_in", [1, 8], F32)
    cc_out = IN("cc_out", [1, 8], F32)
    colb = IN("colb", [1, 512], F32)
    rowb = IN("rowb", [1, 8], F32)
    mb1 = IN("mb1", [1, 512], F32)
    mb2 = IN("mb2", [1, 512], F32)
    rvb = IN("rvb", [1, T], F32)
    vdram = IN("vdram", [2048, D], BF16)
    adram = IN("adram", [D, T], BF16)

    with p.stack:
        ones_bf, cst, b1, b2 = _mk_consts(p)
        ps = [p.psum(f"ps{i}", [128, 512]) for i in range(8)]

        with nc.named_scope("prep"):
            emit_prep(p, w1s, w2s, w3s, q1sh, q2sh, q3sh, cin, cout, cc_in, cc_out,
                      colb, rowb, do_casts=True,
                      ag_q=((q1sh, q1T), (q2sh, q2T), (q3sh, q3ag)),
                      ag_w=((cout["W0aT"], wa16[0]), (cout["W0bT"], wb16[0]),
                            (cout["W1aT"], wa16[1]), (cout["W1bT"], wb16[1]),
                            (cout["Wck"], wck16), (cout["Wcv"], wcv16)))

        with nc.named_scope("ffn0a"):
            emit_l0ffn(p, x0, q1T, q2T, None, x1, ones_bf, cst, b1, b2, ps, q3q=q3q,
                       q_gate=("k_cc", 2, 3))
        with nc.named_scope("ag_x1"):
            emit_cc(p, "AllGather", ALU.bypass, PAIRS, [(x1, xex)])
        with nc.named_scope("attn0"):
            emit_l0attn(p, x1, xex[0:D, :], wqkv, bqkv_cm, bqkv_row, wproj, bproj_cm,
                        kbias, qbias, masks, ones2, x2, ones_bf, cst, b1, b2, ps,
                        vdram, adram, xfh_gate=("k_cc", 10))
        with nc.named_scope("ffn0b"):
            emit_l0ffn(p, x2, q1T, q2T, None, x3, ones_bf, cst, b1, b2, ps, q3q=q3q)

        with nc.named_scope("moe1"):
            emit_l1ffn(p, x3, wgT, wa16, wb16, ba_cm, bb_cm, x4, ones_bf, cst, b1, b2,
                       ps, mb1, mb2, w_gate=("k_cc", 9))
        with nc.named_scope("rwkv1"):
            emit_l1at1(p, x4, Wv, g1cm, b1cm, gtcm, btcm, tdcm, rev, yF, xlnF, vmp,
                       ones_bf, cst, b1, b2, ps, rvb)
        with nc.named_scope("glue1"):
            emit_cc(p, "AllReduce", ALU.add, PAIRS, [(vmp, vmpsum)])
            emit_lastcol_glue(p, xlnF, xlc_in, xlc_out, parmask, xlb_eff)
        with nc.named_scope("rwkv2"):
            emit_l1at2(p, yF, xlnF, xlb_eff, vmpsum, None, tfcm, Wr, Wo, bocm,
                       g2cm, b2cm, gccm, bccm, x2L, xcF, ones_bf, cst, b1, b2, ps)
        with nc.named_scope("glue2"):
            emit_lastcol_glue(p, xcF, xcc_in, xcc_out, parmask, xcb_eff)
        with nc.named_scope("rwkv3"):
            emit_l1at3(p, x4, x2L, xcF, xcb_eff, wck16, wcv16, Wcr, x5,
                       ones_bf, cst, ps, w_gate=("k_cc", 9))
        with nc.named_scope("moe2"):
            emit_l1ffn(p, x5, wgT, wa16, wb16, ba_cm, bb_cm, x6, ones_bf, cst, b1, b2,
                       ps, mb1, mb2, w_gate=("k_cc", 9))
    return p


def _fused_inputs(inp):
    x_full = np.ascontiguousarray(
        np.asarray(inp["L0_attn_hidden_states"], np.float32).reshape(NC * T, D))
    w1T = np.ascontiguousarray(np.asarray(inp["L0_ffn_w1"], np.float32).T)
    w2T = np.ascontiguousarray(np.asarray(inp["L0_ffn_w2"], np.float32).T)
    w3T = np.ascontiguousarray(np.asarray(inp["L0_ffn_w3"], np.float32).T)
    kb_pc, qb_pc, masks_pc = _attn_consts()
    ones_bf = np.ones((128, 1), BF)
    ones2 = np.ones((2, 128), BF)
    shared = {
        "wqkv": np.ascontiguousarray(np.asarray(inp["L0_attn_w_qkv"], np.float32)),
        "bqkv_cm": _cm(inp["L0_attn_b_qkv"]),
        "bqkv_row": np.asarray(inp["L0_attn_b_qkv"], np.float32)[None, :],
        "wproj": np.ascontiguousarray(np.asarray(inp["L0_attn_w_proj"], np.float32)),
        "bproj_cm": _cm(inp["L0_attn_b_proj"]),
        "ones2": ones2, "ones_bf": ones_bf, "cstf": _cstf(),
        "wgT": np.ascontiguousarray(np.asarray(inp["L1_ffn_Wg"], np.float32).T),
        "b0a_cm": _cm(inp["L1_ffn_b0a"]), "b0b_cm": _cm(inp["L1_ffn_b0b"]),
        "b1a_cm": _cm(inp["L1_ffn_b1a"]), "b1b_cm": _cm(inp["L1_ffn_b1b"]),
        "Wv": np.ascontiguousarray(np.asarray(inp["L1_attn_Wv"], np.float32)),
        "Wr": np.ascontiguousarray(np.asarray(inp["L1_attn_Wr"], np.float32)),
        "Wo": np.ascontiguousarray(np.asarray(inp["L1_attn_Wo"], np.float32)),
        "Wcr": np.ascontiguousarray(np.asarray(inp["L1_attn_Wcr"], np.float32)),
        "ln1g_cm": _cm(inp["L1_attn_ln1_g"]), "ln1b_cm": _cm(inp["L1_attn_ln1_b"]),
        "tmg_cm": _cm(inp["L1_attn_tm_g"]), "tmb_cm": _cm(inp["L1_attn_tm_b"]),
        "td_cm": _cm(inp["L1_attn_time_decay"]), "tf_cm": _cm(inp["L1_attn_time_first"]),
        "bo_cm": _cm(inp["L1_attn_bo"]),
        "ln2g_cm": _cm(inp["L1_attn_ln2_g"]), "ln2b_cm": _cm(inp["L1_attn_ln2_b"]),
        "cmg_cm": _cm(inp["L1_attn_cm_g"]), "cmb_cm": _cm(inp["L1_attn_cm_b"]),
    }
    casts = {"W0aT": np.ascontiguousarray(np.asarray(inp["L1_ffn_W0a"], np.float32).T),
             "W0bT": np.ascontiguousarray(np.asarray(inp["L1_ffn_W0b"], np.float32).T),
             "W1aT": np.ascontiguousarray(np.asarray(inp["L1_ffn_W1a"], np.float32).T),
             "W1bT": np.ascontiguousarray(np.asarray(inp["L1_ffn_W1b"], np.float32).T),
             "Wck": np.ascontiguousarray(np.asarray(inp["L1_attn_Wck"], np.float32)),
             "Wcv": np.ascontiguousarray(np.asarray(inp["L1_attn_Wcv"], np.float32))}
    maps = []
    for c in range(NC):
        m = dict(shared)
        m["x0"] = _fm_shard(x_full, c)
        for nm, arr in casts.items():
            rs = arr.shape[0] // NC
            m[nm + "_s"] = np.ascontiguousarray(arr[c * rs:(c + 1) * rs])
        m["w1T_s"] = np.ascontiguousarray(w1T[c * 256:(c + 1) * 256])
        m["w2T_s"] = np.ascontiguousarray(w2T[c * 256:(c + 1) * 256])
        m["w3T_s"] = np.ascontiguousarray(w3T[:, c * 256:(c + 1) * 256])
        m["kbias"] = kb_pc[c]
        m["qbias"] = qb_pc[c]
        m["masks"] = masks_pc[c]
        m["rev_t"] = (2047.0 - (np.arange(T) + (c % 2) * 1024)).astype(np.float32)[None, :]
        m["parmask"] = np.full((128, CD), float(c % 2), BF)
        maps.append(m)
    return maps


def kernel(**inp):
    p = build_fused()
    maps = _fused_inputs(inp)
    res = run_bass_kernel_spmd(p.nc, maps, core_ids=CORES)
    x6 = np.vstack([np.asarray(res.results[c]["x6"], np.float32).T for c in range(NC)])
    return np.ascontiguousarray(x6.reshape(4, 2048, 2048))



_CST = None


def _cstf():
    c = np.zeros((1, 8), np.float32)
    c[0, 0] = LN_EPS
    c[0, 1] = LN_EPS_IN
    return c


def _gather_fm(results, key):
    return np.vstack([np.asarray(results[c][key], np.float32).T for c in range(NC)])


def kernel_multi(**inp):
    ones_bf = np.ones((128, 1), BF)
    cst = _cstf()

    ncA = build_prep()
    resA = run_bass_kernel_spmd(ncA, _prep_inputs(inp), core_ids=CORES)
    W = _prep_gather(resA.results)

    x_full = np.ascontiguousarray(
        np.asarray(inp["L0_attn_hidden_states"], np.float32).reshape(NC * T, D))

    ncB = build_l0ffn()

    def run_l0ffn(xf):
        maps = [{"x": _fm_shard(xf, c), "q1T": W["q1T"], "q2T": W["q2T"],
                 "q3T": W["q3T"], "ones_bf": ones_bf, "cstf": cst} for c in range(NC)]
        res = run_bass_kernel_spmd(ncB, maps, core_ids=CORES)
        return _gather_fm(res.results, "xo")

    x1 = run_l0ffn(x_full)

    kb_pc, qb_pc, masks_pc = _attn_consts()
    ncC = build_l0attn()
    res = run_bass_kernel_spmd(ncC, _attn_inputs(inp, x1, kb_pc, qb_pc, masks_pc),
                               core_ids=CORES)
    x2 = _gather_fm(res.results, "xo")

    x3 = run_l0ffn(x2)

    ncE = build_l1ffn()
    wgT = np.ascontiguousarray(np.asarray(inp["L1_ffn_Wg"], np.float32).T)
    eb = {"wgT": wgT,
          "w0aT": np.ascontiguousarray(W["W0aT16"]), "w0bT": np.ascontiguousarray(W["W0bT16"]),
          "w1aT": np.ascontiguousarray(W["W1aT16"]), "w1bT": np.ascontiguousarray(W["W1bT16"]),
          "b0a_cm": _cm(inp["L1_ffn_b0a"]), "b0b_cm": _cm(inp["L1_ffn_b0b"]),
          "b1a_cm": _cm(inp["L1_ffn_b1a"]), "b1b_cm": _cm(inp["L1_ffn_b1b"]),
          "ones_bf": ones_bf, "cstf": cst}

    def run_l1ffn(xf):
        maps = [dict(eb, x=_fm_shard(xf, c)) for c in range(NC)]
        res = run_bass_kernel_spmd(ncE, maps, core_ids=CORES)
        return _gather_fm(res.results, "xo")

    x4 = run_l1ffn(x3)

    ncF1 = build_l1at1()
    maps = []
    for c in range(NC):
        rev = (2047.0 - (np.arange(T) + (c % 2) * 1024)).astype(np.float32)[None, :]
        maps.append({"x": _fm_shard(x4, c),
                     "Wv": np.ascontiguousarray(np.asarray(inp["L1_attn_Wv"], np.float32)),
                     "ln1g_cm": _cm(inp["L1_attn_ln1_g"]), "ln1b_cm": _cm(inp["L1_attn_ln1_b"]),
                     "tmg_cm": _cm(inp["L1_attn_tm_g"]), "tmb_cm": _cm(inp["L1_attn_tm_b"]),
                     "td_cm": _cm(inp["L1_attn_time_decay"]), "rev_t": rev,
                     "ones_bf": ones_bf, "cstf": cst})
    resF1 = run_bass_kernel_spmd(ncF1, maps, core_ids=CORES)
    r1 = resF1.results

    ncF2 = build_l1at2()
    maps = []
    for c in range(NC):
        even = c - (c % 2)
        if c % 2 == 0:
            xlb = np.zeros((128, CD), BF)
        else:
            xlb = np.ascontiguousarray(
                np.asarray(r1[even]["xlno"])[:, T - 1].reshape(CD, 128).T)
        maps.append({"y": np.asarray(r1[c]["yo"]), "xln": np.asarray(r1[c]["xlno"]),
                     "xlb_cm": xlb, "vmp_own": np.asarray(r1[c]["vmpo"]),
                     "vmp_pair": np.asarray(r1[c ^ 1]["vmpo"]),
                     "tf_cm": _cm(inp["L1_attn_time_first"]),
                     "Wr": np.ascontiguousarray(np.asarray(inp["L1_attn_Wr"], np.float32)),
                     "Wo": np.ascontiguousarray(np.asarray(inp["L1_attn_Wo"], np.float32)),
                     "bo_cm": _cm(inp["L1_attn_bo"]),
                     "ln2g_cm": _cm(inp["L1_attn_ln2_g"]), "ln2b_cm": _cm(inp["L1_attn_ln2_b"]),
                     "cmg_cm": _cm(inp["L1_attn_cm_g"]), "cmb_cm": _cm(inp["L1_attn_cm_b"]),
                     "ones_bf": ones_bf, "cstf": cst})
    resF2 = run_bass_kernel_spmd(ncF2, maps, core_ids=CORES)
    r2 = resF2.results

    ncF3 = build_l1at3()
    wck16 = np.ascontiguousarray(W["Wck16"])
    wcv16 = np.ascontiguousarray(W["Wcv16"])
    wcr = np.ascontiguousarray(np.asarray(inp["L1_attn_Wcr"], np.float32))
    maps = []
    for c in range(NC):
        even = c - (c % 2)
        if c % 2 == 0:
            xcb = np.zeros((128, CD), BF)
        else:
            xcb = np.ascontiguousarray(
                np.asarray(r2[even]["xco"])[:, T - 1].reshape(CD, 128).T)
        maps.append({"x": _fm_shard(x4, c), "x2": np.asarray(r2[c]["x2o"]),
                     "xc": np.asarray(r2[c]["xco"]), "xcb_cm": xcb,
                     "Wck16": wck16, "Wcv16": wcv16, "Wcr": wcr,
                     "ones_bf": ones_bf, "cstf": cst})
    resF3 = run_bass_kernel_spmd(ncF3, maps, core_ids=CORES)
    x5 = _gather_fm(resF3.results, "xo")

    x6 = run_l1ffn(x5)

    return np.ascontiguousarray(x6.reshape(4, 2048, 2048))
